# version 02
# baseline (speedup 1.0000x reference)
import sys
sys.path.insert(0, "/opt/trn_rl_repo")
import numpy as np
import ml_dtypes
import concourse.bass as bass
import concourse.mybir as mybir
from concourse.bass import IndirectOffsetOnAxis
from concourse.bass_utils import run_bass_kernel_spmd
from contextlib import ExitStack

F32 = mybir.dt.float32
BF16 = mybir.dt.bfloat16
I32 = mybir.dt.int32
AF = mybir.ActivationFunctionType
ALU = mybir.AluOpType
AXX = mybir.AxisListType.X

P = 128
T = 4096
D = 1024
ND = D // P
NT = T // P
FF = 4096
NF = FF // P
E = 8
H = 16
HD = 64
SEL = 1024
LN_EPS = 1e-5
BISECT_ITERS = 34

BF = ml_dtypes.bfloat16


class Ctr:
    def __init__(self, nc, name):
        self.sem = nc.alloc_semaphore(name)
        self.n = 0

    def inc(self, inst, k=1):
        self.n += k
        return inst.then_inc(self.sem, k)


class DmaSlots:
    def __init__(self, nc, name, nslots):
        self.nc = nc
        self.sems = [nc.alloc_semaphore(f"{name}{i}") for i in range(nslots)]
        self.cnt = [0] * nslots
        self.tgt = {}
        self.G = nslots

    def issue(self, i, inst):
        s = i % self.G
        self.cnt[s] += 16
        self.tgt[i] = (s, self.cnt[s])
        return inst.then_inc(self.sems[s], 16)

    def wait(self, eng, i):
        if i in self.tgt:
            s, t = self.tgt[i]
        else:
            s, t = i % self.G, 16 * (i // self.G + 1)
        eng.wait_ge(self.sems[s], t)

    def wait_all(self, eng):
        for s in range(self.G):
            if self.cnt[s]:
                eng.wait_ge(self.sems[s], self.cnt[s])


class Prog:
    def __init__(self, nc):
        self.nc = nc
        self.c = {}
        self.t = {}
        self.consts = {}
        self._const_stack = None

    def ctr(self, name):
        if name not in self.c:
            self.c[name] = Ctr(self.nc, name)
        return self.c[name]

    def release(self, ctrs=(), slots=()):
        nc = self.nc
        sems = []
        for c in ctrs:
            if isinstance(c, str):
                c = self.c.pop(c)
            sems.append(c.sem)
        for sl in slots:
            sems.extend(sl.sems)
        with nc.Block() as blk:
            @blk.gpsimd
            def _(g):
                for s in sems:
                    g.dma_reset(semaphore_range=range(s.num, s.num + 1))

            @blk.sync
            def _(sync):
                for s in sems:
                    sync.sem_clear(s)
        for s in sems:
            nc.release_semaphore(s)

    def init_consts(self, stack, values):
        nc = self.nc
        tile = stack.enter_context(
            nc.sbuf_tensor("consts", [P, max(1, len(values))], F32))
        for i, v in enumerate(values):
            self.consts[float(v)] = tile[:, i:i + 1]
        with nc.Block() as blk:
            @blk.vector
            def _(v_):
                for i, v in enumerate(values):
                    v_.memset(tile[:, i:i + 1], float(v))

    def const(self, v):
        return self.consts[float(v)]


def emit_ln(pr, x_src, xn_tok, tag):
    nc = pr.nc
    G = 4
    nc_ = pr.nc
    ld = DmaSlots(nc_, f"lnld{tag}", 4)
    out = DmaSlots(nc_, f"lnou{tag}", 4)
    sq = pr.ctr(f"lnsq{tag}")
    st = pr.ctr(f"lnst{tag}")
    rs = pr.ctr(f"lnrs{tag}")
    fin = pr.ctr(f"lnfi{tag}")
    sq0, st0, rs0, fin0 = sq.n, st.n, rs.n, fin.n

    with ExitStack() as ctx:
        En = ctx.enter_context
        xb = [En(nc.sbuf_tensor(f"lnx{i}{tag}", [P, D], F32)) for i in range(G)]
        sqb = [En(nc.sbuf_tensor(f"lnq{i}{tag}", [P, D], F32)) for i in range(G)]
        xnb = [En(nc.sbuf_tensor(f"lnn{i}{tag}", [P, D], BF16)) for i in range(G)]
        stt = En(nc.sbuf_tensor(f"lns{tag}", [P, 8 * G], F32))

        with nc.Block() as blk:
            @blk.sync
            def _(sync):
                for i in range(NT):
                    if i >= G:
                        sync.wait_ge(fin.sem, fin0 + 2 * (i - G + 1))
                    ld.issue(i, sync.dma_start(xb[i % G][:], x_src[i * P:(i + 1) * P, :]))

            @blk.scalar
            def _(s):
                for i in range(NT + 1):
                    if i < NT:
                        b = i % G
                        ld.wait(s, i)
                        if i >= G:
                            s.wait_ge(st.sem, st0 + 6 * (i - G + 1))
                        sq.inc(s.activation(sqb[b][:], xb[b][:], AF.Square))
                    j = i - 1
                    if j >= 0:
                        o = 8 * (j % G)
                        s.wait_ge(st.sem, st0 + 6 * (j + 1))
                        if j >= G:
                            s.wait_ge(fin.sem, fin0 + 2 * (j - G + 1))
                        rs.inc(s.activation(stt[:, o + 4:o + 5], stt[:, o + 3:o + 4],
                                            AF.Sqrt, bias=pr.const(LN_EPS)))

            @blk.vector
            def _(v):
                for i in range(NT + 1):
                    if i < NT:
                        b = i % G
                        o = 8 * b
                        v.wait_ge(sq.sem, sq0 + (i + 1))
                        if i >= G:
                            v.wait_ge(fin.sem, fin0 + 2 * (i - G + 1))
                        st.inc(v.tensor_reduce(stt[:, o:o + 1], xb[b][:], axis=AXX, op=ALU.add))
                        st.inc(v.tensor_reduce(stt[:, o + 1:o + 2], sqb[b][:], axis=AXX, op=ALU.add))
                        v.wait_ge(st.sem, st.n)
                        st.inc(v.tensor_scalar_mul(stt[:, o + 2:o + 3], stt[:, o:o + 1], 1.0 / D))
                        st.inc(v.tensor_scalar_mul(stt[:, o + 1:o + 2], stt[:, o + 1:o + 2], 1.0 / D))
                        v.wait_ge(st.sem, st.n)
                        st.inc(v.tensor_tensor(stt[:, o + 3:o + 4], stt[:, o + 2:o + 3],
                                               stt[:, o + 2:o + 3], op=ALU.mult))
                        v.wait_ge(st.sem, st.n)
                        st.inc(v.tensor_sub(stt[:, o + 3:o + 4], stt[:, o + 1:o + 2],
                                            stt[:, o + 3:o + 4]))
                    j = i - 1
                    if j >= 0:
                        b = j % G
                        o = 8 * b
                        v.wait_ge(rs.sem, rs0 + (j + 1))
                        fin.inc(v.reciprocal(stt[:, o + 4:o + 5], stt[:, o + 4:o + 5]))
                        v.wait_ge(fin.sem, fin.n)
                        if j >= G:
                            out.wait(v, j - G)
                        fin.inc(v.tensor_scalar(xnb[b][:], xb[b][:],
                                                stt[:, o + 2:o + 3], stt[:, o + 4:o + 5],
                                                op0=ALU.subtract, op1=ALU.mult))

            @blk.gpsimd
            def _(g):
                for j in range(NT):
                    g.wait_ge(fin.sem, fin0 + 2 * (j + 1))
                    out.issue(j, g.dma_start(xn_tok[j * P:(j + 1) * P, :], xnb[j % G][:]))
                out.wait_all(g)

    pr.release(ctrs=[f"lnsq{tag}", f"lnst{tag}", f"lnrs{tag}", f"lnfi{tag}"],
               slots=[ld, out])


def emit_xnT(pr, xn_tok, xnT_slab, tag, xnT_hbm=None):
    nc = pr.nc
    tr = pr.ctr(f"xt{tag}")
    with nc.Block() as blk:
        @blk.sync
        def _(sync):
            for c in range(ND):
                tr.inc(sync.dma_start(xnT_slab[:, c, :],
                                      xn_tok[:, c * P:(c + 1) * P],
                                      transpose=True), 16)
            sync.wait_ge(tr.sem, tr.n)
            if xnT_hbm is not None:
                tr.inc(sync.dma_start(xnT_hbm.ap(), xnT_slab[:, :, :]), 16)
                sync.wait_ge(tr.sem, tr.n)
    pr.release(ctrs=[f"xt{tag}"])


def emit_sel(pr, xnT, gw_dram, cst, m_slab, g_slab, probs, dbg=None):
    nc = pr.nc
    gmm = pr.ctr("sl_gmm")
    gpre = pr.ctr("sl_pre")
    gexp = pr.ctr("sl_exp")
    gpos = pr.ctr("sl_pos")
    bis = pr.ctr("sl_bis")
    bmm = pr.ctr("sl_bmm")
    rnk = pr.ctr("sl_rnk")
    rmm = pr.ctr("sl_rmm")
    ld = pr.ctr("sl_ld")

    c_ones, c_utri, c_slots, c_tok = cst["ones"], cst["utri"], cst["slots"], cst["tok"]

    with ExitStack() as ctx:
        En = ctx.enter_context
        gw = En(nc.sbuf_tensor("gw", [P, ND, E], BF16))
        smax = En(nc.sbuf_tensor("smax", [P, 2, 2], F32))
        sexp = En(nc.sbuf_tensor("sexp", [P, 2, E], F32))
        sstat = En(nc.sbuf_tensor("sstat", [P, 2, 2], F32))
        theta = En(nc.sbuf_tensor("theta", [P, E], F32))
        tlo = En(nc.sbuf_tensor("tlo", [P, E], F32))
        tmp8 = En(nc.sbuf_tensor("tmp8", [P, E], F32))
        d2 = En(nc.sbuf_tensor("d2", [P, E], F32))
        cnt = En(nc.sbuf_tensor("cnt", [P, E], F32))
        maskb = En(nc.sbuf_tensor("maskb", [P, E, NT], F32))
        toff = En(nc.sbuf_tensor("toff", [1, 2 * 64], F32))
        rankp = En(nc.sbuf_tensor("rankp", [P, NT], F32))
        mtile = En(nc.sbuf_tensor("mtile", [P, 2, P], F32))
        mgr_all = En(nc.sbuf_tensor("mgr_all", [P, NT, 2], F32))
        p_gate = En(nc.psum_tensor("p_gate", [P, 2, E], F32))
        p_cnt = En(nc.psum_tensor("p_cnt", [P, E * NT], F32))
        p_pref = En(nc.psum_tensor("p_pref", [P, NT], F32))
        p_tot = En(nc.psum_tensor("p_tot", [1, NT], F32))
        p_mg = En(nc.psum_tensor("p_mg", [P, 16], F32))

        with nc.Block() as blk:
            @blk.gpsimd
            def _(g):
                ld.inc(g.dma_start(gw[:, :, :], gw_dram.ap().rearrange(
                    "(c p) e -> p c e", p=P)), 16)
                g.wait_ge(ld.sem, ld.n)

        with nc.Block() as blk:
            @blk.tensor
            def _(t):
                for i in range(NT):
                    b = i % 2
                    if i >= 2:
                        t.wait_ge(gpos.sem, 3 * (i - 1))
                    for c in range(ND):
                        mm = t.matmul(p_gate[:, b, :], xnT[:, c, i * P:(i + 1) * P],
                                      gw[:, c, :], start=(c == 0), stop=(c == ND - 1))
                    gmm.inc(mm)

            @blk.vector
            def _(v):
                for i in range(NT + 1):
                    if i < NT:
                        b = i % 2
                        v.wait_ge(gmm.sem, i + 1)
                        if i >= 2:
                            v.wait_ge(gexp.sem, i - 1)
                        gpre.inc(v.tensor_reduce(smax[:, b, 0:1], p_gate[:, b, :],
                                                 axis=AXX, op=ALU.max))
                        v.wait_ge(gpre.sem, gpre.n)
                        gpre.inc(v.tensor_scalar_mul(smax[:, b, 1:2], smax[:, b, 0:1], -1.0))
                    j = i - 1
                    if j >= 0:
                        b = j % 2
                        v.wait_ge(gexp.sem, j + 1)
                        gpos.inc(v.tensor_reduce(sstat[:, b, 0:1], sexp[:, b, :],
                                                 axis=AXX, op=ALU.add))
                        v.wait_ge(gpos.sem, gpos.n)
                        gpos.inc(v.reciprocal(sstat[:, b, 1:2], sstat[:, b, 0:1]))
                        v.wait_ge(gpos.sem, gpos.n)
                        gpos.inc(v.tensor_scalar_mul(
                            probs[:, :, j:j + 1], sexp[:, b, :].unsqueeze(2),
                            sstat[:, b, 1:2]))

            @blk.scalar
            def _(s):
                for i in range(NT):
                    b = i % 2
                    s.wait_ge(gpre.sem, 2 * (i + 1))
                    if i >= 2:
                        s.wait_ge(gpos.sem, 3 * (i - 1))
                    gexp.inc(s.activation(sexp[:, b, :], p_gate[:, b, :],
                                          AF.Exp, bias=smax[:, b, 1:2]))

        with nc.Block() as blk:
            @blk.vector
            def _(v):
                bis.inc(v.memset(theta[:], 0.5))
                bis.inc(v.memset(tlo[:], 0.0))
                v.wait_ge(bis.sem, bis.n)
                for k in range(BISECT_ITERS):
                    step = 2.0 ** (-(k + 2))
                    if k > 0:
                        v.wait_ge(bmm.sem, k)
                        v.wait_ge(bis.sem, bis.n)
                    for e in range(E):
                        bis.inc(v.tensor_scalar(maskb[:, e, :], probs[:, e, :],
                                                theta[:, e:e + 1], None, op0=ALU.is_ge))
                    v.wait_ge(bmm.sem, k + 1)
                    bis.inc(v.tensor_reduce(cnt[:, :].unsqueeze(2), p_cnt[:].rearrange(
                        "p (e c) -> p e c", e=E), axis=AXX, op=ALU.add))
                    v.wait_ge(bis.sem, bis.n)
                    bis.inc(v.tensor_scalar(d2[:], cnt[:], float(SEL) - 0.5, None,
                                            op0=ALU.is_ge))
                    v.wait_ge(bis.sem, bis.n)
                    bis.inc(v.tensor_sub(tmp8[:], theta[:], tlo[:]))
                    v.wait_ge(bis.sem, bis.n)
                    bis.inc(v.tensor_mul(tmp8[:], tmp8[:], d2[:]))
                    v.wait_ge(bis.sem, bis.n)
                    bis.inc(v.tensor_add(tlo[:], tlo[:], tmp8[:]))
                    v.wait_ge(bis.sem, bis.n)
                    bis.inc(v.tensor_scalar(tmp8[:], d2[:], 2.0 * step, -step,
                                            op0=ALU.mult, op1=ALU.add))
                    v.wait_ge(bis.sem, bis.n)
                    bis.inc(v.tensor_add(theta[:], theta[:], tmp8[:]))
                assert bis.n == 2 + 15 * BISECT_ITERS

            @blk.tensor
            def _(t):
                for k in range(BISECT_ITERS):
                    t.wait_ge(bis.sem, 2 + 15 * k + 8)
                    bmm.inc(t.matmul(p_cnt[:], c_ones[:], maskb[:].rearrange(
                        "p e c -> p (e c)"), start=True, stop=True))

        with nc.Block() as blk:
            DVE_PER_E = 12 + 8 * NT + 16
            PE_PER_E = 3 + 8 * NT

            @blk.vector
            def _(v):
                for e in range(E):
                    base = DVE_PER_E * e
                    if e > 0:
                        v.wait_ge(rmm.sem, PE_PER_E * (e - 1) + 2)
                    rnk.inc(v.tensor_scalar(maskb[:, 0, :], probs[:, e, :],
                                            tlo[:, e:e + 1], None, op0=ALU.is_ge))
                    v.wait_ge(rmm.sem, PE_PER_E * e + 1)
                    rnk.inc(v.memset(toff[:, 0:64], 0.0))
                    rnk.inc(v.memset(toff[:, 64:128], 0.0))
                    v.wait_ge(rnk.sem, rnk.n)
                    rnk.inc(v.tensor_copy(toff[0:1, 33:64], p_tot[0:1, 0:31]))
                    v.wait_ge(rnk.sem, rnk.n)
                    src_, dst_ = 0, 64
                    for sh in (1, 2, 4, 8, 16):
                        rnk.inc(v.tensor_add(toff[0:1, dst_ + 32:dst_ + 64],
                                             toff[0:1, src_ + 32:src_ + 64],
                                             toff[0:1, src_ + 32 - sh:src_ + 64 - sh]))
                        v.wait_ge(rnk.sem, rnk.n)
                        src_, dst_ = dst_, src_
                    v.wait_ge(rmm.sem, PE_PER_E * e + 3)
                    rnk.inc(v.tensor_mul(rankp[:], p_pref[:], maskb[:, 0, :]))
                    rnk.inc(v.tensor_copy(mgr_all[:, :, 0:1], c_tok[:, :].unsqueeze(2)))
                    rnk.inc(v.tensor_copy(mgr_all[:, :, 1:2], probs[:, e, :].unsqueeze(2)))
                    assert rnk.n == base + 12
                    for st_ in range(E):
                        for c in range(NT):
                            b = c % 2
                            v.wait_ge(rnk.sem, base + 12)
                            if st_ * NT + c >= 2:
                                v.wait_ge(rmm.sem, PE_PER_E * e + 3 + st_ * NT + c - 1)
                            rnk.inc(v.tensor_scalar(
                                mtile[:, b, :], c_slots[:, st_ * P:(st_ + 1) * P],
                                rankp[:, c:c + 1], None, op0=ALU.is_equal))
                    v.wait_ge(rmm.sem, PE_PER_E * (e + 1))
                    for st_ in range(E):
                        rnk.inc(v.tensor_copy(m_slab[:, e * 8 + st_:e * 8 + st_ + 1],
                                              p_mg[:, 2 * st_:2 * st_ + 1]))
                        rnk.inc(v.tensor_copy(g_slab[:, e * 8 + st_:e * 8 + st_ + 1],
                                              p_mg[:, 2 * st_ + 1:2 * st_ + 2]))
                    assert rnk.n == DVE_PER_E * (e + 1)

            @blk.tensor
            def _(t):
                for e in range(E):
                    base = DVE_PER_E * e
                    t.wait_ge(rnk.sem, base + 1)
                    if e > 0:
                        t.wait_ge(rnk.sem, base - (8 * NT + 16) + 4)
                    rmm.inc(t.matmul(p_tot[:, 0:NT], c_ones[:, 0:1], maskb[:, 0, :],
                                     start=True, stop=True))
                    rmm.inc(t.matmul(p_pref[:], c_utri[:], maskb[:, 0, :],
                                     start=True, stop=False))
                    t.wait_ge(rnk.sem, base + 9)
                    rmm.inc(t.matmul(p_pref[:], c_ones[0:1, :], toff[0:1, 64 + 32:64 + 64],
                                     start=False, stop=True))
                    for st_ in range(E):
                        for c in range(NT):
                            b = c % 2
                            t.wait_ge(rnk.sem, base + 12 + st_ * NT + c + 1)
                            rmm.inc(t.matmul(p_mg[:, 2 * st_:2 * st_ + 2],
                                             mtile[:, b, :], mgr_all[:, c, :],
                                             start=(c == 0), stop=(c == NT - 1)))
                    assert rmm.n == PE_PER_E * (e + 1)

        pr.release(ctrs=["sl_gmm", "sl_pre", "sl_exp", "sl_pos", "sl_bis",
                         "sl_bmm", "sl_rnk", "sl_rmm", "sl_ld"])
        if dbg is not None:
            cpd = pr.ctr("sl_dbg")
            with nc.Block() as blk:
                @blk.sync
                def _(sync):
                    cpd.inc(sync.dma_start(dbg["probs"].ap(), probs[:, :, :]), 16)
                    cpd.inc(sync.dma_start(dbg["tlo"].ap(), tlo[:]), 16)
                    cpd.inc(sync.dma_start(dbg["m"].ap(), m_slab[:, :]), 16)
                    cpd.inc(sync.dma_start(dbg["g"].ap(), g_slab[:, :]), 16)
                    sync.wait_ge(cpd.sem, cpd.n)



def emit_attn0(pr, xnT, W, x_dram, x1_dram, outT_hbm, ctx_weights):
    nc = pr.nc
    QB = 512
    NQB = T // QB
    NIT = 2 * NQB

    wld = pr.ctr("a0_wld")
    pmm = pr.ctr("a0_pmm")
    pev = pr.ctr("a0_pev")
    smm = pr.ctr("a0_smm")
    avm = pr.ctr("a0_avm")
    bcm = pr.ctr("a0_bcm")
    xdv = pr.ctr("a0_xdv")
    exc = pr.ctr("a0_exc")
    sot = DmaSlots(nc, "a0_sot", 2)

    En = ctx_weights.enter_context
    wq = En(nc.sbuf_tensor("a0_wq", [P, ND, D], BF16))
    wk = En(nc.sbuf_tensor("a0_wk", [P, ND, D], BF16))
    wv = En(nc.sbuf_tensor("a0_wv", [P, ND, D], BF16))
    wo = En(nc.sbuf_tensor("a0_wo", [P, ND, D], BF16))
    brow = En(nc.sbuf_tensor("a0_brow", [1, 4 * D], BF16))
    ones = En(nc.sbuf_tensor("a0_ones", [1, QB], BF16))

    with nc.Block() as blk:
        @blk.gpsimd
        def _(g):
            for i, (wsb, wd) in enumerate([(wq, W["wq"]), (wk, W["wk"]),
                                           (wv, W["wv"]), (wo, W["wo"])]):
                wld.inc(g.dma_start(wsb[:, :, :], wd.ap().rearrange(
                    "(c p) e -> p c e", p=P)), 16)
            for i, bd in enumerate([W["bq"], W["bk"], W["bv"], W["bo"]]):
                wld.inc(g.dma_start(brow[0:1, i * D:(i + 1) * D], bd.ap()), 16)
            g.wait_ge(wld.sem, wld.n)

        @blk.vector
        def _(v):
            v.memset(ones[:], 1.0)

    bq = brow[0:1, 0 * D:1 * D]
    bk = brow[0:1, 1 * D:2 * D]
    bv = brow[0:1, 2 * D:3 * D]
    bo = brow[0:1, 3 * D:4 * D]

    with ExitStack() as ctx:
        En2 = ctx.enter_context
        qt = En2(nc.sbuf_tensor("a0_qt", [P, T], BF16))
        kt = En2(nc.sbuf_tensor("a0_kt", [P, T], BF16))
        vs = En2(nc.sbuf_tensor("a0_vs", [P, NT, 136], BF16))
        at = En2(nc.sbuf_tensor("a0_at", [P, NT, QB], BF16))
        rlf = En2(nc.sbuf_tensor("a0_rlf", [1, QB], F32))
        rlb = En2(nc.sbuf_tensor("a0_rlb", [1, QB], BF16))
        bcs = En2(nc.sbuf_tensor("a0_bcs", [64, QB], F32))
        ots = [En2(nc.sbuf_tensor(f"a0_ot{i}", [64, QB], BF16)) for i in range(2)]
        psS = [En2(nc.psum_tensor(f"a0_psS{i}", [P, 2 * QB], F32)) for i in range(2)]
        pAV = [En2(nc.psum_tensor(f"a0_pAV{i}", [66, QB], F32)) for i in range(2)]
        pM = En2(nc.psum_tensor("a0_pM", [P, QB], F32))
        pBC = En2(nc.psum_tensor("a0_pBC", [64, QB], F32))

        with nc.Block() as blk:
            @blk.vector
            def _(v):
                ms = pr.ctr("a0_ms")
                ms.inc(v.memset(vs[:, :, 64:68], 0.0))
                ms.inc(v.memset(vs[:, :, 132:136], 0.0))
                v.wait_ge(ms.sem, ms.n)
                v.memset(vs[:, :, 64:65], 1.0)
                v.memset(vs[:, :, 132:133], 1.0)

        with nc.Block() as blk:
            @blk.tensor
            def _(t):
                for pair in range(E):
                    pb_mm = 48 * pair
                    pb_ev = 80 * pair
                    col = slice(pair * P, (pair + 1) * P)
                    for wsb, bias in ((wq, bq), (wk, bk)):
                        qk = 0 if wsb is wq else 1
                        for ts in range(8):
                            ps = psS[ts % 2]
                            if pair > 0 and qk == 0 and ts < 2:
                                t.wait_ge(exc.sem, 16 * NIT * pair)
                            if pb_mm + 8 * qk + ts >= 2:
                                t.wait_ge(pev.sem, pb_ev + 8 * qk + ts - 1)
                            sl = slice(ts * QB, (ts + 1) * QB)
                            for c in range(ND):
                                t.matmul(ps[:, 0:QB], wsb[:, c, col], xnT[:, c, sl],
                                         start=(c == 0), stop=False)
                            pmm.inc(t.matmul(ps[:, 0:QB], bias[:, col], ones[:],
                                             start=False, stop=True))
                    for tt in range(NT):
                        if 16 + tt >= 1:
                            t.wait_ge(pev.sem, pb_ev + 16 + 2 * tt)
                        for c in range(ND):
                            t.matmul(pM[:, 0:P], xnT[:, c, tt * P:(tt + 1) * P],
                                     wv[:, c, col], start=(c == 0), stop=False)
                        pmm.inc(t.matmul(pM[:, 0:P], ones[0:1, 0:P], bv[:, col],
                                         start=False, stop=True))
                    t.wait_ge(pev.sem, pb_ev + 16)
                    for hh in range(2):
                        hsl = slice(hh * 64, hh * 64 + 64)
                        for qb in range(NQB):
                            it = pair * NIT + hh * NQB + qb
                            qsl = slice(qb * QB, (qb + 1) * QB)
                            for g in range(16):
                                if it * 16 + g >= 2:
                                    t.wait_ge(exc.sem, it * 16 + g - 1)
                                ps = psS[g % 2]
                                for k2 in range(2):
                                    kt_i = 2 * g + k2
                                    smm.inc(t.matmul(
                                        ps[:, k2 * QB:(k2 + 1) * QB],
                                        kt[hsl, kt_i * P:(kt_i + 1) * P],
                                        qt[hsl, qsl], start=True, stop=True))
                            if hh == 0 and qb == 0:
                                t.wait_ge(pev.sem, 80 * (pair + 1))
                            pav = pAV[it % 2]
                            for kt_i in range(NT):
                                if kt_i == 0 and it >= 2:
                                    t.wait_ge(xdv.sem, 4 * it - 4)
                                if kt_i % 2 == 0:
                                    t.wait_ge(exc.sem, it * 16 + kt_i // 2 + 1)
                                avm.inc(t.matmul(pav[:, :],
                                                 vs[:, kt_i, hh * 68:hh * 68 + 66],
                                                 at[:, kt_i, :],
                                                 start=(kt_i == 0), stop=(kt_i == NT - 1)))
                            t.wait_ge(xdv.sem, 4 * it + 2)
                            bcm.inc(t.matmul(pBC[:, :], ones[0:1, 0:64], rlb[:],
                                             start=True, stop=True))

            @blk.vector
            def _(v):
                for pair in range(E):
                    pb_mm = 48 * pair
                    pb_ev = 80 * pair
                    for qk in range(2):
                        dst = qt if qk == 0 else kt
                        for ts in range(8):
                            v.wait_ge(pmm.sem, pb_mm + 8 * qk + ts + 1)
                            pev.inc(v.tensor_copy(dst[:, ts * QB:(ts + 1) * QB],
                                                  psS[ts % 2][:, 0:QB]))
                    for tt in range(NT):
                        v.wait_ge(pmm.sem, pb_mm + 16 + tt + 1)
                        pev.inc(v.tensor_copy(vs[:, tt, 0:64], pM[:, 0:64]))
                        pev.inc(v.tensor_copy(vs[:, tt, 68:132], pM[:, 64:128]))
                    for it_l in range(NIT):
                        it = pair * NIT + it_l
                        pav = pAV[it % 2]
                        v.wait_ge(avm.sem, 32 * (it + 1))
                        xdv.inc(v.reciprocal(rlf[:], pav[64:65, :]))
                        v.wait_ge(xdv.sem, xdv.n)
                        xdv.inc(v.tensor_copy(rlb[:], rlf[:]))
                        v.wait_ge(bcm.sem, it + 1)
                        xdv.inc(v.tensor_copy(bcs[:], pBC[:, :]))
                        v.wait_ge(xdv.sem, xdv.n)
                        if it >= 2:
                            sot.wait(v, it - 2)
                        xdv.inc(v.tensor_mul(ots[it % 2][:, :], pav[0:64, :], bcs[:]))

            @blk.scalar
            def _(s):
                for pair in range(E):
                    for it_l in range(NIT):
                        it = pair * NIT + it_l
                        for g in range(16):
                            s.wait_ge(smm.sem, it * 32 + 2 * (g + 1))
                            if it >= 1:
                                s.wait_ge(avm.sem, 32 * (it - 1) + 2 * g + 2)
                            exc.inc(s.activation(
                                at[:, 2 * g:2 * g + 2, :], psS[g % 2][:, :],
                                AF.Exp, scale=0.125))

            @blk.sync
            def _(sync):
                for pair in range(E):
                    for hh in range(2):
                        head = 2 * pair + hh
                        for qb in range(NQB):
                            it = pair * NIT + hh * NQB + qb
                            sync.wait_ge(xdv.sem, 4 * (it + 1))
                            sot.issue(it, sync.dma_start(
                                outT_hbm.ap()[head * 64:(head + 1) * 64,
                                              qb * QB:(qb + 1) * QB],
                                ots[it % 2][:, :]))
                sot.wait_all(sync)

    old = DmaSlots(nc, "a0_old", 2)
    xld = DmaSlots(nc, "a0_xld", 3)
    omm = pr.ctr("a0_omm")
    oev = pr.ctr("a0_oev")
    ost = DmaSlots(nc, "a0_ost", 3)
    outT_v = outT_hbm.ap().rearrange("(c p) t -> p c t", p=P)

    with ExitStack() as ctx:
        En2 = ctx.enter_context
        otb = [En2(nc.sbuf_tensor(f"a0_otb{i}", [P, ND, QB], BF16)) for i in range(2)]
        xtb = [En2(nc.sbuf_tensor(f"a0_xtb{i}", [P, D], F32)) for i in range(3)]
        x1b = [En2(nc.sbuf_tensor(f"a0_x1b{i}", [P, D], F32)) for i in range(3)]
        pO = [En2(nc.psum_tensor(f"a0_pO{i}", [P, D], F32)) for i in range(2)]

        with nc.Block() as blk:
            @blk.sync
            def _(sync):
                for i in range(NT):
                    tg = i // 4
                    if i % 4 == 0:
                        if tg >= 2:
                            sync.wait_ge(omm.sem, 4 * (tg - 1))
                        old.issue(tg, sync.dma_start(otb[tg % 2][:, :, :],
                                                     outT_v[:, :, tg * QB:(tg + 1) * QB]))
                    if i >= 3:
                        sync.wait_ge(oev.sem, i - 2)
                    xld.issue(i, sync.dma_start(xtb[i % 3][:], x_dram[i * P:(i + 1) * P, :]))

            @blk.tensor
            def _(t):
                for i in range(NT):
                    tg, tl = i // 4, i % 4
                    old.wait(t, tg)
                    if i >= 2:
                        t.wait_ge(oev.sem, i - 1)
                    ps = pO[i % 2]
                    for dh in range(2):
                        dsl = slice(dh * QB, (dh + 1) * QB)
                        for c in range(ND):
                            t.matmul(ps[:, dsl], otb[tg % 2][:, c, tl * P:(tl + 1) * P],
                                     wo[:, c, dsl], start=(c == 0), stop=False)
                        last = t.matmul(ps[:, dsl], ones[0:1, 0:P], bo[:, dsl],
                                        start=False, stop=True)
                    omm.inc(last)

            @blk.vector
            def _(v):
                for i in range(NT):
                    v.wait_ge(omm.sem, i + 1)
                    xld.wait(v, i)
                    if i >= 3:
                        ost.wait(v, i - 3)
                    oev.inc(v.tensor_add(x1b[i % 3][:], xtb[i % 3][:], pO[i % 2][:]))

            @blk.gpsimd
            def _(g):
                for i in range(NT):
                    g.wait_ge(oev.sem, i + 1)
                    ost.issue(i, g.dma_start(x1_dram[i * P:(i + 1) * P, :], x1b[i % 3][:]))
                ost.wait_all(g)

    pr.release(ctrs=["a0_wld", "a0_pmm", "a0_pev", "a0_smm", "a0_avm", "a0_bcm",
                     "a0_xdv", "a0_exc", "a0_ms", "a0_omm", "a0_oev"],
               slots=[sot, old, xld, ost])


def emit_ffn0(pr, xn_tok, xin_tok, x1_dram, w1_dram, w2_dram, m_slab, g_slab):
    nc = pr.nc
    FH = FF // 2
    upm = pr.ctr("f0_upm")
    gel = pr.ctr("f0_gel")
    dnm = pr.ctr("f0_dnm")
    dnev = pr.ctr("f0_dnev")
    tr = pr.ctr("f0_tr")
    gth = DmaSlots(nc, "f0_gth", 2)
    wrt = DmaSlots(nc, "f0_wrt", 2)
    w1l = DmaSlots(nc, "f0_w1l", 2)
    w2l = DmaSlots(nc, "f0_w2l", 2)
    scat = DmaSlots(nc, "f0_scat", 2)

    with ExitStack() as ctx:
        En = ctx.enter_context
        xg = [En(nc.sbuf_tensor(f"f0_xg{i}", [P, D], BF16)) for i in range(2)]
        xinT = En(nc.sbuf_tensor("f0_xinT", [P, ND, SEL], BF16))
        w1h = [En(nc.sbuf_tensor(f"f0_w1h{i}", [P, ND, FH], BF16)) for i in range(2)]
        w2h = [En(nc.sbuf_tensor(f"f0_w2h{i}", [P, 16, D], BF16)) for i in range(2)]
        hT = En(nc.sbuf_tensor("f0_hT", [P, 16, SEL], BF16))
        stage = [En(nc.sbuf_tensor(f"f0_st{i}", [P, D], F32)) for i in range(2)]
        pU = [En(nc.psum_tensor(f"f0_pU{i}", [P, D], F32)) for i in range(2)]
        pD = [En(nc.psum_tensor(f"f0_pD{i}", [P, D], F32)) for i in range(2)]

        with nc.Block() as blk:
            @blk.gpsimd
            def _(g):
                for sc in range(8):
                    if sc >= 2:
                        wrt.wait(g, sc - 2)
                    gth.issue(sc, g.indirect_dma_start(
                        xg[sc % 2][:], None, xn_tok,
                        IndirectOffsetOnAxis(m_slab[:, sc:sc + 1], 0)))
                for e in range(E):
                    for fh in range(2):
                        fi = 2 * e + fh
                        if fi >= 2:
                            g.wait_ge(upm.sem, 32 * (fi - 1))
                        w1l.issue(fi, g.dma_start(
                            w1h[fi % 2][:, :, :],
                            w1_dram.ap()[e].rearrange("(c p) f -> p c f", p=P)
                            [:, :, fh * FH:(fh + 1) * FH]))
                        if fi >= 2:
                            g.wait_ge(dnm.sem, 8 * (fi - 1))
                        w2l.issue(fi, g.dma_start(
                            w2h[fi % 2][:, :, :],
                            w2_dram.ap()[e].rearrange("(c p) d -> p c d", p=P)
                            [:, fh * 16:(fh + 1) * 16, :]))
                    if e + 1 < E:
                        for sc in range(8):
                            gi = 8 * (e + 1) + sc
                            wrt.wait(g, gi - 2)
                            gth.issue(gi, g.indirect_dma_start(
                                xg[gi % 2][:], None, xn_tok,
                                IndirectOffsetOnAxis(
                                    m_slab[:, (e + 1) * 8 + sc:(e + 1) * 8 + sc + 1], 0)))
                    for fh in range(2):
                        scat.wait_all(g)
                        for selt in range(8):
                            sidx = 16 * e + 8 * fh + selt
                            g.wait_ge(dnev.sem, 16 * e + 8 * fh + selt + 1)
                            scat.issue(sidx, g.indirect_dma_start(
                                x1_dram, IndirectOffsetOnAxis(
                                    m_slab[:, e * 8 + selt:e * 8 + selt + 1], 0),
                                stage[sidx % 2][:], None, compute_op=ALU.add))
                gth.wait_all(g)
                scat.wait_all(g)

            @blk.sync
            def _(sync):
                for e in range(E):
                    for sc in range(8):
                        gi = 8 * e + sc
                        gth.wait(sync, gi)
                        if sc == 0:
                            sync.wait_ge(tr.sem, 16 * 8 * e)
                        wrt.issue(gi, sync.dma_start(
                            xin_tok.ap()[sc * P:(sc + 1) * P, :], xg[gi % 2][:]))
                    wrt.wait_all(sync)
                    for c in range(ND):
                        if c == 0:
                            sync.wait_ge(upm.sem, 64 * e)
                        tr.inc(sync.dma_start(
                            xinT[:, c, :], xin_tok.ap()[:, c * P:(c + 1) * P],
                            transpose=True), 16)

            @blk.tensor
            def _(t):
                for e in range(E):
                    for fh in range(2):
                        fi = 2 * e + fh
                        w1l.wait(t, fi)
                        if fh == 0:
                            t.wait_ge(tr.sem, 16 * 8 * (e + 1))
                        for ft in range(16):
                            gg = 32 * e + 16 * fh + ft
                            if gg >= 2:
                                t.wait_ge(gel.sem, gg - 1)
                            ps = pU[ft % 2]
                            for sb in range(2):
                                ssl = slice(sb * 512, (sb + 1) * 512)
                                for c in range(ND):
                                    mm = t.matmul(ps[:, ssl],
                                                  w1h[fi % 2][:, c, ft * P:(ft + 1) * P],
                                                  xinT[:, c, ssl],
                                                  start=(c == 0), stop=(c == ND - 1))
                                upm.inc(mm)
                        w2l.wait(t, fi)
                        t.wait_ge(gel.sem, 16 * (fi + 1))
                        for selt in range(8):
                            dg = 16 * e + 8 * fh + selt
                            if dg >= 2:
                                t.wait_ge(dnev.sem, dg - 1)
                            ps = pD[selt % 2]
                            for dh in range(2):
                                dsl = slice(dh * 512, (dh + 1) * 512)
                                for fc in range(16):
                                    mm = t.matmul(ps[:, dsl],
                                                  hT[:, fc, selt * P:(selt + 1) * P],
                                                  w2h[fi % 2][:, fc, dsl],
                                                  start=(fc == 0), stop=(fc == 15))
                            dnm.inc(mm)

            @blk.scalar
            def _(s):
                for e in range(E):
                    for fh in range(2):
                        fi = 2 * e + fh
                        for ft in range(16):
                            s.wait_ge(upm.sem, 64 * e + 32 * fh + 2 * (ft + 1))
                            if fi >= 1:
                                s.wait_ge(dnm.sem, 8 * fi)
                            gel.inc(s.activation(hT[:, ft, :], pU[ft % 2][:],
                                                 AF.Gelu))

            @blk.vector
            def _(v):
                for e in range(E):
                    for fh in range(2):
                        for selt in range(8):
                            sidx = 16 * e + 8 * fh + selt
                            v.wait_ge(dnm.sem, sidx + 1)
                            if sidx >= 2:
                                scat.wait(v, sidx - 2)
                            dnev.inc(v.tensor_scalar_mul(
                                stage[sidx % 2][:], pD[selt % 2][:],
                                g_slab[:, e * 8 + selt:e * 8 + selt + 1]))

    pr.release(ctrs=["f0_upm", "f0_gel", "f0_dnm", "f0_dnev", "f0_tr"],
               slots=[gth, wrt, w1l, w2l, scat])

def emit_attn1_proj(pr, xnT, W, qraw, kraw, vtok, combT_hbm, c_ident, rw_dram, rb_dram, comb_mode=3):
    skip_comb = comb_mode == 0
    nc = pr.nc
    wld = pr.ctr("a1_wld")
    prm = pr.ctr("a1_prm")
    prr = pr.ctr("a1_prr")
    pct = pr.ctr("a1_pct")
    dvq = pr.ctr("a1_dvq")
    dvr = pr.ctr("a1_dvr")
    ccp = pr.ctr("a1_ccp")
    stq = DmaSlots(nc, "a1_stq", 2)
    stk = DmaSlots(nc, "a1_stk", 2)
    stv = DmaSlots(nc, "a1_stv", 2)
    pr.t["a1_cstslots"] = DmaSlots(nc, "a1_cst", 2)

    with ExitStack() as ctx:
        En = ctx.enter_context
        wq = En(nc.sbuf_tensor("a1_wq", [P, ND, D], BF16))
        wk = En(nc.sbuf_tensor("a1_wk", [P, ND, D], BF16))
        wv = En(nc.sbuf_tensor("a1_wv", [P, ND, D], BF16))
        rw = En(nc.sbuf_tensor("a1_rw", [P, ND, E], BF16))
        brow = En(nc.sbuf_tensor("a1_brow", [1, 3 * D + E], BF16))
        ones = En(nc.sbuf_tensor("a1_ones", [1, 512], BF16))
        qsb = [En(nc.sbuf_tensor(f"a1_q{i}", [P, D], F32)) for i in range(2)]
        ksb = [En(nc.sbuf_tensor(f"a1_k{i}", [P, D], F32)) for i in range(2)]
        vsb = [En(nc.sbuf_tensor(f"a1_v{i}", [P, D], BF16)) for i in range(2)]
        rp = En(nc.sbuf_tensor("a1_rp", [P, 8], F32))
        rst = En(nc.sbuf_tensor("a1_rst", [P, 8], F32))
        eqt = En(nc.sbuf_tensor("a1_eq", [P, 8], F32))
        pr.t["a1_comb"] = En(nc.sbuf_tensor("a1_cmb", [P, NT, E], F32))
        combT_st = En(nc.sbuf_tensor("a1_cTs", [8, 2, P], BF16))
        pP = [En(nc.psum_tensor(f"a1_pP{i}", [P, D], F32)) for i in range(2)]
        pR = En(nc.psum_tensor("a1_pR", [P, E], F32))
        pC = En(nc.psum_tensor("a1_pC", [8, 2, P], F32))

        bq = brow[0:1, 0:D]
        bk = brow[0:1, D:2 * D]
        bv = brow[0:1, 2 * D:3 * D]
        rb = brow[0:1, 3 * D:3 * D + E]

        with nc.Block() as blk:
            @blk.gpsimd
            def _(g):
                for wsb, wd in ((wq, W["wq"]), (wk, W["wk"]), (wv, W["wv"])):
                    wld.inc(g.dma_start(wsb[:, :, :],
                                        wd.ap().rearrange("(c p) e -> p c e", p=P)), 16)
                wld.inc(g.dma_start(rw[:, :, :],
                                    rw_dram.ap().rearrange("(c p) e -> p c e", p=P)), 16)
                for i, bd in enumerate([W["bq"], W["bk"], W["bv"]]):
                    wld.inc(g.dma_start(brow[0:1, i * D:(i + 1) * D], bd.ap()), 16)
                wld.inc(g.dma_start(brow[0:1, 3 * D:3 * D + E], rb_dram.ap()), 16)
                g.wait_ge(wld.sem, wld.n)

            @blk.vector
            def _(v):
                v.memset(ones[:], 1.0)

        with nc.Block() as blk:
            @blk.tensor
            def _(t):
                for tt in range(NT):
                    tsl = slice(tt * P, (tt + 1) * P)
                    for j, (wsb, bias) in enumerate(((wq, bq), (wk, bk), (wv, bv))):
                        pj = 3 * tt + j
                        if pj >= 2:
                            t.wait_ge(dvq.sem, pj - 1)
                        ps = pP[pj % 2]
                        for dh in range(2):
                            dsl = slice(dh * 512, (dh + 1) * 512)
                            for c in range(ND):
                                t.matmul(ps[:, dsl], xnT[:, c, tsl], wsb[:, c, dsl],
                                         start=(c == 0), stop=False)
                            mm = t.matmul(ps[:, dsl], ones[0:1, 0:P], bias[:, dsl],
                                          start=False, stop=True)
                        prm.inc(mm)
                    if tt >= 1:
                        t.wait_ge(dvr.sem, 13 * (tt - 1) + 3)
                    for c in range(ND):
                        t.matmul(pR[:, :], xnT[:, c, tsl], rw[:, c, :],
                                 start=(c == 0), stop=False)
                    prr.inc(t.matmul(pR[:, :], ones[0:1, 0:P], rb[:],
                                     start=False, stop=True))
                    jt = tt - 1
                    if jt >= 0 and not skip_comb:
                        t.wait_ge(dvr.sem, 13 * (jt + 1))
                        if jt >= 2 and comb_mode >= 2:
                            t.wait_ge(ccp.sem, jt - 1)
                        pct.inc(t.transpose(pC[:, jt % 2, :],
                                            comb_sl(pr, jt, None), c_ident[:]))
                for jt in ([] if skip_comb else (NT - 1,)):
                    t.wait_ge(dvr.sem, 13 * (jt + 1))
                    if jt >= 2 and comb_mode >= 2:
                        t.wait_ge(ccp.sem, jt - 1)
                    pct.inc(t.transpose(pC[:, jt % 2, :],
                                        comb_sl(pr, jt, None), c_ident[:]))

            @blk.vector
            def _(v):
                for tt in range(NT + 1):
                    if tt < NT:
                        b = tt % 2
                        v.wait_ge(prm.sem, 3 * tt + 1)
                        if tt >= 2:
                            stq.wait(v, tt - 2)
                        dvq.inc(v.tensor_copy(qsb[b][:], pP[(3 * tt) % 2][:]))
                        v.wait_ge(prm.sem, 3 * tt + 2)
                        if tt >= 2:
                            stk.wait(v, tt - 2)
                        dvq.inc(v.tensor_copy(ksb[b][:], pP[(3 * tt + 1) % 2][:]))
                        v.wait_ge(prm.sem, 3 * tt + 3)
                        if tt >= 2:
                            stv.wait(v, tt - 2)
                        dvq.inc(v.tensor_copy(vsb[b][:], pP[(3 * tt + 2) % 2][:]))
                        v.wait_ge(prr.sem, tt + 1)
                        dvr.inc(v.tensor_reduce(rst[:, 0:1], pR[:], axis=AXX, op=ALU.max))
                        v.wait_ge(dvr.sem, dvr.n)
                        dvr.inc(v.tensor_scalar_mul(rst[:, 1:2], rst[:, 0:1], -1.0))
                        dvr.inc(v.tensor_copy(rp[:], pR[:]))
                        v.wait_ge(dvr.sem, dvr.n)
                        dvr.inc(v.tensor_scalar(rp[:], rp[:], rst[:, 1:2], None, op0=ALU.add))
                        v.wait_ge(dvr.sem, dvr.n)
                        v.wait_ge(pr.ctr("a1_rex").sem, tt + 1)
                        dvr.inc(v.tensor_reduce(rst[:, 2:3], rp[:], axis=AXX, op=ALU.add))
                        v.wait_ge(dvr.sem, dvr.n)
                        dvr.inc(v.reciprocal(rst[:, 3:4], rst[:, 2:3]))
                        v.wait_ge(dvr.sem, dvr.n)
                        dvr.inc(v.tensor_scalar_mul(rp[:], rp[:], rst[:, 3:4]))
                        v.wait_ge(dvr.sem, dvr.n)
                        dvr.inc(v.tensor_reduce(rst[:, 4:5], rp[:], axis=AXX, op=ALU.max))
                        v.wait_ge(dvr.sem, dvr.n)
                        dvr.inc(v.tensor_scalar(eqt[:], rp[:], rst[:, 4:5], -1e30,
                                                op0=ALU.is_ge, op1=ALU.mult))
                        v.wait_ge(dvr.sem, dvr.n)
                        dvr.inc(v.tensor_add(eqt[:], eqt[:], rp[:]))
                        v.wait_ge(dvr.sem, dvr.n)
                        dvr.inc(v.tensor_reduce(rst[:, 5:6], eqt[:], axis=AXX, op=ALU.max))
                        v.wait_ge(dvr.sem, dvr.n)
                        dvr.inc(v.tensor_scalar(eqt[:], rp[:], rst[:, 5:6], None, op0=ALU.is_ge))
                        v.wait_ge(dvr.sem, dvr.n)
                        dvr.inc(v.tensor_mul(comb_sl(pr, tt, None), rp[:], eqt[:]))
                    jt = tt - 1
                    if jt >= 0 and comb_mode >= 2:
                        v.wait_ge(pct.sem, jt + 1)
                        if jt >= 2 and comb_mode >= 3:
                            pr.t["a1_cstslots"].wait(v, jt - 2)
                        ccp.inc(v.tensor_copy(combT_st[:, jt % 2, :],
                                              pC[:, jt % 2, :]))

            @blk.scalar
            def _(s):
                rex = pr.ctr("a1_rex")
                for tt in range(NT):
                    s.wait_ge(pr.c["a1_dvr"].sem, 13 * tt + 4)
                    rex.inc(s.activation(rp[:], rp[:], AF.Exp))

            @blk.sync
            def _(sync):
                combT_v = combT_hbm.ap().rearrange("o (e t) -> (o e) t", e=E)
                cslots = pr.t["a1_cstslots"]
                for tt in range(NT + 1):
                    if tt < NT:
                        b = tt % 2
                        sync.wait_ge(dvq.sem, 3 * tt + 1)
                        stq.issue(tt, sync.dma_start(qraw[tt * P:(tt + 1) * P, :], qsb[b][:]))
                        sync.wait_ge(dvq.sem, 3 * tt + 2)
                        stk.issue(tt, sync.dma_start(kraw[tt * P:(tt + 1) * P, :], ksb[b][:]))
                        sync.wait_ge(dvq.sem, 3 * tt + 3)
                        stv.issue(tt, sync.dma_start(vtok[tt * P:(tt + 1) * P, :], vsb[b][:]))
                    jt = tt - 1
                    if jt >= 0 and comb_mode >= 3:
                        sync.wait_ge(ccp.sem, jt + 1)
                        cslots.issue(jt, sync.dma_start(
                            combT_v[:, jt * P:(jt + 1) * P],
                            combT_st[:, jt % 2, :]))
                stq.wait_all(sync)
                stk.wait_all(sync)
                stv.wait_all(sync)
                if comb_mode >= 3:
                    cslots.wait_all(sync)

    pr.release(ctrs=["a1_wld", "a1_prm", "a1_prr", "a1_pct", "a1_dvq", "a1_dvr",
                     "a1_ccp", "a1_rex"],
               slots=[stq, stk, stv, pr.t.pop("a1_cstslots")])


def comb_sl(pr, tt, e):
    cslab = pr.t["a1_comb"]
    if e is None:
        return cslab[:, tt, :]
    return cslab[:, tt, e:e + 1]

def emit_attn1_core(pr, knT, qnT_hbm, vtok, wo_dram, bo_dram, ls_dram,
                    x1_dram, y_dram):
    nc = pr.nc
    QB = 512
    NQB = T // QB
    wld = pr.ctr("c1_wld")
    fld = pr.ctr("c1_fld")
    smm = pr.ctr("c1_smm")
    sig = pr.ctr("c1_sig")
    avm = pr.ctr("c1_avm")
    ave = pr.ctr("c1_ave")
    omm = pr.ctr("c1_omm")
    oev = pr.ctr("c1_oev")
    qld = DmaSlots(nc, "c1_qld", 2)
    vld = DmaSlots(nc, "c1_vld", 2)
    xld = DmaSlots(nc, "c1_xld", 3)
    yst = DmaSlots(nc, "c1_yst", 3)

    with ExitStack() as ctx:
        En = ctx.enter_context
        wo = En(nc.sbuf_tensor("c1_wo", [P, ND, D], BF16))
        lsb = En(nc.sbuf_tensor("c1_lsb", [1, 2 * D], BF16))
        lsbc = En(nc.sbuf_tensor("c1_lsbc", [P, D], BF16))
        ones = En(nc.sbuf_tensor("c1_ones", [1, P], BF16))
        qnb = [En(nc.sbuf_tensor(f"c1_qnb{i}", [P, ND, QB], BF16)) for i in range(2)]
        vb = [En(nc.sbuf_tensor(f"c1_vb{i}", [P, D], BF16)) for i in range(2)]
        at = En(nc.sbuf_tensor("c1_at", [P, NT, QB], BF16))
        otq = En(nc.sbuf_tensor("c1_otq", [P, ND, QB], BF16))
        xtb = [En(nc.sbuf_tensor(f"c1_xtb{i}", [P, D], F32)) for i in range(3)]
        ysb = [En(nc.sbuf_tensor(f"c1_ysb{i}", [P, D], F32)) for i in range(3)]
        psS = [En(nc.psum_tensor(f"c1_psS{i}", [P, 2 * QB], F32)) for i in range(2)]
        pA = [En(nc.psum_tensor(f"c1_pA{i}", [P, QB], F32)) for i in range(4)]

        with nc.Block() as blk:
            @blk.gpsimd
            def _(g):
                wld.inc(g.dma_start(wo[:, :, :],
                                    wo_dram.ap().rearrange("(c p) e -> p c e", p=P)), 16)
                wld.inc(g.dma_start(lsb[0:1, 0:D], ls_dram.ap()), 16)
                wld.inc(g.dma_start(lsb[0:1, D:2 * D], bo_dram.ap()), 16)
                g.wait_ge(wld.sem, wld.n)

            @blk.vector
            def _(v):
                v.memset(ones[:], 1.0)

        with nc.Block() as blk:
            @blk.tensor
            def _(t):
                for half in range(2):
                    fld.inc(t.matmul(psS[0][:, half * QB:(half + 1) * QB],
                                     ones[0:1, 0:P],
                                     lsb[0:1, half * QB:(half + 1) * QB],
                                     start=True, stop=True))

            @blk.vector
            def _(v):
                v.wait_ge(fld.sem, 2)
                fld.inc(v.tensor_copy(lsbc[:], psS[0][:, 0:D]))
                v.wait_ge(fld.sem, fld.n)
                for c in range(ND):
                    fld.inc(v.tensor_mul(wo[:, c, :], wo[:, c, :], lsbc[:]))
                fld.inc(v.tensor_mul(lsb[0:1, D:2 * D], lsb[0:1, D:2 * D],
                                     lsb[0:1, 0:D]))
                v.wait_ge(fld.sem, fld.n)

        bols = lsb[0:1, D:2 * D]

        with nc.Block() as blk:
            @blk.sync
            def _(sync):
                for qb in range(NQB):
                    if qb >= 2:
                        sync.wait_ge(smm.sem, 32 * (qb - 1))
                    qld.issue(qb, sync.dma_start(
                        qnb[qb % 2][:, :, :],
                        qnT_hbm.ap()[:, :, qb * QB:(qb + 1) * QB]))
                    for i in range(4):
                        ii = 4 * qb + i
                        if ii >= 3:
                            sync.wait_ge(oev.sem, ii - 2)
                        xld.issue(ii, sync.dma_start(
                            xtb[ii % 3][:],
                            x1_dram[(qb * QB + i * P):(qb * QB + (i + 1) * P), :]))

            @blk.tensor
            def _(t):
                for qb in range(NQB):
                    qld.wait(t, qb)
                    for g_ in range(16):
                        if qb * 16 + g_ >= 2:
                            t.wait_ge(sig.sem, qb * 16 + g_ - 1)
                        if g_ < 2 and qb >= 1:
                            t.wait_ge(oev.sem, 4 * qb)
                        ps = psS[g_ % 2]
                        for k2 in range(2):
                            kt_i = 2 * g_ + k2
                            for c in range(ND):
                                mm = t.matmul(ps[:, k2 * QB:(k2 + 1) * QB],
                                              knT[:, c, kt_i * P:(kt_i + 1) * P],
                                              qnb[qb % 2][:, c, :],
                                              start=(c == 0), stop=(c == ND - 1))
                            smm.inc(mm)
                    for ps_ in range(2):
                        for kc in range(NT):
                            vi = qb * 64 + ps_ * NT + kc
                            vld.wait(t, vi)
                            if kc % 2 == 0:
                                t.wait_ge(sig.sem, qb * 16 + kc // 2 + 1)
                            if kc == 0 and (qb * 2 + ps_) >= 1:
                                t.wait_ge(ave.sem, 4 * (2 * qb + ps_))
                            for et in range(4):
                                mm = t.matmul(pA[et][:, :],
                                              vb[vi % 2][:, (ps_ * 4 + et) * P:
                                                          (ps_ * 4 + et + 1) * P],
                                              at[:, kc, :],
                                              start=(kc == 0), stop=(kc == NT - 1))
                            avm.inc(mm)
                    t.wait_ge(ave.sem, 8 * (qb + 1))
                    for tl in range(4):
                        ii = 4 * qb + tl
                        if ii >= 2:
                            t.wait_ge(oev.sem, ii - 1)
                        ps = psS[tl % 2]
                        for dh in range(2):
                            dsl = slice(dh * QB, (dh + 1) * QB)
                            for c in range(ND):
                                t.matmul(ps[:, dsl], otq[:, c, tl * P:(tl + 1) * P],
                                         wo[:, c, dsl], start=(c == 0), stop=False)
                            mm = t.matmul(ps[:, dsl], ones[0:1, 0:P], bols[:, dsl],
                                          start=False, stop=True)
                        omm.inc(mm)

            @blk.scalar
            def _(s):
                for qb in range(NQB):
                    for g_ in range(16):
                        s.wait_ge(smm.sem, qb * 32 + 2 * (g_ + 1))
                        if qb >= 1:
                            s.wait_ge(avm.sem, 64 * (qb - 1) + 32 + 2 * g_ + 2)
                        sig.inc(s.activation(at[:, 2 * g_:2 * g_ + 2, :],
                                             psS[g_ % 2][:, :], AF.Sigmoid,
                                             scale=1.0 / 32.0,
                                             bias=pr.const(-8.317766166719343)))

            @blk.vector
            def _(v):
                for qb in range(NQB):
                    for ps_ in range(2):
                        v.wait_ge(avm.sem, qb * 64 + 32 * (ps_ + 1))
                        for et in range(4):
                            ave.inc(v.tensor_copy(otq[:, ps_ * 4 + et, :], pA[et][:, :]))
                    for tl in range(4):
                        ii = 4 * qb + tl
                        v.wait_ge(omm.sem, ii + 1)
                        xld.wait(v, ii)
                        if ii >= 3:
                            yst.wait(v, ii - 3)
                        oev.inc(v.tensor_add(ysb[ii % 3][:], xtb[ii % 3][:],
                                             psS[tl % 2][:]))

            @blk.gpsimd
            def _(g):
                for qb in range(NQB):
                    for ps in range(2):
                        for kc in range(NT):
                            vi = qb * 64 + ps * NT + kc
                            if vi >= 2:
                                g.wait_ge(avm.sem, vi - 1)
                            vld.issue(vi, g.dma_start(
                                vb[vi % 2][:], vtok[kc * P:(kc + 1) * P, :]))
                    for tl in range(4):
                        ii = 4 * qb + tl
                        g.wait_ge(oev.sem, ii + 1)
                        yst.issue(ii, g.dma_start(
                            y_dram[(qb * QB + tl * P):(qb * QB + (tl + 1) * P), :],
                            ysb[ii % 3][:]))
                yst.wait_all(g)
                vld.wait_all(g)

    pr.release(ctrs=["c1_wld", "c1_fld", "c1_smm", "c1_sig", "c1_avm", "c1_ave",
                     "c1_omm", "c1_oev"],
               slots=[qld, vld, xld, yst])


def emit_ffn1(pr, xnT_hbm, combT, up_dram, dn_dram, y_dram):
    nc = pr.nc
    TB = 512
    NTB = T // TB
    upm = pr.ctr("f1_upm")
    gel = pr.ctr("f1_gel")
    cbm = pr.ctr("f1_cbm")
    cbb = pr.ctr("f1_cbb")
    cbe = pr.ctr("f1_cbe")
    dnm = pr.ctr("f1_dnm")
    dnev = pr.ctr("f1_dnev")
    wul = DmaSlots(nc, "f1_wul", 1)
    wdl = DmaSlots(nc, "f1_wdl", 1)
    xbl = DmaSlots(nc, "f1_xbl", 2)
    acc = DmaSlots(nc, "f1_acc", 8)
    cbl = DmaSlots(nc, "f1_cbl", 1)

    with ExitStack() as ctx:
        En = ctx.enter_context
        up = En(nc.sbuf_tensor("f1_up", [P, ND, FF], BF16))
        dn = En(nc.sbuf_tensor("f1_dn", [P, NF, D], BF16))
        hq = En(nc.sbuf_tensor("f1_hq", [P, NF, TB], BF16))
        xb = [En(nc.sbuf_tensor(f"f1_xb{i}", [P, ND, TB], BF16)) for i in range(2)]
        cb = En(nc.sbuf_tensor("f1_cb", [P, TB], BF16))
        cbrow = En(nc.sbuf_tensor("f1_cbrow", [1, T], BF16))
        stg = [En(nc.sbuf_tensor(f"f1_stg{i}", [P, TB], F32)) for i in range(8)]
        ones = En(nc.sbuf_tensor("f1_ones", [1, P], BF16))
        pU = [En(nc.psum_tensor(f"f1_pU{i}", [P, TB], F32)) for i in range(2)]
        pD = [En(nc.psum_tensor(f"f1_pD{i}", [P, TB], F32)) for i in range(2)]
        pC = En(nc.psum_tensor("f1_pC", [P, TB], F32))

        with nc.Block() as blk:
            @blk.vector
            def _(v):
                v.memset(ones[:], 1.0)

        with nc.Block() as blk:
            @blk.gpsimd
            def _(g):
                for e in range(E):
                    if e > 0:
                        g.wait_ge(cbb.sem, NTB * e)
                    cbl.issue(e, g.dma_start(cbrow[0:1, :],
                                             combT[0:1, e * T:(e + 1) * T]))
                    if e > 0:
                        g.wait_ge(upm.sem, 32 * NTB * e)
                    wul.issue(e, g.dma_start(
                        up[:, :, :], up_dram.ap()[e].rearrange("(c p) f -> p c f", p=P)))
                    if e > 0:
                        g.wait_ge(dnm.sem, 8 * NTB * e)
                    wdl.issue(e, g.dma_start(
                        dn[:, :, :], dn_dram.ap()[e].rearrange("(c p) d -> p c d", p=P)))
                    def acc_batch(g, tj):
                        tbj = tj % NTB
                        for tl in range(4):
                            for dh in range(2):
                                si = tj * 8 + tl * 2 + dh
                                g.wait_ge(dnev.sem, si + 1)
                                acc.issue(si, g.dma_start(
                                    y_dram[(tbj * TB + tl * P):(tbj * TB + (tl + 1) * P),
                                           dh * TB:(dh + 1) * TB],
                                    stg[si % 8][:], accum_op=ALU.add))
                    for tb in range(NTB):
                        ti = e * NTB + tb
                        if ti >= 2:
                            g.wait_ge(upm.sem, 32 * (ti - 1))
                        xbl.issue(ti, g.dma_start(
                            xb[ti % 2][:, :, :],
                            xnT_hbm.ap()[:, :, tb * TB:(tb + 1) * TB]))
                        if ti >= 1:
                            acc_batch(g, ti - 1)
                    if e == E - 1:
                        acc_batch(g, E * NTB - 1)
                acc.wait_all(g)

            @blk.tensor
            def _(t):
                for e in range(E):
                    wul.wait(t, e)
                    for tb in range(NTB):
                        ti = e * NTB + tb
                        xbl.wait(t, ti)
                        if ti >= 1:
                            t.wait_ge(cbe.sem, ti)
                        if tb == 0:
                            cbl.wait(t, e)
                        cbb.inc(t.matmul(pC[:, :], ones[0:1, 0:P],
                                         cbrow[0:1, tb * TB:(tb + 1) * TB],
                                         start=True, stop=True))
                        for ft in range(NF):
                            gg = ti * 32 + ft
                            if gg >= 2:
                                t.wait_ge(gel.sem, gg - 1)
                            ps = pU[ft % 2]
                            for c in range(ND):
                                mm = t.matmul(ps[:, :], up[:, c, ft * P:(ft + 1) * P],
                                              xb[ti % 2][:, c, :],
                                              start=(c == 0), stop=(c == ND - 1))
                            upm.inc(mm)
                        if tb == 0:
                            wdl.wait(t, e)
                        t.wait_ge(cbm.sem, 32 * (ti + 1))
                        for tl in range(4):
                            for dh in range(2):
                                dg = ti * 8 + tl * 2 + dh
                                if dg >= 2:
                                    t.wait_ge(dnev.sem, dg - 1)
                                ps = pD[dg % 2]
                                for fc in range(NF):
                                    mm = t.matmul(ps[:, :], hq[:, fc, tl * P:(tl + 1) * P],
                                                  dn[:, fc, dh * TB:(dh + 1) * TB],
                                                  start=(fc == 0), stop=(fc == NF - 1))
                                dnm.inc(mm)

            @blk.scalar
            def _(s):
                for e in range(E):
                    for tb in range(NTB):
                        ti = e * NTB + tb
                        for ft in range(NF):
                            s.wait_ge(upm.sem, ti * 32 + ft + 1)
                            if ti >= 1:
                                s.wait_ge(dnm.sem, 8 * ti)
                            gel.inc(s.activation(hq[:, ft, :], pU[ft % 2][:], AF.Gelu))

            @blk.vector
            def _(v):
                for e in range(E):
                    for tb in range(NTB):
                        ti = e * NTB + tb
                        v.wait_ge(cbb.sem, ti + 1)
                        if ti >= 1:
                            v.wait_ge(cbm.sem, 32 * ti)
                        cbe.inc(v.tensor_copy(cb[:], pC[:, :]))
                        v.wait_ge(cbe.sem, cbe.n)
                        for ft in range(NF):
                            v.wait_ge(gel.sem, ti * 32 + ft + 1)
                            cbm.inc(v.tensor_mul(hq[:, ft, :], hq[:, ft, :], cb[:]))
                        for tl in range(4):
                            for dh in range(2):
                                dg = ti * 8 + tl * 2 + dh
                                v.wait_ge(dnm.sem, dg + 1)
                                if dg >= 8:
                                    acc.wait(v, dg - 8)
                                dnev.inc(v.tensor_copy(stg[dg % 8][:], pD[dg % 2][:]))

    pr.release(ctrs=["f1_upm", "f1_gel", "f1_cbm", "f1_cbb", "f1_cbe", "f1_dnm",
                     "f1_dnev"],
               slots=[wul, wdl, xbl, acc, cbl])
def make_consts(nc):
    ones128 = nc.inline_tensor(np.ones((P, P), np.float32), "c_ones128")
    utri = nc.inline_tensor(np.triu(np.ones((P, P), np.float32)), "c_utri")
    slots = nc.inline_tensor(
        np.tile(np.arange(1, SEL + 1, dtype=np.float32), (P, 1)), "c_slots")
    tok = nc.inline_tensor(
        (np.arange(NT, dtype=np.float32)[None, :] * P
         + np.arange(P, dtype=np.float32)[:, None]), "c_tok")
    return {"ones": ones128, "utri": utri, "slots": slots, "tok": tok}


def emit_load_consts(pr, cdram, csb):
    nc = pr.nc
    ld = pr.ctr("cst_ld")
    with nc.Block() as blk:
        @blk.sync
        def _(sync):
            for k in cdram:
                ld.inc(sync.dma_start(csb[k][:], cdram[k].ap()), 16)
            sync.wait_ge(ld.sem, ld.n)


def build_program(debug_stop=None):
    nc = bass.Bass("TRN2", target_bir_lowering=False)
    pr = Prog(nc)
    l1dbg = debug_stop in ("ln1", "attn1p", "attn1q", "attn1t", "attn1u", "lnq1", "attn1c")
    full = debug_stop is None or l1dbg

    x = nc.dram_tensor("x", [T, D], F32, kind="ExternalInput")
    gwT = nc.dram_tensor("gwT", [D, E], F32, kind="ExternalInput")
    W0 = {}
    for nm in ("wq", "wk", "wv", "wo"):
        W0[nm] = nc.dram_tensor(nm + "0", [D, D], F32, kind="ExternalInput")
    for nm in ("bq", "bk", "bv", "bo"):
        W0[nm] = nc.dram_tensor(nm + "0", [1, D], F32, kind="ExternalInput")
    w1 = nc.dram_tensor("w1", [E, D, FF], F32, kind="ExternalInput")
    w2 = nc.dram_tensor("w2", [E, FF, D], F32, kind="ExternalInput")
    W1 = {}
    if full:
        for nm in ("wq", "wk", "wv", "wo"):
            W1[nm] = nc.dram_tensor(nm + "1", [D, D], F32, kind="ExternalInput")
        for nm in ("bq", "bk", "bv", "bo"):
            W1[nm] = nc.dram_tensor(nm + "1", [1, D], F32, kind="ExternalInput")
        ls = nc.dram_tensor("ls", [1, D], F32, kind="ExternalInput")
        rw = nc.dram_tensor("rw", [D, E], F32, kind="ExternalInput")
        rb = nc.dram_tensor("rb", [1, E], F32, kind="ExternalInput")
        upw = nc.dram_tensor("upw", [E, D, FF], F32, kind="ExternalInput")
        dnw = nc.dram_tensor("dnw", [E, FF, D], F32, kind="ExternalInput")
    y = nc.dram_tensor("y", [T, D], F32, kind="ExternalOutput")

    xn_tok = nc.dram_tensor("xn_tok", [T, D], BF16, kind="Internal")
    x1 = nc.dram_tensor("x1", [T, D], F32, kind="Internal")
    outT_hbm = nc.dram_tensor("outT", [D, T], BF16, kind="Internal")
    xin_tok = nc.dram_tensor("xin_tok", [SEL, D], BF16, kind="Internal")
    if full:
        qraw = nc.dram_tensor("qraw", [T, D], F32, kind="Internal")
        kraw = nc.dram_tensor("kraw", [T, D], F32, kind="Internal")
        vtok1 = nc.dram_tensor("vtok1", [T, D], BF16, kind="Internal")
        qn_tok = nc.dram_tensor("qn_tok", [T, D], BF16, kind="Internal")
        kn_tok = nc.dram_tensor("kn_tok", [T, D], BF16, kind="Internal")
        xnT_hbm = nc.dram_tensor("xnT_hbm", [P, ND, T], BF16, kind="Internal")
        qnT_hbm = nc.dram_tensor("qnT_hbm", [P, ND, T], BF16, kind="Internal")
        combT_hbm = nc.dram_tensor("combT_hbm", [1, E * T], BF16, kind="Internal")

    cdram = make_consts(nc)
    c_ident_d = nc.inline_tensor(np.eye(P, dtype=np.float32), "c_ident")

    with ExitStack() as perm:
        En = perm.enter_context
        pr.init_consts(perm, [LN_EPS, -float(np.log(T))])
        l0s = ExitStack()
        En0 = l0s.enter_context
        csb = {
            "ones": En0(nc.sbuf_tensor("cs_ones", [P, P], F32)),
            "utri": En0(nc.sbuf_tensor("cs_utri", [P, P], F32)),
            "slots": En0(nc.sbuf_tensor("cs_slots", [P, SEL], F32)),
            "tok": En0(nc.sbuf_tensor("cs_tok", [P, NT], F32)),
        }
        m_slab = En0(nc.sbuf_tensor("m_slab", [P, E * 8], I32))
        g_slab = En0(nc.sbuf_tensor("g_slab", [P, E * 8], F32))
        probs = En0(nc.sbuf_tensor("probs", [P, E, NT], F32))

        emit_load_consts(pr, cdram, csb)

        emit_ln(pr, x.ap(), xn_tok.ap(), "0")
        if debug_stop == "ln":
            xn_dbg = nc.dram_tensor("xn_dbg", [T, D], BF16, kind="ExternalOutput")
            xnT_dbg = nc.dram_tensor("xnT_dbg", [P, ND, T], BF16, kind="ExternalOutput")
            with ExitStack() as ctx:
                xnT = ctx.enter_context(nc.sbuf_tensor("xnT", [P, ND, T], BF16))
                emit_xnT(pr, xn_tok.ap(), xnT, "0")
                cp = pr.ctr("dbgcp")
                with nc.Block() as blk:
                    @blk.sync
                    def _(sync):
                        cp.inc(sync.dma_start(xn_dbg.ap(), xn_tok.ap()), 16)
                        cp.inc(sync.dma_start(xnT_dbg.ap(), xnT[:, :, :]), 16)
                        sync.wait_ge(cp.sem, cp.n)
            l0s.close()
            return nc

        dbg = None
        if debug_stop == "sel":
            dbg = {
                "probs": nc.dram_tensor("d_probs", [P, E, NT], F32, kind="ExternalOutput"),
                "tlo": nc.dram_tensor("d_tlo", [P, E], F32, kind="ExternalOutput"),
                "m": nc.dram_tensor("d_m", [P, E * 8], I32, kind="ExternalOutput"),
                "g": nc.dram_tensor("d_g", [P, E * 8], F32, kind="ExternalOutput"),
            }
        with ExitStack() as ctx:
            xnT = ctx.enter_context(nc.sbuf_tensor("xnT", [P, ND, T], BF16))
            emit_xnT(pr, xn_tok.ap(), xnT, "0")
            emit_sel(pr, xnT, gwT, csb, m_slab, g_slab, probs, dbg=dbg)
            if debug_stop != "sel":
                with ExitStack() as actx:
                    emit_attn0(pr, xnT, W0, x.ap(), x1.ap(), outT_hbm, actx)
        if debug_stop == "sel":
            l0s.close()
            return nc
        if debug_stop == "attn0":
            x1_dbg = nc.dram_tensor("x1_dbg", [T, D], F32, kind="ExternalOutput")
            _dbg_copy(pr, nc, x1_dbg, x1)
            l0s.close()
            return nc
        emit_ffn0(pr, xn_tok.ap(), xin_tok, x1.ap(), w1, w2, m_slab, g_slab)
        l0s.close()
        if debug_stop == "l0":
            x1_dbg = nc.dram_tensor("x1_dbg", [T, D], F32, kind="ExternalOutput")
            _dbg_copy(pr, nc, x1_dbg, x1)
            return nc

        emit_ln(pr, x1.ap(), xn_tok.ap(), "1")
        if debug_stop == "ln1":
            xn_dbg = nc.dram_tensor("xn_dbg", [T, D], BF16, kind="ExternalOutput")
            xnT_dbg = nc.dram_tensor("xnT_dbg", [P, ND, T], BF16, kind="ExternalOutput")
            with ExitStack() as ctx:
                xnT1 = ctx.enter_context(nc.sbuf_tensor("xnT1", [P, ND, T], BF16))
                emit_xnT(pr, xn_tok.ap(), xnT1, "1", xnT_hbm=xnT_hbm)
                _dbg_copy(pr, nc, xn_dbg, xn_tok)
                _dbg_copy(pr, nc, xnT_dbg, xnT_hbm)
            return nc
        with ExitStack() as ctx:
            xnT1 = ctx.enter_context(nc.sbuf_tensor("xnT1", [P, ND, T], BF16))
            emit_xnT(pr, xn_tok.ap(), xnT1, "1", xnT_hbm=xnT_hbm)
            csb_ident = ctx.enter_context(nc.sbuf_tensor("cs_ident", [P, P], F32))
            ldi = pr.ctr("ldi")
            with nc.Block() as blk:
                @blk.sync
                def _(sync):
                    ldi.inc(sync.dma_start(csb_ident[:], c_ident_d.ap()), 16)
                    sync.wait_ge(ldi.sem, ldi.n)
            emit_attn1_proj(pr, xnT1, W1, qraw.ap(), kraw.ap(), vtok1.ap(),
                            combT_hbm, csb_ident, rw, rb,
                            comb_mode={"attn1q": 0, "attn1t": 1,
                                       "attn1u": 2}.get(debug_stop, 3))
        if debug_stop in ("attn1q", "attn1t", "attn1u"):
            q_dbg = nc.dram_tensor("q_dbg", [T, D], F32, kind="ExternalOutput")
            k_dbg = nc.dram_tensor("k_dbg", [T, D], F32, kind="ExternalOutput")
            _dbg_copy(pr, nc, q_dbg, qraw)
            _dbg_copy(pr, nc, k_dbg, kraw)
            return nc
        if debug_stop == "attn1p":
            q_dbg = nc.dram_tensor("q_dbg", [T, D], F32, kind="ExternalOutput")
            k_dbg = nc.dram_tensor("k_dbg", [T, D], F32, kind="ExternalOutput")
            cT_dbg = nc.dram_tensor("cT_dbg", [1, E * T], BF16, kind="ExternalOutput")
            _dbg_copy(pr, nc, q_dbg, qraw)
            _dbg_copy(pr, nc, k_dbg, kraw)
            _dbg_copy(pr, nc, cT_dbg, combT_hbm)
            return nc
        emit_ln(pr, qraw.ap(), qn_tok.ap(), "q1")
        emit_ln(pr, kraw.ap(), kn_tok.ap(), "k1")
        if debug_stop == "lnq1":
            qn_dbg = nc.dram_tensor("qn_dbg", [T, D], BF16, kind="ExternalOutput")
            kn_dbg = nc.dram_tensor("kn_dbg", [T, D], BF16, kind="ExternalOutput")
            _dbg_copy(pr, nc, qn_dbg, qn_tok)
            _dbg_copy(pr, nc, kn_dbg, kn_tok)
            return nc
        with ExitStack() as ctx:
            slab2 = ctx.enter_context(nc.sbuf_tensor("slab2", [P, ND, T], BF16))
            emit_xnT(pr, qn_tok.ap(), slab2, "tq", xnT_hbm=qnT_hbm)
            emit_xnT(pr, kn_tok.ap(), slab2, "tk")
            emit_attn1_core(pr, slab2, qnT_hbm, vtok1.ap(), W1["wo"], W1["bo"],
                            ls, x1.ap(), y.ap())
        if debug_stop == "attn1c":
            return nc
        emit_ffn1(pr, xnT_hbm, combT_hbm, upw, dnw, y.ap())
    return nc


def _dbg_copy(pr, nc, dst, srcd):
    cp = pr.ctr("dbgcp2")
    with nc.Block() as blk:
        @blk.sync
        def _(sync):
            cp.inc(sync.dma_start(dst.ap(), srcd.ap()), 16)
            sync.wait_ge(cp.sem, cp.n)


def _c(a):
    return np.ascontiguousarray(np.asarray(a, dtype=np.float32))


def kernel(L0_attn_x, L0_attn_Wq, L0_attn_bq, L0_attn_Wk, L0_attn_bk,
           L0_attn_Wv, L0_attn_bv, L0_attn_Wo, L0_attn_bo,
           L0_ffn_gate_w, L0_ffn_w1, L0_ffn_w2,
           L1_attn_Wq, L1_attn_bq, L1_attn_Wk, L1_attn_bk, L1_attn_Wv,
           L1_attn_bv, L1_attn_Wo, L1_attn_bo, L1_attn_layerscale,
           L1_ffn_router_w, L1_ffn_router_b, L1_ffn_expert_up,
           L1_ffn_expert_down):
    nc = build_program()
    common = {
        "gwT": _c(np.asarray(L0_ffn_gate_w).T),
        "wq0": _c(np.asarray(L0_attn_Wq).T), "wk0": _c(np.asarray(L0_attn_Wk).T),
        "wv0": _c(np.asarray(L0_attn_Wv).T), "wo0": _c(np.asarray(L0_attn_Wo).T),
        "bq0": _c(L0_attn_bq)[None, :], "bk0": _c(L0_attn_bk)[None, :],
        "bv0": _c(L0_attn_bv)[None, :], "bo0": _c(L0_attn_bo)[None, :],
        "w1": _c(L0_ffn_w1), "w2": _c(L0_ffn_w2),
        "wq1": _c(np.asarray(L1_attn_Wq).T), "wk1": _c(np.asarray(L1_attn_Wk).T),
        "wv1": _c(np.asarray(L1_attn_Wv).T), "wo1": _c(np.asarray(L1_attn_Wo).T),
        "bq1": _c(L1_attn_bq)[None, :], "bk1": _c(L1_attn_bk)[None, :],
        "bv1": _c(L1_attn_bv)[None, :], "bo1": _c(L1_attn_bo)[None, :],
        "ls": _c(L1_attn_layerscale)[None, :],
        "rw": _c(L1_ffn_router_w), "rb": _c(L1_ffn_router_b)[None, :],
        "upw": _c(np.asarray(L1_ffn_expert_up).transpose(0, 2, 1)),
        "dnw": _c(np.asarray(L1_ffn_expert_down).transpose(0, 2, 1)),
    }
    xs = np.asarray(L0_attn_x, dtype=np.float32)
    in_maps = [dict(common, x=np.ascontiguousarray(xs[b])) for b in range(xs.shape[0])]
    res = run_bass_kernel_spmd(nc, in_maps, core_ids=list(range(xs.shape[0])))
    return np.stack([r["y"] for r in res.results], axis=0)


# version 04
# speedup vs baseline: 1.0894x; 1.0894x over previous
import sys
sys.path.insert(0, "/opt/trn_rl_repo")
import numpy as np
import ml_dtypes
import concourse.bass as bass
import concourse.mybir as mybir
from concourse.bass import IndirectOffsetOnAxis
from concourse.bass_utils import run_bass_kernel_spmd
from contextlib import ExitStack

F32 = mybir.dt.float32
BF16 = mybir.dt.bfloat16
I32 = mybir.dt.int32
AF = mybir.ActivationFunctionType
ALU = mybir.AluOpType
AXX = mybir.AxisListType.X

P = 128
T = 4096
D = 1024
ND = D // P
NT = T // P
FF = 4096
NF = FF // P
E = 8
H = 16
HD = 64
SEL = 1024
LN_EPS = 1e-5
BISECT_ITERS = 34

BF = ml_dtypes.bfloat16


class Ctr:
    def __init__(self, nc, name):
        self.sem = nc.alloc_semaphore(name)
        self.n = 0

    def inc(self, inst, k=1):
        self.n += k
        return inst.then_inc(self.sem, k)


class DmaSlots:
    def __init__(self, nc, name, nslots):
        self.nc = nc
        self.sems = [nc.alloc_semaphore(f"{name}{i}") for i in range(nslots)]
        self.cnt = [0] * nslots
        self.tgt = {}
        self.G = nslots

    def issue(self, i, inst):
        s = i % self.G
        self.cnt[s] += 16
        self.tgt[i] = (s, self.cnt[s])
        return inst.then_inc(self.sems[s], 16)

    def wait(self, eng, i):
        if i in self.tgt:
            s, t = self.tgt[i]
        else:
            s, t = i % self.G, 16 * (i // self.G + 1)
        eng.wait_ge(self.sems[s], t)

    def wait_all(self, eng):
        for s in range(self.G):
            if self.cnt[s]:
                eng.wait_ge(self.sems[s], self.cnt[s])


class Prog:
    def __init__(self, nc):
        self.nc = nc
        self.c = {}
        self.t = {}
        self.consts = {}
        self._const_stack = None

    def ctr(self, name):
        if name not in self.c:
            self.c[name] = Ctr(self.nc, name)
        return self.c[name]

    def release(self, ctrs=(), slots=()):
        nc = self.nc
        sems = []
        for c in ctrs:
            if isinstance(c, str):
                c = self.c.pop(c)
            sems.append(c.sem)
        for sl in slots:
            sems.extend(sl.sems)
        with nc.Block() as blk:
            @blk.gpsimd
            def _(g):
                for s in sems:
                    g.dma_reset(semaphore_range=range(s.num, s.num + 1))

            @blk.sync
            def _(sync):
                for s in sems:
                    sync.sem_clear(s)
        for s in sems:
            nc.release_semaphore(s)

    def init_consts(self, stack, values):
        nc = self.nc
        tile = stack.enter_context(
            nc.sbuf_tensor("consts", [P, max(1, len(values))], F32))
        for i, v in enumerate(values):
            self.consts[float(v)] = tile[:, i:i + 1]
        with nc.Block() as blk:
            @blk.vector
            def _(v_):
                for i, v in enumerate(values):
                    v_.memset(tile[:, i:i + 1], float(v))

    def const(self, v):
        return self.consts[float(v)]


def emit_ln(pr, x_src, xn_tok, tag):
    nc = pr.nc
    G = 4
    nc_ = pr.nc
    ld = DmaSlots(nc_, f"lnld{tag}", 4)
    out = DmaSlots(nc_, f"lnou{tag}", 4)
    sq = pr.ctr(f"lnsq{tag}")
    st = pr.ctr(f"lnst{tag}")
    rs = pr.ctr(f"lnrs{tag}")
    fin = pr.ctr(f"lnfi{tag}")
    sq0, st0, rs0, fin0 = sq.n, st.n, rs.n, fin.n

    with ExitStack() as ctx:
        En = ctx.enter_context
        xb = [En(nc.sbuf_tensor(f"lnx{i}{tag}", [P, D], F32)) for i in range(G)]
        sqb = [En(nc.sbuf_tensor(f"lnq{i}{tag}", [P, D], F32)) for i in range(G)]
        xnb = [En(nc.sbuf_tensor(f"lnn{i}{tag}", [P, D], BF16)) for i in range(G)]
        stt = En(nc.sbuf_tensor(f"lns{tag}", [P, 8 * G], F32))

        with nc.Block() as blk:
            @blk.sync
            def _(sync):
                for i in range(NT):
                    if i >= G:
                        sync.wait_ge(fin.sem, fin0 + 2 * (i - G + 1))
                    ld.issue(i, sync.dma_start(xb[i % G][:], x_src[i * P:(i + 1) * P, :]))

            @blk.scalar
            def _(s):
                for i in range(NT + 1):
                    if i < NT:
                        b = i % G
                        ld.wait(s, i)
                        if i >= G:
                            s.wait_ge(st.sem, st0 + 6 * (i - G + 1))
                        sq.inc(s.activation(sqb[b][:], xb[b][:], AF.Square))
                    j = i - 1
                    if j >= 0:
                        o = 8 * (j % G)
                        s.wait_ge(st.sem, st0 + 6 * (j + 1))
                        if j >= G:
                            s.wait_ge(fin.sem, fin0 + 2 * (j - G + 1))
                        rs.inc(s.activation(stt[:, o + 4:o + 5], stt[:, o + 3:o + 4],
                                            AF.Sqrt, bias=pr.const(LN_EPS)))

            @blk.vector
            def _(v):
                for i in range(NT + 1):
                    if i < NT:
                        b = i % G
                        o = 8 * b
                        v.wait_ge(sq.sem, sq0 + (i + 1))
                        if i >= G:
                            v.wait_ge(fin.sem, fin0 + 2 * (i - G + 1))
                        st.inc(v.tensor_reduce(stt[:, o:o + 1], xb[b][:], axis=AXX, op=ALU.add))
                        st.inc(v.tensor_reduce(stt[:, o + 1:o + 2], sqb[b][:], axis=AXX, op=ALU.add))
                        v.wait_ge(st.sem, st.n)
                        st.inc(v.tensor_scalar_mul(stt[:, o + 2:o + 3], stt[:, o:o + 1], 1.0 / D))
                        st.inc(v.tensor_scalar_mul(stt[:, o + 1:o + 2], stt[:, o + 1:o + 2], 1.0 / D))
                        v.wait_ge(st.sem, st.n)
                        st.inc(v.tensor_tensor(stt[:, o + 3:o + 4], stt[:, o + 2:o + 3],
                                               stt[:, o + 2:o + 3], op=ALU.mult))
                        v.wait_ge(st.sem, st.n)
                        st.inc(v.tensor_sub(stt[:, o + 3:o + 4], stt[:, o + 1:o + 2],
                                            stt[:, o + 3:o + 4]))
                    j = i - 1
                    if j >= 0:
                        b = j % G
                        o = 8 * b
                        v.wait_ge(rs.sem, rs0 + (j + 1))
                        fin.inc(v.reciprocal(stt[:, o + 4:o + 5], stt[:, o + 4:o + 5]))
                        v.wait_ge(fin.sem, fin.n)
                        if j >= G:
                            out.wait(v, j - G)
                        fin.inc(v.tensor_scalar(xnb[b][:], xb[b][:],
                                                stt[:, o + 2:o + 3], stt[:, o + 4:o + 5],
                                                op0=ALU.subtract, op1=ALU.mult))

            @blk.gpsimd
            def _(g):
                for j in range(NT):
                    g.wait_ge(fin.sem, fin0 + 2 * (j + 1))
                    out.issue(j, g.dma_start(xn_tok[j * P:(j + 1) * P, :], xnb[j % G][:]))
                out.wait_all(g)

    pr.release(ctrs=[f"lnsq{tag}", f"lnst{tag}", f"lnrs{tag}", f"lnfi{tag}"],
               slots=[ld, out])


def emit_xnT(pr, xn_tok, xnT_slab, tag, xnT_hbm=None):
    nc = pr.nc
    tr = pr.ctr(f"xt{tag}")
    with nc.Block() as blk:
        @blk.sync
        def _(sync):
            for c in range(ND):
                tr.inc(sync.dma_start(xnT_slab[:, c, :],
                                      xn_tok[:, c * P:(c + 1) * P],
                                      transpose=True), 16)
            sync.wait_ge(tr.sem, tr.n)
            if xnT_hbm is not None:
                tr.inc(sync.dma_start(xnT_hbm.ap(), xnT_slab[:, :, :]), 16)
                sync.wait_ge(tr.sem, tr.n)
    pr.release(ctrs=[f"xt{tag}"])


def emit_sel(pr, xnT, gw_dram, cst, m_slab, g_slab, probs, c_ident, dbg=None):
    nc = pr.nc
    gmm = pr.ctr("sl_gmm")
    gpre = pr.ctr("sl_pre")
    gexp = pr.ctr("sl_exp")
    gpos = pr.ctr("sl_pos")
    bis = pr.ctr("sl_bis")
    bmm = pr.ctr("sl_bmm")
    rnk = pr.ctr("sl_rnk")
    rmm = pr.ctr("sl_rmm")
    ld = pr.ctr("sl_ld")

    c_ones, c_utri, c_slots, c_tok = cst["ones"], cst["utri"], cst["slots"], cst["tok"]

    with ExitStack() as ctx:
        En = ctx.enter_context
        gw = En(nc.sbuf_tensor("gw", [P, ND, E], BF16))
        smax = En(nc.sbuf_tensor("smax", [P, 2, 2], F32))
        sexp = En(nc.sbuf_tensor("sexp", [P, 2, E], F32))
        sstat = En(nc.sbuf_tensor("sstat", [P, 2, 2], F32))
        theta = En(nc.sbuf_tensor("theta", [P, E], F32))
        tlo = En(nc.sbuf_tensor("tlo", [P, E], F32))
        tmp8 = En(nc.sbuf_tensor("tmp8", [P, E], F32))
        d2 = En(nc.sbuf_tensor("d2", [P, E], F32))
        cnt = En(nc.sbuf_tensor("cnt", [P, E], F32))
        maskb = En(nc.sbuf_tensor("maskb", [P, E, NT], F32))
        toff = En(nc.sbuf_tensor("toff", [1, 2 * 64], F32))
        rankp = En(nc.sbuf_tensor("rankp", [P, NT], F32))
        mtile = En(nc.sbuf_tensor("mtile", [P, 2, SEL], F32))
        mgr_all = En(nc.sbuf_tensor("mgr_all", [P, NT, 2], F32))
        mg_row = En(nc.sbuf_tensor("mg_row", [2, E * P], F32))
        p_gate = En(nc.psum_tensor("p_gate", [P, 2, E], F32))
        p_cnt = En(nc.psum_tensor("p_cnt", [P, E * NT], F32))
        p_pref = En(nc.psum_tensor("p_pref", [P, NT], F32))
        p_tot = En(nc.psum_tensor("p_tot", [1, NT], F32))
        p_mgN = En(nc.psum_tensor("p_mgN", [2, 2, 512], F32))
        p_tr = En(nc.psum_tensor("p_tr", [P, 2, 512], F32))

        with nc.Block() as blk:
            @blk.gpsimd
            def _(g):
                ld.inc(g.dma_start(gw[:, :, :], gw_dram.ap().rearrange(
                    "(c p) e -> p c e", p=P)), 16)
                g.wait_ge(ld.sem, ld.n)

        with nc.Block() as blk:
            @blk.tensor
            def _(t):
                for i in range(NT):
                    b = i % 2
                    if i >= 2:
                        t.wait_ge(gpos.sem, 3 * (i - 1))
                    for c in range(ND):
                        mm = t.matmul(p_gate[:, b, :], xnT[:, c, i * P:(i + 1) * P],
                                      gw[:, c, :], start=(c == 0), stop=(c == ND - 1))
                    gmm.inc(mm)

            @blk.vector
            def _(v):
                for i in range(NT + 1):
                    if i < NT:
                        b = i % 2
                        v.wait_ge(gmm.sem, i + 1)
                        if i >= 2:
                            v.wait_ge(gexp.sem, i - 1)
                        gpre.inc(v.tensor_reduce(smax[:, b, 0:1], p_gate[:, b, :],
                                                 axis=AXX, op=ALU.max))
                        v.wait_ge(gpre.sem, gpre.n)
                        gpre.inc(v.tensor_scalar_mul(smax[:, b, 1:2], smax[:, b, 0:1], -1.0))
                    j = i - 1
                    if j >= 0:
                        b = j % 2
                        v.wait_ge(gexp.sem, j + 1)
                        gpos.inc(v.tensor_reduce(sstat[:, b, 0:1], sexp[:, b, :],
                                                 axis=AXX, op=ALU.add))
                        v.wait_ge(gpos.sem, gpos.n)
                        gpos.inc(v.reciprocal(sstat[:, b, 1:2], sstat[:, b, 0:1]))
                        v.wait_ge(gpos.sem, gpos.n)
                        gpos.inc(v.tensor_scalar_mul(
                            probs[:, :, j:j + 1], sexp[:, b, :].unsqueeze(2),
                            sstat[:, b, 1:2]))

            @blk.scalar
            def _(s):
                for i in range(NT):
                    b = i % 2
                    s.wait_ge(gpre.sem, 2 * (i + 1))
                    if i >= 2:
                        s.wait_ge(gpos.sem, 3 * (i - 1))
                    gexp.inc(s.activation(sexp[:, b, :], p_gate[:, b, :],
                                          AF.Exp, bias=smax[:, b, 1:2]))

        with nc.Block() as blk:
            @blk.vector
            def _(v):
                bis.inc(v.memset(theta[:], 0.5))
                bis.inc(v.memset(tlo[:], 0.0))
                v.wait_ge(bis.sem, bis.n)
                for k in range(BISECT_ITERS):
                    step = 2.0 ** (-(k + 2))
                    if k > 0:
                        v.wait_ge(bmm.sem, k)
                        v.wait_ge(bis.sem, bis.n)
                    for e in range(E):
                        bis.inc(v.tensor_scalar(maskb[:, e, :], probs[:, e, :],
                                                theta[:, e:e + 1], None, op0=ALU.is_ge))
                    v.wait_ge(bmm.sem, k + 1)
                    bis.inc(v.tensor_reduce(cnt[:, :].unsqueeze(2), p_cnt[:].rearrange(
                        "p (e c) -> p e c", e=E), axis=AXX, op=ALU.add))
                    v.wait_ge(bis.sem, bis.n)
                    bis.inc(v.tensor_scalar(d2[:], cnt[:], float(SEL) - 0.5, None,
                                            op0=ALU.is_ge))
                    v.wait_ge(bis.sem, bis.n)
                    bis.inc(v.tensor_sub(tmp8[:], theta[:], tlo[:]))
                    v.wait_ge(bis.sem, bis.n)
                    bis.inc(v.tensor_mul(tmp8[:], tmp8[:], d2[:]))
                    v.wait_ge(bis.sem, bis.n)
                    bis.inc(v.tensor_add(tlo[:], tlo[:], tmp8[:]))
                    v.wait_ge(bis.sem, bis.n)
                    bis.inc(v.tensor_scalar(tmp8[:], d2[:], 2.0 * step, -step,
                                            op0=ALU.mult, op1=ALU.add))
                    v.wait_ge(bis.sem, bis.n)
                    bis.inc(v.tensor_add(theta[:], theta[:], tmp8[:]))
                assert bis.n == 2 + 15 * BISECT_ITERS

            @blk.tensor
            def _(t):
                for k in range(BISECT_ITERS):
                    t.wait_ge(bis.sem, 2 + 15 * k + 8)
                    bmm.inc(t.matmul(p_cnt[:], c_ones[:], maskb[:].rearrange(
                        "p e c -> p (e c)"), start=True, stop=True))

        with nc.Block() as blk:
            DVE_PER_E = 12 + NT + 2 + 16
            PE_PER_E = 3 + 2 * NT + 8

            @blk.vector
            def _(v):
                for e in range(E):
                    base = DVE_PER_E * e
                    pbase = PE_PER_E * e
                    if e > 0:
                        v.wait_ge(rmm.sem, PE_PER_E * (e - 1) + 2)
                    rnk.inc(v.tensor_scalar(maskb[:, 0, :], probs[:, e, :],
                                            tlo[:, e:e + 1], None, op0=ALU.is_ge))
                    v.wait_ge(rmm.sem, pbase + 1)
                    rnk.inc(v.memset(toff[:, 0:64], 0.0))
                    rnk.inc(v.memset(toff[:, 64:128], 0.0))
                    v.wait_ge(rnk.sem, rnk.n)
                    rnk.inc(v.tensor_copy(toff[0:1, 33:64], p_tot[0:1, 0:31]))
                    v.wait_ge(rnk.sem, rnk.n)
                    src_, dst_ = 0, 64
                    for sh in (1, 2, 4, 8, 16):
                        rnk.inc(v.tensor_add(toff[0:1, dst_ + 32:dst_ + 64],
                                             toff[0:1, src_ + 32:src_ + 64],
                                             toff[0:1, src_ + 32 - sh:src_ + 64 - sh]))
                        v.wait_ge(rnk.sem, rnk.n)
                        src_, dst_ = dst_, src_
                    v.wait_ge(rmm.sem, pbase + 3)
                    rnk.inc(v.tensor_mul(rankp[:], p_pref[:], maskb[:, 0, :]))
                    rnk.inc(v.tensor_copy(mgr_all[:, :, 0:1], c_tok[:, :].unsqueeze(2)))
                    rnk.inc(v.tensor_copy(mgr_all[:, :, 1:2], probs[:, e, :].unsqueeze(2)))
                    assert rnk.n == base + 12
                    for c in range(NT):
                        v.wait_ge(rnk.sem, base + 12)
                        if c >= 2:
                            v.wait_ge(rmm.sem, pbase + 3 + 2 * (c - 1))
                        rnk.inc(v.tensor_scalar(mtile[:, c % 2, :], c_slots[:, :],
                                                rankp[:, c:c + 1], None, op0=ALU.is_equal))
                    v.wait_ge(rmm.sem, pbase + 3 + 2 * NT)
                    if e > 0:
                        v.wait_ge(rmm.sem, PE_PER_E * e)
                    rnk.inc(v.tensor_copy(mg_row[:, 0:512], p_mgN[:, 0, :]))
                    rnk.inc(v.tensor_copy(mg_row[:, 512:1024], p_mgN[:, 1, :]))
                    assert rnk.n == base + 12 + NT + 2
                    for sc in range(E):
                        v.wait_ge(rmm.sem, pbase + 3 + 2 * NT + sc + 1)
                        rnk.inc(v.tensor_copy(m_slab[:, e * 8 + sc:e * 8 + sc + 1],
                                              p_tr[:, sc % 2, 0:1]))
                        rnk.inc(v.tensor_copy(g_slab[:, e * 8 + sc:e * 8 + sc + 1],
                                              p_tr[:, sc % 2, 1:2]))
                    assert rnk.n == DVE_PER_E * (e + 1)

            @blk.tensor
            def _(t):
                for e in range(E):
                    base = DVE_PER_E * e
                    pbase = PE_PER_E * e
                    t.wait_ge(rnk.sem, base + 1)
                    if e > 0:
                        t.wait_ge(rnk.sem, base - DVE_PER_E + 4)
                    rmm.inc(t.matmul(p_tot[:, 0:NT], c_ones[:, 0:1], maskb[:, 0, :],
                                     start=True, stop=True))
                    rmm.inc(t.matmul(p_pref[:], c_utri[:], maskb[:, 0, :],
                                     start=True, stop=False))
                    t.wait_ge(rnk.sem, base + 9)
                    rmm.inc(t.matmul(p_pref[:], c_ones[0:1, :], toff[0:1, 64 + 32:64 + 64],
                                     start=False, stop=True))
                    if e > 0:
                        t.wait_ge(rnk.sem, base - DVE_PER_E + 12 + NT + 2)
                    for c in range(NT):
                        t.wait_ge(rnk.sem, base + 12 + c + 1)
                        for ch in range(2):
                            rmm.inc(t.matmul(p_mgN[:, ch, :], mgr_all[:, c, :],
                                             mtile[:, c % 2, ch * 512:(ch + 1) * 512],
                                             start=(c == 0), stop=(c == NT - 1)))
                    t.wait_ge(rnk.sem, base + 12 + NT + 2)
                    for sc in range(E):
                        if sc >= 2:
                            t.wait_ge(rnk.sem, base + 12 + NT + 2 + 2 * (sc - 1))
                        rmm.inc(t.matmul(p_tr[:, sc % 2, 0:2],
                                         mg_row[0:2, sc * P:(sc + 1) * P],
                                         c_ident[0:2, 0:2],
                                         start=True, stop=True))
                    assert rmm.n == PE_PER_E * (e + 1)

        pr.release(ctrs=["sl_gmm", "sl_pre", "sl_exp", "sl_pos", "sl_bis",
                         "sl_bmm", "sl_rnk", "sl_rmm", "sl_ld"])
        if dbg is not None:
            cpd = pr.ctr("sl_dbg")
            with nc.Block() as blk:
                @blk.sync
                def _(sync):
                    cpd.inc(sync.dma_start(dbg["probs"].ap(), probs[:, :, :]), 16)
                    cpd.inc(sync.dma_start(dbg["tlo"].ap(), tlo[:]), 16)
                    cpd.inc(sync.dma_start(dbg["m"].ap(), m_slab[:, :]), 16)
                    cpd.inc(sync.dma_start(dbg["g"].ap(), g_slab[:, :]), 16)
                    sync.wait_ge(cpd.sem, cpd.n)



def emit_attn0(pr, xnT, W, x_dram, x1_dram, outT_hbm, ctx_weights):
    nc = pr.nc
    QB = 512
    NQB = T // QB
    NIT = 2 * NQB

    wld = pr.ctr("a0_wld")
    pmm = pr.ctr("a0_pmm")
    pev = pr.ctr("a0_pev")
    smm = pr.ctr("a0_smm")
    avm = pr.ctr("a0_avm")
    bcm = pr.ctr("a0_bcm")
    xdv = pr.ctr("a0_xdv")
    exc = pr.ctr("a0_exc")
    sot = DmaSlots(nc, "a0_sot", 2)

    En = ctx_weights.enter_context
    wq = En(nc.sbuf_tensor("a0_wq", [P, ND, D], BF16))
    wk = En(nc.sbuf_tensor("a0_wk", [P, ND, D], BF16))
    wv = En(nc.sbuf_tensor("a0_wv", [P, ND, D], BF16))
    wo = En(nc.sbuf_tensor("a0_wo", [P, ND, D], BF16))
    brow = En(nc.sbuf_tensor("a0_brow", [1, 4 * D], BF16))
    ones = En(nc.sbuf_tensor("a0_ones", [1, QB], BF16))

    with nc.Block() as blk:
        @blk.gpsimd
        def _(g):
            for i, (wsb, wd) in enumerate([(wq, W["wq"]), (wk, W["wk"]),
                                           (wv, W["wv"]), (wo, W["wo"])]):
                wld.inc(g.dma_start(wsb[:, :, :], wd.ap().rearrange(
                    "(c p) e -> p c e", p=P)), 16)
            for i, bd in enumerate([W["bq"], W["bk"], W["bv"], W["bo"]]):
                wld.inc(g.dma_start(brow[0:1, i * D:(i + 1) * D], bd.ap()), 16)
            g.wait_ge(wld.sem, wld.n)

        @blk.vector
        def _(v):
            v.memset(ones[:], 1.0)

    bq = brow[0:1, 0 * D:1 * D]
    bk = brow[0:1, 1 * D:2 * D]
    bv = brow[0:1, 2 * D:3 * D]
    bo = brow[0:1, 3 * D:4 * D]

    with ExitStack() as ctx:
        En2 = ctx.enter_context
        qt = En2(nc.sbuf_tensor("a0_qt", [P, T], BF16))
        kt = En2(nc.sbuf_tensor("a0_kt", [P, T], BF16))
        vs = En2(nc.sbuf_tensor("a0_vs", [P, NT, 136], BF16))
        at = En2(nc.sbuf_tensor("a0_at", [P, NT, QB], BF16))
        rlf = En2(nc.sbuf_tensor("a0_rlf", [1, QB], F32))
        rlb = En2(nc.sbuf_tensor("a0_rlb", [1, QB], BF16))
        bcs = En2(nc.sbuf_tensor("a0_bcs", [64, QB], F32))
        ots = [En2(nc.sbuf_tensor(f"a0_ot{i}", [64, QB], BF16)) for i in range(2)]
        psS = [En2(nc.psum_tensor(f"a0_psS{i}", [P, 2 * QB], F32)) for i in range(2)]
        pAV = [En2(nc.psum_tensor(f"a0_pAV{i}", [66, QB], F32)) for i in range(2)]
        pM = En2(nc.psum_tensor("a0_pM", [P, QB], F32))
        pBC = En2(nc.psum_tensor("a0_pBC", [64, QB], F32))

        with nc.Block() as blk:
            @blk.vector
            def _(v):
                ms = pr.ctr("a0_ms")
                ms.inc(v.memset(vs[:, :, 64:68], 0.0))
                ms.inc(v.memset(vs[:, :, 132:136], 0.0))
                v.wait_ge(ms.sem, ms.n)
                v.memset(vs[:, :, 64:65], 1.0)
                v.memset(vs[:, :, 132:133], 1.0)

        with nc.Block() as blk:
            @blk.tensor
            def _(t):
                for pair in range(E):
                    pb_mm = 48 * pair
                    pb_ev = 80 * pair
                    col = slice(pair * P, (pair + 1) * P)
                    for wsb, bias in ((wq, bq), (wk, bk)):
                        qk = 0 if wsb is wq else 1
                        for ts in range(8):
                            ps = psS[ts % 2]
                            if pair > 0 and qk == 0 and ts < 2:
                                t.wait_ge(exc.sem, 16 * NIT * pair)
                            if pb_mm + 8 * qk + ts >= 2:
                                t.wait_ge(pev.sem, pb_ev + 8 * qk + ts - 1)
                            sl = slice(ts * QB, (ts + 1) * QB)
                            for c in range(ND):
                                t.matmul(ps[:, 0:QB], wsb[:, c, col], xnT[:, c, sl],
                                         start=(c == 0), stop=False)
                            pmm.inc(t.matmul(ps[:, 0:QB], bias[:, col], ones[:],
                                             start=False, stop=True))
                    for tt in range(NT):
                        if 16 + tt >= 1:
                            t.wait_ge(pev.sem, pb_ev + 16 + 2 * tt)
                        for c in range(ND):
                            t.matmul(pM[:, 0:P], xnT[:, c, tt * P:(tt + 1) * P],
                                     wv[:, c, col], start=(c == 0), stop=False)
                        pmm.inc(t.matmul(pM[:, 0:P], ones[0:1, 0:P], bv[:, col],
                                         start=False, stop=True))
                    t.wait_ge(pev.sem, pb_ev + 16)
                    for hh in range(2):
                        hsl = slice(hh * 64, hh * 64 + 64)
                        for qb in range(NQB):
                            it = pair * NIT + hh * NQB + qb
                            qsl = slice(qb * QB, (qb + 1) * QB)
                            for g in range(16):
                                if it * 16 + g >= 2:
                                    t.wait_ge(exc.sem, it * 16 + g - 1)
                                ps = psS[g % 2]
                                for k2 in range(2):
                                    kt_i = 2 * g + k2
                                    smm.inc(t.matmul(
                                        ps[:, k2 * QB:(k2 + 1) * QB],
                                        kt[hsl, kt_i * P:(kt_i + 1) * P],
                                        qt[hsl, qsl], start=True, stop=True))
                            if hh == 0 and qb == 0:
                                t.wait_ge(pev.sem, 80 * (pair + 1))
                            pav = pAV[it % 2]
                            for kt_i in range(NT):
                                if kt_i == 0 and it >= 2:
                                    t.wait_ge(xdv.sem, 4 * it - 4)
                                if kt_i % 2 == 0:
                                    t.wait_ge(exc.sem, it * 16 + kt_i // 2 + 1)
                                avm.inc(t.matmul(pav[:, :],
                                                 vs[:, kt_i, hh * 68:hh * 68 + 66],
                                                 at[:, kt_i, :],
                                                 start=(kt_i == 0), stop=(kt_i == NT - 1)))
                            t.wait_ge(xdv.sem, 4 * it + 2)
                            bcm.inc(t.matmul(pBC[:, :], ones[0:1, 0:64], rlb[:],
                                             start=True, stop=True))

            @blk.vector
            def _(v):
                for pair in range(E):
                    pb_mm = 48 * pair
                    pb_ev = 80 * pair
                    for qk in range(2):
                        dst = qt if qk == 0 else kt
                        for ts in range(8):
                            v.wait_ge(pmm.sem, pb_mm + 8 * qk + ts + 1)
                            pev.inc(v.tensor_copy(dst[:, ts * QB:(ts + 1) * QB],
                                                  psS[ts % 2][:, 0:QB]))
                    for tt in range(NT):
                        v.wait_ge(pmm.sem, pb_mm + 16 + tt + 1)
                        pev.inc(v.tensor_copy(vs[:, tt, 0:64], pM[:, 0:64]))
                        pev.inc(v.tensor_copy(vs[:, tt, 68:132], pM[:, 64:128]))
                    for it_l in range(NIT):
                        it = pair * NIT + it_l
                        pav = pAV[it % 2]
                        v.wait_ge(avm.sem, 32 * (it + 1))
                        xdv.inc(v.reciprocal(rlf[:], pav[64:65, :]))
                        v.wait_ge(xdv.sem, xdv.n)
                        xdv.inc(v.tensor_copy(rlb[:], rlf[:]))
                        v.wait_ge(bcm.sem, it + 1)
                        xdv.inc(v.tensor_copy(bcs[:], pBC[:, :]))
                        v.wait_ge(xdv.sem, xdv.n)
                        if it >= 2:
                            sot.wait(v, it - 2)
                        xdv.inc(v.tensor_mul(ots[it % 2][:, :], pav[0:64, :], bcs[:]))

            @blk.scalar
            def _(s):
                for pair in range(E):
                    for it_l in range(NIT):
                        it = pair * NIT + it_l
                        for g in range(16):
                            s.wait_ge(smm.sem, it * 32 + 2 * (g + 1))
                            if it >= 1:
                                s.wait_ge(avm.sem, 32 * (it - 1) + 2 * g + 2)
                            exc.inc(s.activation(
                                at[:, 2 * g:2 * g + 2, :], psS[g % 2][:, :],
                                AF.Exp, scale=0.125))

            @blk.sync
            def _(sync):
                for pair in range(E):
                    for hh in range(2):
                        head = 2 * pair + hh
                        for qb in range(NQB):
                            it = pair * NIT + hh * NQB + qb
                            sync.wait_ge(xdv.sem, 4 * (it + 1))
                            sot.issue(it, sync.dma_start(
                                outT_hbm.ap()[head * 64:(head + 1) * 64,
                                              qb * QB:(qb + 1) * QB],
                                ots[it % 2][:, :]))
                sot.wait_all(sync)

    old = DmaSlots(nc, "a0_old", 2)
    xld = DmaSlots(nc, "a0_xld", 3)
    omm = pr.ctr("a0_omm")
    oev = pr.ctr("a0_oev")
    ost = DmaSlots(nc, "a0_ost", 3)
    outT_v = outT_hbm.ap().rearrange("(c p) t -> p c t", p=P)

    with ExitStack() as ctx:
        En2 = ctx.enter_context
        otb = [En2(nc.sbuf_tensor(f"a0_otb{i}", [P, ND, QB], BF16)) for i in range(2)]
        xtb = [En2(nc.sbuf_tensor(f"a0_xtb{i}", [P, D], F32)) for i in range(3)]
        x1b = [En2(nc.sbuf_tensor(f"a0_x1b{i}", [P, D], F32)) for i in range(3)]
        pO = [En2(nc.psum_tensor(f"a0_pO{i}", [P, D], F32)) for i in range(2)]

        with nc.Block() as blk:
            @blk.sync
            def _(sync):
                for i in range(NT):
                    tg = i // 4
                    if i % 4 == 0:
                        if tg >= 2:
                            sync.wait_ge(omm.sem, 4 * (tg - 1))
                        old.issue(tg, sync.dma_start(otb[tg % 2][:, :, :],
                                                     outT_v[:, :, tg * QB:(tg + 1) * QB]))
                    if i >= 3:
                        sync.wait_ge(oev.sem, i - 2)
                    xld.issue(i, sync.dma_start(xtb[i % 3][:], x_dram[i * P:(i + 1) * P, :]))

            @blk.tensor
            def _(t):
                for i in range(NT):
                    tg, tl = i // 4, i % 4
                    old.wait(t, tg)
                    if i >= 2:
                        t.wait_ge(oev.sem, i - 1)
                    ps = pO[i % 2]
                    for dh in range(2):
                        dsl = slice(dh * QB, (dh + 1) * QB)
                        for c in range(ND):
                            t.matmul(ps[:, dsl], otb[tg % 2][:, c, tl * P:(tl + 1) * P],
                                     wo[:, c, dsl], start=(c == 0), stop=False)
                        last = t.matmul(ps[:, dsl], ones[0:1, 0:P], bo[:, dsl],
                                        start=False, stop=True)
                    omm.inc(last)

            @blk.vector
            def _(v):
                for i in range(NT):
                    v.wait_ge(omm.sem, i + 1)
                    xld.wait(v, i)
                    if i >= 3:
                        ost.wait(v, i - 3)
                    oev.inc(v.tensor_add(x1b[i % 3][:], xtb[i % 3][:], pO[i % 2][:]))

            @blk.gpsimd
            def _(g):
                for i in range(NT):
                    g.wait_ge(oev.sem, i + 1)
                    ost.issue(i, g.dma_start(x1_dram[i * P:(i + 1) * P, :], x1b[i % 3][:]))
                ost.wait_all(g)

    pr.release(ctrs=["a0_wld", "a0_pmm", "a0_pev", "a0_smm", "a0_avm", "a0_bcm",
                     "a0_xdv", "a0_exc", "a0_ms", "a0_omm", "a0_oev"],
               slots=[sot, old, xld, ost])


def emit_ffn0(pr, xn_tok, xin_tok, x1_dram, w1_dram, w2_dram, m_slab, g_slab):
    nc = pr.nc
    FH = FF // 2
    upm = pr.ctr("f0_upm")
    gel = pr.ctr("f0_gel")
    dnm = pr.ctr("f0_dnm")
    dnev = pr.ctr("f0_dnev")
    tr = pr.ctr("f0_tr")
    gth = DmaSlots(nc, "f0_gth", 2)
    wrt = DmaSlots(nc, "f0_wrt", 2)
    w1l = DmaSlots(nc, "f0_w1l", 2)
    w2l = DmaSlots(nc, "f0_w2l", 2)
    scat = DmaSlots(nc, "f0_scat", 2)

    with ExitStack() as ctx:
        En = ctx.enter_context
        xg = [En(nc.sbuf_tensor(f"f0_xg{i}", [P, D], BF16)) for i in range(2)]
        xinT = En(nc.sbuf_tensor("f0_xinT", [P, ND, SEL], BF16))
        w1h = [En(nc.sbuf_tensor(f"f0_w1h{i}", [P, ND, FH], BF16)) for i in range(2)]
        w2h = [En(nc.sbuf_tensor(f"f0_w2h{i}", [P, 16, D], BF16)) for i in range(2)]
        hT = En(nc.sbuf_tensor("f0_hT", [P, 16, SEL], BF16))
        stage = [En(nc.sbuf_tensor(f"f0_st{i}", [P, D], F32)) for i in range(2)]
        pU = [En(nc.psum_tensor(f"f0_pU{i}", [P, D], F32)) for i in range(2)]
        pD = [En(nc.psum_tensor(f"f0_pD{i}", [P, D], F32)) for i in range(2)]

        with nc.Block() as blk:
            @blk.gpsimd
            def _(g):
                for sc in range(8):
                    if sc >= 2:
                        wrt.wait(g, sc - 2)
                    gth.issue(sc, g.indirect_dma_start(
                        xg[sc % 2][:], None, xn_tok,
                        IndirectOffsetOnAxis(m_slab[:, sc:sc + 1], 0)))
                for e in range(E):
                    for fh in range(2):
                        fi = 2 * e + fh
                        if fi >= 2:
                            g.wait_ge(upm.sem, 32 * (fi - 1))
                        w1l.issue(fi, g.dma_start(
                            w1h[fi % 2][:, :, :],
                            w1_dram.ap()[e].rearrange("(c p) f -> p c f", p=P)
                            [:, :, fh * FH:(fh + 1) * FH]))
                        if fi >= 2:
                            g.wait_ge(dnm.sem, 8 * (fi - 1))
                        w2l.issue(fi, g.dma_start(
                            w2h[fi % 2][:, :, :],
                            w2_dram.ap()[e].rearrange("(c p) d -> p c d", p=P)
                            [:, fh * 16:(fh + 1) * 16, :]))
                    if e + 1 < E:
                        for sc in range(8):
                            gi = 8 * (e + 1) + sc
                            wrt.wait(g, gi - 2)
                            gth.issue(gi, g.indirect_dma_start(
                                xg[gi % 2][:], None, xn_tok,
                                IndirectOffsetOnAxis(
                                    m_slab[:, (e + 1) * 8 + sc:(e + 1) * 8 + sc + 1], 0)))
                    for fh in range(2):
                        scat.wait_all(g)
                        for selt in range(8):
                            sidx = 16 * e + 8 * fh + selt
                            g.wait_ge(dnev.sem, 16 * e + 8 * fh + selt + 1)
                            scat.issue(sidx, g.indirect_dma_start(
                                x1_dram, IndirectOffsetOnAxis(
                                    m_slab[:, e * 8 + selt:e * 8 + selt + 1], 0),
                                stage[sidx % 2][:], None, compute_op=ALU.add))
                gth.wait_all(g)
                scat.wait_all(g)

            @blk.sync
            def _(sync):
                for e in range(E):
                    for sc in range(8):
                        gi = 8 * e + sc
                        gth.wait(sync, gi)
                        if sc == 0:
                            sync.wait_ge(tr.sem, 16 * 8 * e)
                        wrt.issue(gi, sync.dma_start(
                            xin_tok.ap()[sc * P:(sc + 1) * P, :], xg[gi % 2][:]))
                    wrt.wait_all(sync)
                    for c in range(ND):
                        if c == 0:
                            sync.wait_ge(upm.sem, 64 * e)
                        tr.inc(sync.dma_start(
                            xinT[:, c, :], xin_tok.ap()[:, c * P:(c + 1) * P],
                            transpose=True), 16)

            @blk.tensor
            def _(t):
                for e in range(E):
                    for fh in range(2):
                        fi = 2 * e + fh
                        w1l.wait(t, fi)
                        if fh == 0:
                            t.wait_ge(tr.sem, 16 * 8 * (e + 1))
                        for ft in range(16):
                            gg = 32 * e + 16 * fh + ft
                            if gg >= 2:
                                t.wait_ge(gel.sem, gg - 1)
                            ps = pU[ft % 2]
                            for sb in range(2):
                                ssl = slice(sb * 512, (sb + 1) * 512)
                                for c in range(ND):
                                    mm = t.matmul(ps[:, ssl],
                                                  w1h[fi % 2][:, c, ft * P:(ft + 1) * P],
                                                  xinT[:, c, ssl],
                                                  start=(c == 0), stop=(c == ND - 1))
                                upm.inc(mm)
                        w2l.wait(t, fi)
                        t.wait_ge(gel.sem, 16 * (fi + 1))
                        for selt in range(8):
                            dg = 16 * e + 8 * fh + selt
                            if dg >= 2:
                                t.wait_ge(dnev.sem, dg - 1)
                            ps = pD[selt % 2]
                            for dh in range(2):
                                dsl = slice(dh * 512, (dh + 1) * 512)
                                for fc in range(16):
                                    mm = t.matmul(ps[:, dsl],
                                                  hT[:, fc, selt * P:(selt + 1) * P],
                                                  w2h[fi % 2][:, fc, dsl],
                                                  start=(fc == 0), stop=(fc == 15))
                            dnm.inc(mm)

            @blk.scalar
            def _(s):
                for e in range(E):
                    for fh in range(2):
                        fi = 2 * e + fh
                        for ft in range(16):
                            s.wait_ge(upm.sem, 64 * e + 32 * fh + 2 * (ft + 1))
                            if fi >= 1:
                                s.wait_ge(dnm.sem, 8 * fi)
                            gel.inc(s.activation(hT[:, ft, :], pU[ft % 2][:],
                                                 AF.Gelu))

            @blk.vector
            def _(v):
                for e in range(E):
                    for fh in range(2):
                        for selt in range(8):
                            sidx = 16 * e + 8 * fh + selt
                            v.wait_ge(dnm.sem, sidx + 1)
                            if sidx >= 2:
                                scat.wait(v, sidx - 2)
                            dnev.inc(v.tensor_scalar_mul(
                                stage[sidx % 2][:], pD[selt % 2][:],
                                g_slab[:, e * 8 + selt:e * 8 + selt + 1]))

    pr.release(ctrs=["f0_upm", "f0_gel", "f0_dnm", "f0_dnev", "f0_tr"],
               slots=[gth, wrt, w1l, w2l, scat])

def emit_attn1_proj(pr, xnT, W, qraw, kraw, vtok, combT_hbm, c_ident, rw_dram, rb_dram, comb_mode=3):
    skip_comb = comb_mode == 0
    nc = pr.nc
    wld = pr.ctr("a1_wld")
    prm = pr.ctr("a1_prm")
    prr = pr.ctr("a1_prr")
    pct = pr.ctr("a1_pct")
    dvq = pr.ctr("a1_dvq")
    dvr = pr.ctr("a1_dvr")
    ccp = pr.ctr("a1_ccp")
    stq = DmaSlots(nc, "a1_stq", 2)
    stk = DmaSlots(nc, "a1_stk", 2)
    stv = DmaSlots(nc, "a1_stv", 2)
    pr.t["a1_cstslots"] = DmaSlots(nc, "a1_cst", 2)

    with ExitStack() as ctx:
        En = ctx.enter_context
        wq = En(nc.sbuf_tensor("a1_wq", [P, ND, D], BF16))
        wk = En(nc.sbuf_tensor("a1_wk", [P, ND, D], BF16))
        wv = En(nc.sbuf_tensor("a1_wv", [P, ND, D], BF16))
        rw = En(nc.sbuf_tensor("a1_rw", [P, ND, E], BF16))
        brow = En(nc.sbuf_tensor("a1_brow", [1, 3 * D + E], BF16))
        ones = En(nc.sbuf_tensor("a1_ones", [1, 512], BF16))
        qsb = [En(nc.sbuf_tensor(f"a1_q{i}", [P, D], F32)) for i in range(2)]
        ksb = [En(nc.sbuf_tensor(f"a1_k{i}", [P, D], F32)) for i in range(2)]
        vsb = [En(nc.sbuf_tensor(f"a1_v{i}", [P, D], BF16)) for i in range(2)]
        rp = En(nc.sbuf_tensor("a1_rp", [P, 8], F32))
        rst = En(nc.sbuf_tensor("a1_rst", [P, 8], F32))
        eqt = En(nc.sbuf_tensor("a1_eq", [P, 8], F32))
        pr.t["a1_comb"] = En(nc.sbuf_tensor("a1_cmb", [P, NT, E], F32))
        combT_st = En(nc.sbuf_tensor("a1_cTs", [8, 2, P], BF16))
        pP = [En(nc.psum_tensor(f"a1_pP{i}", [P, D], F32)) for i in range(2)]
        pR = En(nc.psum_tensor("a1_pR", [P, E], F32))
        pC = En(nc.psum_tensor("a1_pC", [8, 2, P], F32))

        bq = brow[0:1, 0:D]
        bk = brow[0:1, D:2 * D]
        bv = brow[0:1, 2 * D:3 * D]
        rb = brow[0:1, 3 * D:3 * D + E]

        with nc.Block() as blk:
            @blk.gpsimd
            def _(g):
                for wsb, wd in ((wq, W["wq"]), (wk, W["wk"]), (wv, W["wv"])):
                    wld.inc(g.dma_start(wsb[:, :, :],
                                        wd.ap().rearrange("(c p) e -> p c e", p=P)), 16)
                wld.inc(g.dma_start(rw[:, :, :],
                                    rw_dram.ap().rearrange("(c p) e -> p c e", p=P)), 16)
                for i, bd in enumerate([W["bq"], W["bk"], W["bv"]]):
                    wld.inc(g.dma_start(brow[0:1, i * D:(i + 1) * D], bd.ap()), 16)
                wld.inc(g.dma_start(brow[0:1, 3 * D:3 * D + E], rb_dram.ap()), 16)
                g.wait_ge(wld.sem, wld.n)

            @blk.vector
            def _(v):
                v.memset(ones[:], 1.0)

        with nc.Block() as blk:
            @blk.tensor
            def _(t):
                for tt in range(NT):
                    tsl = slice(tt * P, (tt + 1) * P)
                    for j, (wsb, bias) in enumerate(((wq, bq), (wk, bk), (wv, bv))):
                        pj = 3 * tt + j
                        if pj >= 2:
                            t.wait_ge(dvq.sem, pj - 1)
                        ps = pP[pj % 2]
                        for dh in range(2):
                            dsl = slice(dh * 512, (dh + 1) * 512)
                            for c in range(ND):
                                t.matmul(ps[:, dsl], xnT[:, c, tsl], wsb[:, c, dsl],
                                         start=(c == 0), stop=False)
                            mm = t.matmul(ps[:, dsl], ones[0:1, 0:P], bias[:, dsl],
                                          start=False, stop=True)
                        prm.inc(mm)
                    if tt >= 1:
                        t.wait_ge(dvr.sem, 13 * (tt - 1) + 3)
                    for c in range(ND):
                        t.matmul(pR[:, :], xnT[:, c, tsl], rw[:, c, :],
                                 start=(c == 0), stop=False)
                    prr.inc(t.matmul(pR[:, :], ones[0:1, 0:P], rb[:],
                                     start=False, stop=True))
                    jt = tt - 1
                    if jt >= 0 and not skip_comb:
                        t.wait_ge(dvr.sem, 13 * (jt + 1))
                        if jt >= 2 and comb_mode >= 2:
                            t.wait_ge(ccp.sem, jt - 1)
                        pct.inc(t.transpose(pC[:, jt % 2, :],
                                            comb_sl(pr, jt, None), c_ident[:]))
                for jt in ([] if skip_comb else (NT - 1,)):
                    t.wait_ge(dvr.sem, 13 * (jt + 1))
                    if jt >= 2 and comb_mode >= 2:
                        t.wait_ge(ccp.sem, jt - 1)
                    pct.inc(t.transpose(pC[:, jt % 2, :],
                                        comb_sl(pr, jt, None), c_ident[:]))

            @blk.vector
            def _(v):
                for tt in range(NT + 1):
                    if tt < NT:
                        b = tt % 2
                        v.wait_ge(prm.sem, 3 * tt + 1)
                        if tt >= 2:
                            stq.wait(v, tt - 2)
                        dvq.inc(v.tensor_copy(qsb[b][:], pP[(3 * tt) % 2][:]))
                        v.wait_ge(prm.sem, 3 * tt + 2)
                        if tt >= 2:
                            stk.wait(v, tt - 2)
                        dvq.inc(v.tensor_copy(ksb[b][:], pP[(3 * tt + 1) % 2][:]))
                        v.wait_ge(prm.sem, 3 * tt + 3)
                        if tt >= 2:
                            stv.wait(v, tt - 2)
                        dvq.inc(v.tensor_copy(vsb[b][:], pP[(3 * tt + 2) % 2][:]))
                        v.wait_ge(prr.sem, tt + 1)
                        dvr.inc(v.tensor_reduce(rst[:, 0:1], pR[:], axis=AXX, op=ALU.max))
                        v.wait_ge(dvr.sem, dvr.n)
                        dvr.inc(v.tensor_scalar_mul(rst[:, 1:2], rst[:, 0:1], -1.0))
                        dvr.inc(v.tensor_copy(rp[:], pR[:]))
                        v.wait_ge(dvr.sem, dvr.n)
                        dvr.inc(v.tensor_scalar(rp[:], rp[:], rst[:, 1:2], None, op0=ALU.add))
                        v.wait_ge(dvr.sem, dvr.n)
                        v.wait_ge(pr.ctr("a1_rex").sem, tt + 1)
                        dvr.inc(v.tensor_reduce(rst[:, 2:3], rp[:], axis=AXX, op=ALU.add))
                        v.wait_ge(dvr.sem, dvr.n)
                        dvr.inc(v.reciprocal(rst[:, 3:4], rst[:, 2:3]))
                        v.wait_ge(dvr.sem, dvr.n)
                        dvr.inc(v.tensor_scalar_mul(rp[:], rp[:], rst[:, 3:4]))
                        v.wait_ge(dvr.sem, dvr.n)
                        dvr.inc(v.tensor_reduce(rst[:, 4:5], rp[:], axis=AXX, op=ALU.max))
                        v.wait_ge(dvr.sem, dvr.n)
                        dvr.inc(v.tensor_scalar(eqt[:], rp[:], rst[:, 4:5], -1e30,
                                                op0=ALU.is_ge, op1=ALU.mult))
                        v.wait_ge(dvr.sem, dvr.n)
                        dvr.inc(v.tensor_add(eqt[:], eqt[:], rp[:]))
                        v.wait_ge(dvr.sem, dvr.n)
                        dvr.inc(v.tensor_reduce(rst[:, 5:6], eqt[:], axis=AXX, op=ALU.max))
                        v.wait_ge(dvr.sem, dvr.n)
                        dvr.inc(v.tensor_scalar(eqt[:], rp[:], rst[:, 5:6], None, op0=ALU.is_ge))
                        v.wait_ge(dvr.sem, dvr.n)
                        dvr.inc(v.tensor_mul(comb_sl(pr, tt, None), rp[:], eqt[:]))
                    jt = tt - 1
                    if jt >= 0 and comb_mode >= 2:
                        v.wait_ge(pct.sem, jt + 1)
                        if jt >= 2 and comb_mode >= 3:
                            pr.t["a1_cstslots"].wait(v, jt - 2)
                        ccp.inc(v.tensor_copy(combT_st[:, jt % 2, :],
                                              pC[:, jt % 2, :]))

            @blk.scalar
            def _(s):
                rex = pr.ctr("a1_rex")
                for tt in range(NT):
                    s.wait_ge(pr.c["a1_dvr"].sem, 13 * tt + 4)
                    rex.inc(s.activation(rp[:], rp[:], AF.Exp))

            @blk.sync
            def _(sync):
                combT_v = combT_hbm.ap().rearrange("o (e t) -> (o e) t", e=E)
                cslots = pr.t["a1_cstslots"]
                for tt in range(NT + 1):
                    if tt < NT:
                        b = tt % 2
                        sync.wait_ge(dvq.sem, 3 * tt + 1)
                        stq.issue(tt, sync.dma_start(qraw[tt * P:(tt + 1) * P, :], qsb[b][:]))
                        sync.wait_ge(dvq.sem, 3 * tt + 2)
                        stk.issue(tt, sync.dma_start(kraw[tt * P:(tt + 1) * P, :], ksb[b][:]))
                        sync.wait_ge(dvq.sem, 3 * tt + 3)
                        stv.issue(tt, sync.dma_start(vtok[tt * P:(tt + 1) * P, :], vsb[b][:]))
                    jt = tt - 1
                    if jt >= 0 and comb_mode >= 3:
                        sync.wait_ge(ccp.sem, jt + 1)
                        cslots.issue(jt, sync.dma_start(
                            combT_v[:, jt * P:(jt + 1) * P],
                            combT_st[:, jt % 2, :]))
                stq.wait_all(sync)
                stk.wait_all(sync)
                stv.wait_all(sync)
                if comb_mode >= 3:
                    cslots.wait_all(sync)

    pr.release(ctrs=["a1_wld", "a1_prm", "a1_prr", "a1_pct", "a1_dvq", "a1_dvr",
                     "a1_ccp", "a1_rex"],
               slots=[stq, stk, stv, pr.t.pop("a1_cstslots")])


def comb_sl(pr, tt, e):
    cslab = pr.t["a1_comb"]
    if e is None:
        return cslab[:, tt, :]
    return cslab[:, tt, e:e + 1]

def emit_attn1_core(pr, knT, qnT_hbm, vtok, wo_dram, bo_dram, ls_dram,
                    x1_dram, y_dram):
    nc = pr.nc
    QB = 512
    NQB = T // QB
    wld = pr.ctr("c1_wld")
    fld = pr.ctr("c1_fld")
    smm = pr.ctr("c1_smm")
    sig = pr.ctr("c1_sig")
    avm = pr.ctr("c1_avm")
    ave = pr.ctr("c1_ave")
    omm = pr.ctr("c1_omm")
    oev = pr.ctr("c1_oev")
    qld = DmaSlots(nc, "c1_qld", 2)
    vld = DmaSlots(nc, "c1_vld", 2)
    xld = DmaSlots(nc, "c1_xld", 3)
    yst = DmaSlots(nc, "c1_yst", 3)

    with ExitStack() as ctx:
        En = ctx.enter_context
        wo = En(nc.sbuf_tensor("c1_wo", [P, ND, D], BF16))
        lsb = En(nc.sbuf_tensor("c1_lsb", [1, 2 * D], BF16))
        lsbc = En(nc.sbuf_tensor("c1_lsbc", [P, D], BF16))
        ones = En(nc.sbuf_tensor("c1_ones", [1, P], BF16))
        qnb = [En(nc.sbuf_tensor(f"c1_qnb{i}", [P, ND, QB], BF16)) for i in range(2)]
        vb = [En(nc.sbuf_tensor(f"c1_vb{i}", [P, D], BF16)) for i in range(2)]
        at = En(nc.sbuf_tensor("c1_at", [P, NT, QB], BF16))
        otq = En(nc.sbuf_tensor("c1_otq", [P, ND, QB], BF16))
        xtb = [En(nc.sbuf_tensor(f"c1_xtb{i}", [P, D], F32)) for i in range(3)]
        ysb = [En(nc.sbuf_tensor(f"c1_ysb{i}", [P, D], F32)) for i in range(3)]
        psS = [En(nc.psum_tensor(f"c1_psS{i}", [P, 2 * QB], F32)) for i in range(2)]
        pA = [En(nc.psum_tensor(f"c1_pA{i}", [P, QB], F32)) for i in range(4)]

        with nc.Block() as blk:
            @blk.gpsimd
            def _(g):
                wld.inc(g.dma_start(wo[:, :, :],
                                    wo_dram.ap().rearrange("(c p) e -> p c e", p=P)), 16)
                wld.inc(g.dma_start(lsb[0:1, 0:D], ls_dram.ap()), 16)
                wld.inc(g.dma_start(lsb[0:1, D:2 * D], bo_dram.ap()), 16)
                g.wait_ge(wld.sem, wld.n)

            @blk.vector
            def _(v):
                v.memset(ones[:], 1.0)

        with nc.Block() as blk:
            @blk.tensor
            def _(t):
                for half in range(2):
                    fld.inc(t.matmul(psS[0][:, half * QB:(half + 1) * QB],
                                     ones[0:1, 0:P],
                                     lsb[0:1, half * QB:(half + 1) * QB],
                                     start=True, stop=True))

            @blk.vector
            def _(v):
                v.wait_ge(fld.sem, 2)
                fld.inc(v.tensor_copy(lsbc[:], psS[0][:, 0:D]))
                v.wait_ge(fld.sem, fld.n)
                for c in range(ND):
                    fld.inc(v.tensor_mul(wo[:, c, :], wo[:, c, :], lsbc[:]))
                fld.inc(v.tensor_mul(lsb[0:1, D:2 * D], lsb[0:1, D:2 * D],
                                     lsb[0:1, 0:D]))
                v.wait_ge(fld.sem, fld.n)

        bols = lsb[0:1, D:2 * D]

        with nc.Block() as blk:
            @blk.sync
            def _(sync):
                for qb in range(NQB):
                    if qb >= 2:
                        sync.wait_ge(smm.sem, 32 * (qb - 1))
                    qld.issue(qb, sync.dma_start(
                        qnb[qb % 2][:, :, :],
                        qnT_hbm.ap()[:, :, qb * QB:(qb + 1) * QB]))
                    for i in range(4):
                        ii = 4 * qb + i
                        if ii >= 3:
                            sync.wait_ge(oev.sem, ii - 2)
                        xld.issue(ii, sync.dma_start(
                            xtb[ii % 3][:],
                            x1_dram[(qb * QB + i * P):(qb * QB + (i + 1) * P), :]))

            @blk.tensor
            def _(t):
                for qb in range(NQB):
                    qld.wait(t, qb)
                    for g_ in range(16):
                        if qb * 16 + g_ >= 2:
                            t.wait_ge(sig.sem, qb * 16 + g_ - 1)
                        if g_ < 2 and qb >= 1:
                            t.wait_ge(oev.sem, 4 * qb)
                        ps = psS[g_ % 2]
                        for k2 in range(2):
                            kt_i = 2 * g_ + k2
                            for c in range(ND):
                                mm = t.matmul(ps[:, k2 * QB:(k2 + 1) * QB],
                                              knT[:, c, kt_i * P:(kt_i + 1) * P],
                                              qnb[qb % 2][:, c, :],
                                              start=(c == 0), stop=(c == ND - 1))
                            smm.inc(mm)
                    for ps_ in range(2):
                        for kc in range(NT):
                            vi = qb * 64 + ps_ * NT + kc
                            vld.wait(t, vi)
                            if kc % 2 == 0:
                                t.wait_ge(sig.sem, qb * 16 + kc // 2 + 1)
                            if kc == 0 and (qb * 2 + ps_) >= 1:
                                t.wait_ge(ave.sem, 4 * (2 * qb + ps_))
                            for et in range(4):
                                mm = t.matmul(pA[et][:, :],
                                              vb[vi % 2][:, (ps_ * 4 + et) * P:
                                                          (ps_ * 4 + et + 1) * P],
                                              at[:, kc, :],
                                              start=(kc == 0), stop=(kc == NT - 1))
                            avm.inc(mm)
                    t.wait_ge(ave.sem, 8 * (qb + 1))
                    for tl in range(4):
                        ii = 4 * qb + tl
                        if ii >= 2:
                            t.wait_ge(oev.sem, ii - 1)
                        ps = psS[tl % 2]
                        for dh in range(2):
                            dsl = slice(dh * QB, (dh + 1) * QB)
                            for c in range(ND):
                                t.matmul(ps[:, dsl], otq[:, c, tl * P:(tl + 1) * P],
                                         wo[:, c, dsl], start=(c == 0), stop=False)
                            mm = t.matmul(ps[:, dsl], ones[0:1, 0:P], bols[:, dsl],
                                          start=False, stop=True)
                        omm.inc(mm)

            @blk.scalar
            def _(s):
                for qb in range(NQB):
                    for g_ in range(16):
                        s.wait_ge(smm.sem, qb * 32 + 2 * (g_ + 1))
                        if qb >= 1:
                            s.wait_ge(avm.sem, 64 * (qb - 1) + 32 + 2 * g_ + 2)
                        sig.inc(s.activation(at[:, 2 * g_:2 * g_ + 2, :],
                                             psS[g_ % 2][:, :], AF.Sigmoid,
                                             scale=1.0 / 32.0,
                                             bias=pr.const(-8.317766166719343)))

            @blk.vector
            def _(v):
                for qb in range(NQB):
                    for ps_ in range(2):
                        v.wait_ge(avm.sem, qb * 64 + 32 * (ps_ + 1))
                        for et in range(4):
                            ave.inc(v.tensor_copy(otq[:, ps_ * 4 + et, :], pA[et][:, :]))
                    for tl in range(4):
                        ii = 4 * qb + tl
                        v.wait_ge(omm.sem, ii + 1)
                        xld.wait(v, ii)
                        if ii >= 3:
                            yst.wait(v, ii - 3)
                        oev.inc(v.tensor_add(ysb[ii % 3][:], xtb[ii % 3][:],
                                             psS[tl % 2][:]))

            @blk.gpsimd
            def _(g):
                for qb in range(NQB):
                    for ps in range(2):
                        for kc in range(NT):
                            vi = qb * 64 + ps * NT + kc
                            if vi >= 2:
                                g.wait_ge(avm.sem, vi - 1)
                            vld.issue(vi, g.dma_start(
                                vb[vi % 2][:], vtok[kc * P:(kc + 1) * P, :]))
                    for tl in range(4):
                        ii = 4 * qb + tl
                        g.wait_ge(oev.sem, ii + 1)
                        yst.issue(ii, g.dma_start(
                            y_dram[(qb * QB + tl * P):(qb * QB + (tl + 1) * P), :],
                            ysb[ii % 3][:]))
                yst.wait_all(g)
                vld.wait_all(g)

    pr.release(ctrs=["c1_wld", "c1_fld", "c1_smm", "c1_sig", "c1_avm", "c1_ave",
                     "c1_omm", "c1_oev"],
               slots=[qld, vld, xld, yst])


def emit_ffn1(pr, xnT_hbm, combT, up_dram, dn_dram, y_dram):
    nc = pr.nc
    TB = 512
    NTB = T // TB
    upm = pr.ctr("f1_upm")
    gel = pr.ctr("f1_gel")
    cbm = pr.ctr("f1_cbm")
    cbb = pr.ctr("f1_cbb")
    cbe = pr.ctr("f1_cbe")
    dnm = pr.ctr("f1_dnm")
    dnev = pr.ctr("f1_dnev")
    wul = DmaSlots(nc, "f1_wul", 1)
    wdl = DmaSlots(nc, "f1_wdl", 1)
    xbl = DmaSlots(nc, "f1_xbl", 2)
    acc = DmaSlots(nc, "f1_acc", 8)
    cbl = DmaSlots(nc, "f1_cbl", 1)

    with ExitStack() as ctx:
        En = ctx.enter_context
        up = En(nc.sbuf_tensor("f1_up", [P, ND, FF], BF16))
        dn = En(nc.sbuf_tensor("f1_dn", [P, NF, D], BF16))
        hq = En(nc.sbuf_tensor("f1_hq", [P, NF, TB], BF16))
        xb = [En(nc.sbuf_tensor(f"f1_xb{i}", [P, ND, TB], BF16)) for i in range(2)]
        cb = En(nc.sbuf_tensor("f1_cb", [P, TB], BF16))
        cbrow = En(nc.sbuf_tensor("f1_cbrow", [1, T], BF16))
        stg = [En(nc.sbuf_tensor(f"f1_stg{i}", [P, TB], F32)) for i in range(8)]
        ones = En(nc.sbuf_tensor("f1_ones", [1, P], BF16))
        pU = [En(nc.psum_tensor(f"f1_pU{i}", [P, TB], F32)) for i in range(2)]
        pD = [En(nc.psum_tensor(f"f1_pD{i}", [P, TB], F32)) for i in range(2)]
        pC = En(nc.psum_tensor("f1_pC", [P, TB], F32))

        with nc.Block() as blk:
            @blk.vector
            def _(v):
                v.memset(ones[:], 1.0)

        with nc.Block() as blk:
            @blk.gpsimd
            def _(g):
                for e in range(E):
                    if e > 0:
                        g.wait_ge(cbb.sem, NTB * e)
                    cbl.issue(e, g.dma_start(cbrow[0:1, :],
                                             combT[0:1, e * T:(e + 1) * T]))
                    if e > 0:
                        g.wait_ge(upm.sem, 32 * NTB * e)
                    wul.issue(e, g.dma_start(
                        up[:, :, :], up_dram.ap()[e].rearrange("(c p) f -> p c f", p=P)))
                    if e > 0:
                        g.wait_ge(dnm.sem, 8 * NTB * e)
                    wdl.issue(e, g.dma_start(
                        dn[:, :, :], dn_dram.ap()[e].rearrange("(c p) d -> p c d", p=P)))
                    def acc_batch(g, tj):
                        tbj = tj % NTB
                        for tl in range(4):
                            for dh in range(2):
                                si = tj * 8 + tl * 2 + dh
                                g.wait_ge(dnev.sem, si + 1)
                                acc.issue(si, g.dma_start(
                                    y_dram[(tbj * TB + tl * P):(tbj * TB + (tl + 1) * P),
                                           dh * TB:(dh + 1) * TB],
                                    stg[si % 8][:], accum_op=ALU.add))
                    for tb in range(NTB):
                        ti = e * NTB + tb
                        if ti >= 2:
                            g.wait_ge(upm.sem, 32 * (ti - 1))
                        xbl.issue(ti, g.dma_start(
                            xb[ti % 2][:, :, :],
                            xnT_hbm.ap()[:, :, tb * TB:(tb + 1) * TB]))
                        if ti >= 1:
                            acc_batch(g, ti - 1)
                    if e == E - 1:
                        acc_batch(g, E * NTB - 1)
                acc.wait_all(g)

            @blk.tensor
            def _(t):
                for e in range(E):
                    wul.wait(t, e)
                    for tb in range(NTB):
                        ti = e * NTB + tb
                        xbl.wait(t, ti)
                        if ti >= 1:
                            t.wait_ge(cbe.sem, ti)
                        if tb == 0:
                            cbl.wait(t, e)
                        cbb.inc(t.matmul(pC[:, :], ones[0:1, 0:P],
                                         cbrow[0:1, tb * TB:(tb + 1) * TB],
                                         start=True, stop=True))
                        for ft in range(NF):
                            gg = ti * 32 + ft
                            if gg >= 2:
                                t.wait_ge(gel.sem, gg - 1)
                            ps = pU[ft % 2]
                            for c in range(ND):
                                mm = t.matmul(ps[:, :], up[:, c, ft * P:(ft + 1) * P],
                                              xb[ti % 2][:, c, :],
                                              start=(c == 0), stop=(c == ND - 1))
                            upm.inc(mm)
                        if tb == 0:
                            wdl.wait(t, e)
                        t.wait_ge(cbm.sem, 32 * (ti + 1))
                        for tl in range(4):
                            for dh in range(2):
                                dg = ti * 8 + tl * 2 + dh
                                if dg >= 2:
                                    t.wait_ge(dnev.sem, dg - 1)
                                ps = pD[dg % 2]
                                for fc in range(NF):
                                    mm = t.matmul(ps[:, :], hq[:, fc, tl * P:(tl + 1) * P],
                                                  dn[:, fc, dh * TB:(dh + 1) * TB],
                                                  start=(fc == 0), stop=(fc == NF - 1))
                                dnm.inc(mm)

            @blk.scalar
            def _(s):
                for e in range(E):
                    for tb in range(NTB):
                        ti = e * NTB + tb
                        for ft in range(NF):
                            s.wait_ge(upm.sem, ti * 32 + ft + 1)
                            if ti >= 1:
                                s.wait_ge(dnm.sem, 8 * ti)
                            gel.inc(s.activation(hq[:, ft, :], pU[ft % 2][:], AF.Gelu))

            @blk.vector
            def _(v):
                for e in range(E):
                    for tb in range(NTB):
                        ti = e * NTB + tb
                        v.wait_ge(cbb.sem, ti + 1)
                        if ti >= 1:
                            v.wait_ge(cbm.sem, 32 * ti)
                        cbe.inc(v.tensor_copy(cb[:], pC[:, :]))
                        v.wait_ge(cbe.sem, cbe.n)
                        for ft in range(NF):
                            v.wait_ge(gel.sem, ti * 32 + ft + 1)
                            cbm.inc(v.tensor_mul(hq[:, ft, :], hq[:, ft, :], cb[:]))
                        for tl in range(4):
                            for dh in range(2):
                                dg = ti * 8 + tl * 2 + dh
                                v.wait_ge(dnm.sem, dg + 1)
                                if dg >= 8:
                                    acc.wait(v, dg - 8)
                                dnev.inc(v.tensor_copy(stg[dg % 8][:], pD[dg % 2][:]))

    pr.release(ctrs=["f1_upm", "f1_gel", "f1_cbm", "f1_cbb", "f1_cbe", "f1_dnm",
                     "f1_dnev"],
               slots=[wul, wdl, xbl, acc, cbl])
def make_consts(nc):
    ones128 = nc.inline_tensor(np.ones((P, P), np.float32), "c_ones128")
    utri = nc.inline_tensor(np.triu(np.ones((P, P), np.float32)), "c_utri")
    slots = nc.inline_tensor(
        np.tile(np.arange(1, SEL + 1, dtype=np.float32), (P, 1)), "c_slots")
    tok = nc.inline_tensor(
        (np.arange(NT, dtype=np.float32)[None, :] * P
         + np.arange(P, dtype=np.float32)[:, None]), "c_tok")
    ident = nc.inline_tensor(np.eye(P, dtype=np.float32), "c_ident")
    return {"ones": ones128, "utri": utri, "slots": slots, "tok": tok,
            "ident": ident}


def emit_load_consts(pr, cdram, csb):
    nc = pr.nc
    ld = pr.ctr("cst_ld")
    with nc.Block() as blk:
        @blk.sync
        def _(sync):
            for k in cdram:
                ld.inc(sync.dma_start(csb[k][:], cdram[k].ap()), 16)
            sync.wait_ge(ld.sem, ld.n)


def build_program(debug_stop=None):
    nc = bass.Bass("TRN2", target_bir_lowering=False)
    pr = Prog(nc)
    l1dbg = debug_stop in ("ln1", "attn1p", "attn1q", "attn1t", "attn1u", "lnq1", "attn1c")
    full = debug_stop is None or l1dbg

    x = nc.dram_tensor("x", [T, D], F32, kind="ExternalInput")
    gwT = nc.dram_tensor("gwT", [D, E], F32, kind="ExternalInput")
    W0 = {}
    for nm in ("wq", "wk", "wv", "wo"):
        W0[nm] = nc.dram_tensor(nm + "0", [D, D], F32, kind="ExternalInput")
    for nm in ("bq", "bk", "bv", "bo"):
        W0[nm] = nc.dram_tensor(nm + "0", [1, D], F32, kind="ExternalInput")
    w1 = nc.dram_tensor("w1", [E, D, FF], F32, kind="ExternalInput")
    w2 = nc.dram_tensor("w2", [E, FF, D], F32, kind="ExternalInput")
    W1 = {}
    if full:
        for nm in ("wq", "wk", "wv", "wo"):
            W1[nm] = nc.dram_tensor(nm + "1", [D, D], F32, kind="ExternalInput")
        for nm in ("bq", "bk", "bv", "bo"):
            W1[nm] = nc.dram_tensor(nm + "1", [1, D], F32, kind="ExternalInput")
        ls = nc.dram_tensor("ls", [1, D], F32, kind="ExternalInput")
        rw = nc.dram_tensor("rw", [D, E], F32, kind="ExternalInput")
        rb = nc.dram_tensor("rb", [1, E], F32, kind="ExternalInput")
        upw = nc.dram_tensor("upw", [E, D, FF], F32, kind="ExternalInput")
        dnw = nc.dram_tensor("dnw", [E, FF, D], F32, kind="ExternalInput")
    y = nc.dram_tensor("y", [T, D], F32, kind="ExternalOutput")

    xn_tok = nc.dram_tensor("xn_tok", [T, D], BF16, kind="Internal")
    x1 = nc.dram_tensor("x1", [T, D], F32, kind="Internal")
    outT_hbm = nc.dram_tensor("outT", [D, T], BF16, kind="Internal")
    xin_tok = nc.dram_tensor("xin_tok", [SEL, D], BF16, kind="Internal")
    if full:
        qraw = nc.dram_tensor("qraw", [T, D], F32, kind="Internal")
        kraw = nc.dram_tensor("kraw", [T, D], F32, kind="Internal")
        vtok1 = nc.dram_tensor("vtok1", [T, D], BF16, kind="Internal")
        qn_tok = nc.dram_tensor("qn_tok", [T, D], BF16, kind="Internal")
        kn_tok = nc.dram_tensor("kn_tok", [T, D], BF16, kind="Internal")
        xnT_hbm = nc.dram_tensor("xnT_hbm", [P, ND, T], BF16, kind="Internal")
        qnT_hbm = nc.dram_tensor("qnT_hbm", [P, ND, T], BF16, kind="Internal")
        combT_hbm = nc.dram_tensor("combT_hbm", [1, E * T], BF16, kind="Internal")

    cdram = make_consts(nc)

    with ExitStack() as perm:
        En = perm.enter_context
        pr.init_consts(perm, [LN_EPS, -float(np.log(T))])
        csb_ident = En(nc.sbuf_tensor("cs_ident", [P, P], F32))
        l0s = ExitStack()
        En0 = l0s.enter_context
        csb = {
            "ones": En0(nc.sbuf_tensor("cs_ones", [P, P], F32)),
            "utri": En0(nc.sbuf_tensor("cs_utri", [P, P], F32)),
            "slots": En0(nc.sbuf_tensor("cs_slots", [P, SEL], F32)),
            "tok": En0(nc.sbuf_tensor("cs_tok", [P, NT], F32)),
            "ident": csb_ident,
        }
        m_slab = En0(nc.sbuf_tensor("m_slab", [P, E * 8], I32))
        g_slab = En0(nc.sbuf_tensor("g_slab", [P, E * 8], F32))
        probs = En0(nc.sbuf_tensor("probs", [P, E, NT], F32))

        emit_load_consts(pr, cdram, csb)

        emit_ln(pr, x.ap(), xn_tok.ap(), "0")
        if debug_stop == "ln":
            xn_dbg = nc.dram_tensor("xn_dbg", [T, D], BF16, kind="ExternalOutput")
            xnT_dbg = nc.dram_tensor("xnT_dbg", [P, ND, T], BF16, kind="ExternalOutput")
            with ExitStack() as ctx:
                xnT = ctx.enter_context(nc.sbuf_tensor("xnT", [P, ND, T], BF16))
                emit_xnT(pr, xn_tok.ap(), xnT, "0")
                cp = pr.ctr("dbgcp")
                with nc.Block() as blk:
                    @blk.sync
                    def _(sync):
                        cp.inc(sync.dma_start(xn_dbg.ap(), xn_tok.ap()), 16)
                        cp.inc(sync.dma_start(xnT_dbg.ap(), xnT[:, :, :]), 16)
                        sync.wait_ge(cp.sem, cp.n)
            l0s.close()
            return nc

        dbg = None
        if debug_stop == "sel":
            dbg = {
                "probs": nc.dram_tensor("d_probs", [P, E, NT], F32, kind="ExternalOutput"),
                "tlo": nc.dram_tensor("d_tlo", [P, E], F32, kind="ExternalOutput"),
                "m": nc.dram_tensor("d_m", [P, E * 8], I32, kind="ExternalOutput"),
                "g": nc.dram_tensor("d_g", [P, E * 8], F32, kind="ExternalOutput"),
            }
        with ExitStack() as ctx:
            xnT = ctx.enter_context(nc.sbuf_tensor("xnT", [P, ND, T], BF16))
            emit_xnT(pr, xn_tok.ap(), xnT, "0")
            emit_sel(pr, xnT, gwT, csb, m_slab, g_slab, probs,
                     csb['ident'], dbg=dbg)
            if debug_stop != "sel":
                with ExitStack() as actx:
                    emit_attn0(pr, xnT, W0, x.ap(), x1.ap(), outT_hbm, actx)
        if debug_stop == "sel":
            l0s.close()
            return nc
        if debug_stop == "attn0":
            x1_dbg = nc.dram_tensor("x1_dbg", [T, D], F32, kind="ExternalOutput")
            _dbg_copy(pr, nc, x1_dbg, x1)
            l0s.close()
            return nc
        emit_ffn0(pr, xn_tok.ap(), xin_tok, x1.ap(), w1, w2, m_slab, g_slab)
        l0s.close()
        if debug_stop == "l0":
            x1_dbg = nc.dram_tensor("x1_dbg", [T, D], F32, kind="ExternalOutput")
            _dbg_copy(pr, nc, x1_dbg, x1)
            return nc

        emit_ln(pr, x1.ap(), xn_tok.ap(), "1")
        if debug_stop == "ln1":
            xn_dbg = nc.dram_tensor("xn_dbg", [T, D], BF16, kind="ExternalOutput")
            xnT_dbg = nc.dram_tensor("xnT_dbg", [P, ND, T], BF16, kind="ExternalOutput")
            with ExitStack() as ctx:
                xnT1 = ctx.enter_context(nc.sbuf_tensor("xnT1", [P, ND, T], BF16))
                emit_xnT(pr, xn_tok.ap(), xnT1, "1", xnT_hbm=xnT_hbm)
                _dbg_copy(pr, nc, xn_dbg, xn_tok)
                _dbg_copy(pr, nc, xnT_dbg, xnT_hbm)
            return nc
        with ExitStack() as ctx:
            xnT1 = ctx.enter_context(nc.sbuf_tensor("xnT1", [P, ND, T], BF16))
            emit_xnT(pr, xn_tok.ap(), xnT1, "1", xnT_hbm=xnT_hbm)
            emit_attn1_proj(pr, xnT1, W1, qraw.ap(), kraw.ap(), vtok1.ap(),
                            combT_hbm, csb_ident, rw, rb,
                            comb_mode={"attn1q": 0, "attn1t": 1,
                                       "attn1u": 2}.get(debug_stop, 3))
        if debug_stop in ("attn1q", "attn1t", "attn1u"):
            q_dbg = nc.dram_tensor("q_dbg", [T, D], F32, kind="ExternalOutput")
            k_dbg = nc.dram_tensor("k_dbg", [T, D], F32, kind="ExternalOutput")
            _dbg_copy(pr, nc, q_dbg, qraw)
            _dbg_copy(pr, nc, k_dbg, kraw)
            return nc
        if debug_stop == "attn1p":
            q_dbg = nc.dram_tensor("q_dbg", [T, D], F32, kind="ExternalOutput")
            k_dbg = nc.dram_tensor("k_dbg", [T, D], F32, kind="ExternalOutput")
            cT_dbg = nc.dram_tensor("cT_dbg", [1, E * T], BF16, kind="ExternalOutput")
            _dbg_copy(pr, nc, q_dbg, qraw)
            _dbg_copy(pr, nc, k_dbg, kraw)
            _dbg_copy(pr, nc, cT_dbg, combT_hbm)
            return nc
        emit_ln(pr, qraw.ap(), qn_tok.ap(), "q1")
        emit_ln(pr, kraw.ap(), kn_tok.ap(), "k1")
        if debug_stop == "lnq1":
            qn_dbg = nc.dram_tensor("qn_dbg", [T, D], BF16, kind="ExternalOutput")
            kn_dbg = nc.dram_tensor("kn_dbg", [T, D], BF16, kind="ExternalOutput")
            _dbg_copy(pr, nc, qn_dbg, qn_tok)
            _dbg_copy(pr, nc, kn_dbg, kn_tok)
            return nc
        with ExitStack() as ctx:
            slab2 = ctx.enter_context(nc.sbuf_tensor("slab2", [P, ND, T], BF16))
            emit_xnT(pr, qn_tok.ap(), slab2, "tq", xnT_hbm=qnT_hbm)
            emit_xnT(pr, kn_tok.ap(), slab2, "tk")
            emit_attn1_core(pr, slab2, qnT_hbm, vtok1.ap(), W1["wo"], W1["bo"],
                            ls, x1.ap(), y.ap())
        if debug_stop == "attn1c":
            return nc
        emit_ffn1(pr, xnT_hbm, combT_hbm, upw, dnw, y.ap())
    return nc


def _dbg_copy(pr, nc, dst, srcd):
    cp = pr.ctr("dbgcp2")
    with nc.Block() as blk:
        @blk.sync
        def _(sync):
            cp.inc(sync.dma_start(dst.ap(), srcd.ap()), 16)
            sync.wait_ge(cp.sem, cp.n)


def _c(a):
    return np.ascontiguousarray(np.asarray(a, dtype=np.float32))


def kernel(L0_attn_x, L0_attn_Wq, L0_attn_bq, L0_attn_Wk, L0_attn_bk,
           L0_attn_Wv, L0_attn_bv, L0_attn_Wo, L0_attn_bo,
           L0_ffn_gate_w, L0_ffn_w1, L0_ffn_w2,
           L1_attn_Wq, L1_attn_bq, L1_attn_Wk, L1_attn_bk, L1_attn_Wv,
           L1_attn_bv, L1_attn_Wo, L1_attn_bo, L1_attn_layerscale,
           L1_ffn_router_w, L1_ffn_router_b, L1_ffn_expert_up,
           L1_ffn_expert_down):
    nc = build_program()
    common = {
        "gwT": _c(np.asarray(L0_ffn_gate_w).T),
        "wq0": _c(np.asarray(L0_attn_Wq).T), "wk0": _c(np.asarray(L0_attn_Wk).T),
        "wv0": _c(np.asarray(L0_attn_Wv).T), "wo0": _c(np.asarray(L0_attn_Wo).T),
        "bq0": _c(L0_attn_bq)[None, :], "bk0": _c(L0_attn_bk)[None, :],
        "bv0": _c(L0_attn_bv)[None, :], "bo0": _c(L0_attn_bo)[None, :],
        "w1": _c(L0_ffn_w1), "w2": _c(L0_ffn_w2),
        "wq1": _c(np.asarray(L1_attn_Wq).T), "wk1": _c(np.asarray(L1_attn_Wk).T),
        "wv1": _c(np.asarray(L1_attn_Wv).T), "wo1": _c(np.asarray(L1_attn_Wo).T),
        "bq1": _c(L1_attn_bq)[None, :], "bk1": _c(L1_attn_bk)[None, :],
        "bv1": _c(L1_attn_bv)[None, :], "bo1": _c(L1_attn_bo)[None, :],
        "ls": _c(L1_attn_layerscale)[None, :],
        "rw": _c(L1_ffn_router_w), "rb": _c(L1_ffn_router_b)[None, :],
        "upw": _c(np.asarray(L1_ffn_expert_up).transpose(0, 2, 1)),
        "dnw": _c(np.asarray(L1_ffn_expert_down).transpose(0, 2, 1)),
    }
    xs = np.asarray(L0_attn_x, dtype=np.float32)
    in_maps = [dict(common, x=np.ascontiguousarray(xs[b])) for b in range(xs.shape[0])]
    res = run_bass_kernel_spmd(nc, in_maps, core_ids=list(range(xs.shape[0])))
    return np.stack([r["y"] for r in res.results], axis=0)


# version 05
# speedup vs baseline: 1.1650x; 1.0694x over previous
import sys
sys.path.insert(0, "/opt/trn_rl_repo")
import numpy as np
import ml_dtypes
import concourse.bass as bass
import concourse.mybir as mybir
from concourse.bass import IndirectOffsetOnAxis
from concourse.bass_utils import run_bass_kernel_spmd
from contextlib import ExitStack

F32 = mybir.dt.float32
BF16 = mybir.dt.bfloat16
I32 = mybir.dt.int32
AF = mybir.ActivationFunctionType
ALU = mybir.AluOpType
AXX = mybir.AxisListType.X

P = 128
T = 4096
D = 1024
ND = D // P
NT = T // P
FF = 4096
NF = FF // P
E = 8
H = 16
HD = 64
SEL = 1024
LN_EPS = 1e-5
BISECT_ITERS = 34

BF = ml_dtypes.bfloat16


class Ctr:
    def __init__(self, nc, name):
        self.sem = nc.alloc_semaphore(name)
        self.n = 0

    def inc(self, inst, k=1):
        self.n += k
        return inst.then_inc(self.sem, k)


class DmaSlots:
    def __init__(self, nc, name, nslots):
        self.nc = nc
        self.sems = [nc.alloc_semaphore(f"{name}{i}") for i in range(nslots)]
        self.cnt = [0] * nslots
        self.tgt = {}
        self.G = nslots

    def issue(self, i, inst):
        s = i % self.G
        self.cnt[s] += 16
        self.tgt[i] = (s, self.cnt[s])
        return inst.then_inc(self.sems[s], 16)

    def wait(self, eng, i):
        if i in self.tgt:
            s, t = self.tgt[i]
        else:
            s, t = i % self.G, 16 * (i // self.G + 1)
        eng.wait_ge(self.sems[s], t)

    def wait_all(self, eng):
        for s in range(self.G):
            if self.cnt[s]:
                eng.wait_ge(self.sems[s], self.cnt[s])


class Prog:
    def __init__(self, nc):
        self.nc = nc
        self.c = {}
        self.t = {}
        self.consts = {}
        self._const_stack = None

    def ctr(self, name):
        if name not in self.c:
            self.c[name] = Ctr(self.nc, name)
        return self.c[name]

    def release(self, ctrs=(), slots=()):
        nc = self.nc
        sems = []
        for c in ctrs:
            if isinstance(c, str):
                c = self.c.pop(c)
            sems.append(c.sem)
        for sl in slots:
            sems.extend(sl.sems)
        with nc.Block() as blk:
            @blk.gpsimd
            def _(g):
                for s in sems:
                    g.dma_reset(semaphore_range=range(s.num, s.num + 1))

            @blk.sync
            def _(sync):
                for s in sems:
                    sync.sem_clear(s)
        for s in sems:
            nc.release_semaphore(s)

    def init_consts(self, stack, values):
        nc = self.nc
        tile = stack.enter_context(
            nc.sbuf_tensor("consts", [P, max(1, len(values))], F32))
        for i, v in enumerate(values):
            self.consts[float(v)] = tile[:, i:i + 1]
        with nc.Block() as blk:
            @blk.vector
            def _(v_):
                for i, v in enumerate(values):
                    v_.memset(tile[:, i:i + 1], float(v))

    def const(self, v):
        return self.consts[float(v)]


def emit_ln(pr, x_src, xn_tok, tag):
    nc = pr.nc
    G = 4
    nc_ = pr.nc
    ld = DmaSlots(nc_, f"lnld{tag}", 4)
    out = DmaSlots(nc_, f"lnou{tag}", 4)
    sq = pr.ctr(f"lnsq{tag}")
    st = pr.ctr(f"lnst{tag}")
    rs = pr.ctr(f"lnrs{tag}")
    fin = pr.ctr(f"lnfi{tag}")
    sq0, st0, rs0, fin0 = sq.n, st.n, rs.n, fin.n

    with ExitStack() as ctx:
        En = ctx.enter_context
        xb = [En(nc.sbuf_tensor(f"lnx{i}{tag}", [P, D], F32)) for i in range(G)]
        sqb = [En(nc.sbuf_tensor(f"lnq{i}{tag}", [P, D], F32)) for i in range(G)]
        xnb = [En(nc.sbuf_tensor(f"lnn{i}{tag}", [P, D], BF16)) for i in range(G)]
        stt = En(nc.sbuf_tensor(f"lns{tag}", [P, 8 * G], F32))

        with nc.Block() as blk:
            @blk.sync
            def _(sync):
                for i in range(NT):
                    if i >= G:
                        sync.wait_ge(fin.sem, fin0 + 2 * (i - G + 1))
                    ld.issue(i, sync.dma_start(xb[i % G][:], x_src[i * P:(i + 1) * P, :]))

            @blk.scalar
            def _(s):
                for i in range(NT + 1):
                    if i < NT:
                        b = i % G
                        ld.wait(s, i)
                        if i >= G:
                            s.wait_ge(st.sem, st0 + 6 * (i - G + 1))
                        sq.inc(s.activation(sqb[b][:], xb[b][:], AF.Square))
                    j = i - 1
                    if j >= 0:
                        o = 8 * (j % G)
                        s.wait_ge(st.sem, st0 + 6 * (j + 1))
                        if j >= G:
                            s.wait_ge(fin.sem, fin0 + 2 * (j - G + 1))
                        rs.inc(s.activation(stt[:, o + 4:o + 5], stt[:, o + 3:o + 4],
                                            AF.Sqrt, bias=pr.const(LN_EPS)))

            @blk.vector
            def _(v):
                for i in range(NT + 1):
                    if i < NT:
                        b = i % G
                        o = 8 * b
                        v.wait_ge(sq.sem, sq0 + (i + 1))
                        if i >= G:
                            v.wait_ge(fin.sem, fin0 + 2 * (i - G + 1))
                        st.inc(v.tensor_reduce(stt[:, o:o + 1], xb[b][:], axis=AXX, op=ALU.add))
                        st.inc(v.tensor_reduce(stt[:, o + 1:o + 2], sqb[b][:], axis=AXX, op=ALU.add))
                        v.wait_ge(st.sem, st.n)
                        st.inc(v.tensor_scalar_mul(stt[:, o + 2:o + 3], stt[:, o:o + 1], 1.0 / D))
                        st.inc(v.tensor_scalar_mul(stt[:, o + 1:o + 2], stt[:, o + 1:o + 2], 1.0 / D))
                        v.wait_ge(st.sem, st.n)
                        st.inc(v.tensor_tensor(stt[:, o + 3:o + 4], stt[:, o + 2:o + 3],
                                               stt[:, o + 2:o + 3], op=ALU.mult))
                        v.wait_ge(st.sem, st.n)
                        st.inc(v.tensor_sub(stt[:, o + 3:o + 4], stt[:, o + 1:o + 2],
                                            stt[:, o + 3:o + 4]))
                    j = i - 1
                    if j >= 0:
                        b = j % G
                        o = 8 * b
                        v.wait_ge(rs.sem, rs0 + (j + 1))
                        fin.inc(v.reciprocal(stt[:, o + 4:o + 5], stt[:, o + 4:o + 5]))
                        v.wait_ge(fin.sem, fin.n)
                        if j >= G:
                            out.wait(v, j - G)
                        fin.inc(v.tensor_scalar(xnb[b][:], xb[b][:],
                                                stt[:, o + 2:o + 3], stt[:, o + 4:o + 5],
                                                op0=ALU.subtract, op1=ALU.mult))

            @blk.gpsimd
            def _(g):
                for j in range(NT):
                    g.wait_ge(fin.sem, fin0 + 2 * (j + 1))
                    out.issue(j, g.dma_start(xn_tok[j * P:(j + 1) * P, :], xnb[j % G][:]))
                out.wait_all(g)

    pr.release(ctrs=[f"lnsq{tag}", f"lnst{tag}", f"lnrs{tag}", f"lnfi{tag}"],
               slots=[ld, out])


def emit_xnT(pr, xn_tok, xnT_slab, tag, xnT_hbm=None):
    nc = pr.nc
    tr = pr.ctr(f"xt{tag}")
    with nc.Block() as blk:
        @blk.sync
        def _(sync):
            for c in range(ND):
                tr.inc(sync.dma_start(xnT_slab[:, c, :],
                                      xn_tok[:, c * P:(c + 1) * P],
                                      transpose=True), 16)
            sync.wait_ge(tr.sem, tr.n)
            if xnT_hbm is not None:
                tr.inc(sync.dma_start(xnT_hbm.ap(), xnT_slab[:, :, :]), 16)
                sync.wait_ge(tr.sem, tr.n)
    pr.release(ctrs=[f"xt{tag}"])


def emit_sel(pr, xnT, gw_dram, cst, m_slab, g_slab, probs, c_ident, dbg=None):
    nc = pr.nc
    gmm = pr.ctr("sl_gmm")
    gpre = pr.ctr("sl_pre")
    gexp = pr.ctr("sl_exp")
    gpos = pr.ctr("sl_pos")
    bis = pr.ctr("sl_bis")
    bmm = pr.ctr("sl_bmm")
    rnk = pr.ctr("sl_rnk")
    rmm = pr.ctr("sl_rmm")
    ld = pr.ctr("sl_ld")

    c_ones, c_utri, c_slots, c_tok = cst["ones"], cst["utri"], cst["slots"], cst["tok"]

    with ExitStack() as ctx:
        En = ctx.enter_context
        gw = En(nc.sbuf_tensor("gw", [P, ND, E], BF16))
        smax = En(nc.sbuf_tensor("smax", [P, 2, 2], F32))
        sexp = En(nc.sbuf_tensor("sexp", [P, 2, E], F32))
        sstat = En(nc.sbuf_tensor("sstat", [P, 2, 2], F32))
        theta = En(nc.sbuf_tensor("theta", [P, E], F32))
        tlo = En(nc.sbuf_tensor("tlo", [P, E], F32))
        tmp8 = En(nc.sbuf_tensor("tmp8", [P, E], F32))
        d2 = En(nc.sbuf_tensor("d2", [P, E], F32))
        cnt = En(nc.sbuf_tensor("cnt", [P, E], F32))
        maskb = En(nc.sbuf_tensor("maskb", [P, E, NT], F32))
        toff = En(nc.sbuf_tensor("toff", [1, 2 * 64], F32))
        rankp = En(nc.sbuf_tensor("rankp", [P, NT], F32))
        mtile = En(nc.sbuf_tensor("mtile", [P, 2, SEL], F32))
        mgr_all = En(nc.sbuf_tensor("mgr_all", [P, NT, 2], F32))
        mg_row = En(nc.sbuf_tensor("mg_row", [2, E * P], F32))
        p_gate = En(nc.psum_tensor("p_gate", [P, 2, E], F32))
        p_cnt = En(nc.psum_tensor("p_cnt", [P, E * NT], F32))
        p_pref = En(nc.psum_tensor("p_pref", [P, NT], F32))
        p_tot = En(nc.psum_tensor("p_tot", [1, NT], F32))
        p_mgN = En(nc.psum_tensor("p_mgN", [2, 2, 512], F32))
        p_tr = En(nc.psum_tensor("p_tr", [P, 2, 512], F32))

        with nc.Block() as blk:
            @blk.gpsimd
            def _(g):
                ld.inc(g.dma_start(gw[:, :, :], gw_dram.ap().rearrange(
                    "(c p) e -> p c e", p=P)), 16)
                g.wait_ge(ld.sem, ld.n)

        with nc.Block() as blk:
            @blk.tensor
            def _(t):
                for i in range(NT):
                    b = i % 2
                    if i >= 2:
                        t.wait_ge(gpos.sem, 3 * (i - 1))
                    for c in range(ND):
                        mm = t.matmul(p_gate[:, b, :], xnT[:, c, i * P:(i + 1) * P],
                                      gw[:, c, :], start=(c == 0), stop=(c == ND - 1))
                    gmm.inc(mm)

            @blk.vector
            def _(v):
                for i in range(NT + 1):
                    if i < NT:
                        b = i % 2
                        v.wait_ge(gmm.sem, i + 1)
                        if i >= 2:
                            v.wait_ge(gexp.sem, i - 1)
                        gpre.inc(v.tensor_reduce(smax[:, b, 0:1], p_gate[:, b, :],
                                                 axis=AXX, op=ALU.max))
                        v.wait_ge(gpre.sem, gpre.n)
                        gpre.inc(v.tensor_scalar_mul(smax[:, b, 1:2], smax[:, b, 0:1], -1.0))
                    j = i - 1
                    if j >= 0:
                        b = j % 2
                        v.wait_ge(gexp.sem, j + 1)
                        gpos.inc(v.tensor_reduce(sstat[:, b, 0:1], sexp[:, b, :],
                                                 axis=AXX, op=ALU.add))
                        v.wait_ge(gpos.sem, gpos.n)
                        gpos.inc(v.reciprocal(sstat[:, b, 1:2], sstat[:, b, 0:1]))
                        v.wait_ge(gpos.sem, gpos.n)
                        gpos.inc(v.tensor_scalar_mul(
                            probs[:, :, j:j + 1], sexp[:, b, :].unsqueeze(2),
                            sstat[:, b, 1:2]))

            @blk.scalar
            def _(s):
                for i in range(NT):
                    b = i % 2
                    s.wait_ge(gpre.sem, 2 * (i + 1))
                    if i >= 2:
                        s.wait_ge(gpos.sem, 3 * (i - 1))
                    gexp.inc(s.activation(sexp[:, b, :], p_gate[:, b, :],
                                          AF.Exp, bias=smax[:, b, 1:2]))

        with nc.Block() as blk:
            @blk.vector
            def _(v):
                bis.inc(v.memset(theta[:], 0.5))
                bis.inc(v.memset(tlo[:], 0.0))
                v.wait_ge(bis.sem, bis.n)
                for k in range(BISECT_ITERS):
                    step = 2.0 ** (-(k + 2))
                    if k > 0:
                        v.wait_ge(bmm.sem, k)
                        v.wait_ge(bis.sem, bis.n)
                    for e in range(E):
                        bis.inc(v.tensor_scalar(maskb[:, e, :], probs[:, e, :],
                                                theta[:, e:e + 1], None, op0=ALU.is_ge))
                    v.wait_ge(bmm.sem, k + 1)
                    bis.inc(v.tensor_reduce(cnt[:, :].unsqueeze(2), p_cnt[:].rearrange(
                        "p (e c) -> p e c", e=E), axis=AXX, op=ALU.add))
                    v.wait_ge(bis.sem, bis.n)
                    bis.inc(v.tensor_scalar(d2[:], cnt[:], float(SEL) - 0.5, None,
                                            op0=ALU.is_ge))
                    v.wait_ge(bis.sem, bis.n)
                    bis.inc(v.tensor_sub(tmp8[:], theta[:], tlo[:]))
                    v.wait_ge(bis.sem, bis.n)
                    bis.inc(v.tensor_mul(tmp8[:], tmp8[:], d2[:]))
                    v.wait_ge(bis.sem, bis.n)
                    bis.inc(v.tensor_add(tlo[:], tlo[:], tmp8[:]))
                    v.wait_ge(bis.sem, bis.n)
                    bis.inc(v.tensor_scalar(tmp8[:], d2[:], 2.0 * step, -step,
                                            op0=ALU.mult, op1=ALU.add))
                    v.wait_ge(bis.sem, bis.n)
                    bis.inc(v.tensor_add(theta[:], theta[:], tmp8[:]))
                assert bis.n == 2 + 15 * BISECT_ITERS

            @blk.tensor
            def _(t):
                for k in range(BISECT_ITERS):
                    t.wait_ge(bis.sem, 2 + 15 * k + 8)
                    bmm.inc(t.matmul(p_cnt[:], c_ones[:], maskb[:].rearrange(
                        "p e c -> p (e c)"), start=True, stop=True))

        with nc.Block() as blk:
            DVE_PER_E = 12 + NT + 2 + 16
            PE_PER_E = 3 + 2 * NT + 8

            @blk.vector
            def _(v):
                for e in range(E):
                    base = DVE_PER_E * e
                    pbase = PE_PER_E * e
                    if e > 0:
                        v.wait_ge(rmm.sem, PE_PER_E * (e - 1) + 2)
                    rnk.inc(v.tensor_scalar(maskb[:, 0, :], probs[:, e, :],
                                            tlo[:, e:e + 1], None, op0=ALU.is_ge))
                    v.wait_ge(rmm.sem, pbase + 1)
                    rnk.inc(v.memset(toff[:, 0:64], 0.0))
                    rnk.inc(v.memset(toff[:, 64:128], 0.0))
                    v.wait_ge(rnk.sem, rnk.n)
                    rnk.inc(v.tensor_copy(toff[0:1, 33:64], p_tot[0:1, 0:31]))
                    v.wait_ge(rnk.sem, rnk.n)
                    src_, dst_ = 0, 64
                    for sh in (1, 2, 4, 8, 16):
                        rnk.inc(v.tensor_add(toff[0:1, dst_ + 32:dst_ + 64],
                                             toff[0:1, src_ + 32:src_ + 64],
                                             toff[0:1, src_ + 32 - sh:src_ + 64 - sh]))
                        v.wait_ge(rnk.sem, rnk.n)
                        src_, dst_ = dst_, src_
                    v.wait_ge(rmm.sem, pbase + 3)
                    rnk.inc(v.tensor_mul(rankp[:], p_pref[:], maskb[:, 0, :]))
                    rnk.inc(v.tensor_copy(mgr_all[:, :, 0:1], c_tok[:, :].unsqueeze(2)))
                    rnk.inc(v.tensor_copy(mgr_all[:, :, 1:2], probs[:, e, :].unsqueeze(2)))
                    assert rnk.n == base + 12
                    for c in range(NT):
                        v.wait_ge(rnk.sem, base + 12)
                        if c >= 2:
                            v.wait_ge(rmm.sem, pbase + 3 + 2 * (c - 1))
                        rnk.inc(v.tensor_scalar(mtile[:, c % 2, :], c_slots[:, :],
                                                rankp[:, c:c + 1], None, op0=ALU.is_equal))
                    v.wait_ge(rmm.sem, pbase + 3 + 2 * NT)
                    if e > 0:
                        v.wait_ge(rmm.sem, PE_PER_E * e)
                    rnk.inc(v.tensor_copy(mg_row[:, 0:512], p_mgN[:, 0, :]))
                    rnk.inc(v.tensor_copy(mg_row[:, 512:1024], p_mgN[:, 1, :]))
                    assert rnk.n == base + 12 + NT + 2
                    for sc in range(E):
                        v.wait_ge(rmm.sem, pbase + 3 + 2 * NT + sc + 1)
                        rnk.inc(v.tensor_copy(m_slab[:, e * 8 + sc:e * 8 + sc + 1],
                                              p_tr[:, sc % 2, 0:1]))
                        rnk.inc(v.tensor_copy(g_slab[:, e * 8 + sc:e * 8 + sc + 1],
                                              p_tr[:, sc % 2, 1:2]))
                    assert rnk.n == DVE_PER_E * (e + 1)

            @blk.tensor
            def _(t):
                for e in range(E):
                    base = DVE_PER_E * e
                    pbase = PE_PER_E * e
                    t.wait_ge(rnk.sem, base + 1)
                    if e > 0:
                        t.wait_ge(rnk.sem, base - DVE_PER_E + 4)
                    rmm.inc(t.matmul(p_tot[:, 0:NT], c_ones[:, 0:1], maskb[:, 0, :],
                                     start=True, stop=True))
                    rmm.inc(t.matmul(p_pref[:], c_utri[:], maskb[:, 0, :],
                                     start=True, stop=False))
                    t.wait_ge(rnk.sem, base + 9)
                    rmm.inc(t.matmul(p_pref[:], c_ones[0:1, :], toff[0:1, 64 + 32:64 + 64],
                                     start=False, stop=True))
                    if e > 0:
                        t.wait_ge(rnk.sem, base - DVE_PER_E + 12 + NT + 2)
                    for c in range(NT):
                        t.wait_ge(rnk.sem, base + 12 + c + 1)
                        for ch in range(2):
                            rmm.inc(t.matmul(p_mgN[:, ch, :], mgr_all[:, c, :],
                                             mtile[:, c % 2, ch * 512:(ch + 1) * 512],
                                             start=(c == 0), stop=(c == NT - 1)))
                    t.wait_ge(rnk.sem, base + 12 + NT + 2)
                    for sc in range(E):
                        if sc >= 2:
                            t.wait_ge(rnk.sem, base + 12 + NT + 2 + 2 * (sc - 1))
                        rmm.inc(t.matmul(p_tr[:, sc % 2, 0:2],
                                         mg_row[0:2, sc * P:(sc + 1) * P],
                                         c_ident[0:2, 0:2],
                                         start=True, stop=True))
                    assert rmm.n == PE_PER_E * (e + 1)

        pr.release(ctrs=["sl_gmm", "sl_pre", "sl_exp", "sl_pos", "sl_bis",
                         "sl_bmm", "sl_rnk", "sl_rmm", "sl_ld"])
        if dbg is not None:
            cpd = pr.ctr("sl_dbg")
            with nc.Block() as blk:
                @blk.sync
                def _(sync):
                    cpd.inc(sync.dma_start(dbg["probs"].ap(), probs[:, :, :]), 16)
                    cpd.inc(sync.dma_start(dbg["tlo"].ap(), tlo[:]), 16)
                    cpd.inc(sync.dma_start(dbg["m"].ap(), m_slab[:, :]), 16)
                    cpd.inc(sync.dma_start(dbg["g"].ap(), g_slab[:, :]), 16)
                    sync.wait_ge(cpd.sem, cpd.n)



def emit_attn0(pr, xnT, W, x_dram, x1_dram, outT_hbm, ctx_weights):
    nc = pr.nc
    QB = 512
    NQB = T // QB
    NIT = 2 * NQB

    wld = pr.ctr("a0_wld")
    pmm = pr.ctr("a0_pmm")
    pev = pr.ctr("a0_pev")
    smm = pr.ctr("a0_smm")
    avm = pr.ctr("a0_avm")
    bcm = pr.ctr("a0_bcm")
    xdv = pr.ctr("a0_xdv")
    exc = pr.ctr("a0_exc")
    sot = DmaSlots(nc, "a0_sot", 2)

    En = ctx_weights.enter_context
    wq = En(nc.sbuf_tensor("a0_wq", [P, ND, D], BF16))
    wk = En(nc.sbuf_tensor("a0_wk", [P, ND, D], BF16))
    wv = En(nc.sbuf_tensor("a0_wv", [P, ND, D], BF16))
    wo = En(nc.sbuf_tensor("a0_wo", [P, ND, D], BF16))
    brow = En(nc.sbuf_tensor("a0_brow", [1, 4 * D], BF16))
    ones = En(nc.sbuf_tensor("a0_ones", [1, QB], BF16))

    with nc.Block() as blk:
        @blk.gpsimd
        def _(g):
            for i, (wsb, wd) in enumerate([(wq, W["wq"]), (wk, W["wk"]),
                                           (wv, W["wv"]), (wo, W["wo"])]):
                wld.inc(g.dma_start(wsb[:, :, :], wd.ap().rearrange(
                    "(c p) e -> p c e", p=P)), 16)
            for i, bd in enumerate([W["bq"], W["bk"], W["bv"], W["bo"]]):
                wld.inc(g.dma_start(brow[0:1, i * D:(i + 1) * D], bd.ap()), 16)
            g.wait_ge(wld.sem, wld.n)

        @blk.vector
        def _(v):
            v.memset(ones[:], 1.0)

    bq = brow[0:1, 0 * D:1 * D]
    bk = brow[0:1, 1 * D:2 * D]
    bv = brow[0:1, 2 * D:3 * D]
    bo = brow[0:1, 3 * D:4 * D]

    with ExitStack() as ctx:
        En2 = ctx.enter_context
        qt = En2(nc.sbuf_tensor("a0_qt", [P, T], BF16))
        kt = En2(nc.sbuf_tensor("a0_kt", [P, T], BF16))
        vs = En2(nc.sbuf_tensor("a0_vs", [P, NT, 136], BF16))
        at = En2(nc.sbuf_tensor("a0_at", [P, 2, 2, QB], BF16))
        rlf = En2(nc.sbuf_tensor("a0_rlf", [1, QB], F32))
        rlb = En2(nc.sbuf_tensor("a0_rlb", [1, QB], BF16))
        bcs = En2(nc.sbuf_tensor("a0_bcs", [64, QB], F32))
        ots = [En2(nc.sbuf_tensor(f"a0_ot{i}", [64, 2, QB], BF16)) for i in range(2)]
        psH = [[En2(nc.psum_tensor(f"a0_psH{h}{i}", [P, QB], F32)) for i in range(2)]
               for h in range(2)]
        pAV = [En2(nc.psum_tensor(f"a0_pAV{h}", [66, QB], F32)) for h in range(2)]
        pM = En2(nc.psum_tensor("a0_pM", [P, QB], F32))
        pBC = En2(nc.psum_tensor("a0_pBC", [64, QB], F32))

        with nc.Block() as blk:
            @blk.vector
            def _(v):
                ms = pr.ctr("a0_ms")
                ms.inc(v.memset(vs[:, :, 64:68], 0.0))
                ms.inc(v.memset(vs[:, :, 132:136], 0.0))
                v.wait_ge(ms.sem, ms.n)
                v.memset(vs[:, :, 64:65], 1.0)
                v.memset(vs[:, :, 132:133], 1.0)

        with nc.Block() as blk:
            @blk.tensor
            def _(t):
                for pair in range(E):
                    pb_mm = 48 * pair
                    pb_ev = 80 * pair
                    col = slice(pair * P, (pair + 1) * P)
                    for wsb, bias in ((wq, bq), (wk, bk)):
                        qk = 0 if wsb is wq else 1
                        for ts in range(8):
                            ps = psH[0][ts % 2]
                            if pair > 0 and qk == 0 and ts < 2:
                                t.wait_ge(exc.sem, 512 * pair)
                            if pb_mm + 8 * qk + ts >= 2:
                                t.wait_ge(pev.sem, pb_ev + 8 * qk + ts - 1)
                            sl = slice(ts * QB, (ts + 1) * QB)
                            for c in range(ND):
                                t.matmul(ps[:, :], wsb[:, c, col], xnT[:, c, sl],
                                         start=(c == 0), stop=False)
                            pmm.inc(t.matmul(ps[:, :], bias[:, col], ones[:],
                                             start=False, stop=True))
                    for tt in range(NT):
                        if 16 + tt >= 1:
                            t.wait_ge(pev.sem, pb_ev + 16 + 2 * tt)
                        for c in range(ND):
                            t.matmul(pM[:, 0:P], xnT[:, c, tt * P:(tt + 1) * P],
                                     wv[:, c, col], start=(c == 0), stop=False)
                        pmm.inc(t.matmul(pM[:, 0:P], ones[0:1, 0:P], bv[:, col],
                                         start=False, stop=True))
                    t.wait_ge(pev.sem, pb_ev + 16)
                    for qb in range(NQB):
                        itq = pair * NQB + qb
                        sb = 64 * itq
                        qsl = slice(qb * QB, (qb + 1) * QB)

                        def em_scores(kt_i):
                            req = sb + 2 * kt_i - 2
                            if req > 0:
                                t.wait_ge(exc.sem, req)
                            for h in range(2):
                                smm.inc(t.matmul(
                                    psH[h][kt_i % 2][:, :],
                                    kt[64 * h:64 * h + 64, kt_i * P:(kt_i + 1) * P],
                                    qt[64 * h:64 * h + 64, qsl],
                                    tile_position=(64 * h, 0),
                                    start=True, stop=True))

                        def em_av(kt_i):
                            for h in range(2):
                                t.wait_ge(exc.sem, sb + 2 * kt_i + h + 1)
                                if kt_i == 0 and h == 0:
                                    if itq >= 1:
                                        t.wait_ge(xdv.sem, 8 * itq)
                                    if qb == 0:
                                        t.wait_ge(pev.sem, 80 * (pair + 1))
                                avm.inc(t.matmul(pAV[h][:, :],
                                                 vs[:, kt_i, 68 * h:68 * h + 66],
                                                 at[:, h, kt_i % 2, :],
                                                 start=(kt_i == 0),
                                                 stop=(kt_i == NT - 1)))

                        for kt_i in range(NT):
                            em_scores(kt_i)
                            if kt_i >= 1:
                                em_av(kt_i - 1)
                        em_av(NT - 1)
                        for h in range(2):
                            t.wait_ge(xdv.sem, 8 * itq + 4 * h + 2)
                            bcm.inc(t.matmul(pBC[:, :], ones[0:1, 0:64], rlb[:],
                                             start=True, stop=True))

            @blk.vector
            def _(v):
                for pair in range(E):
                    pb_mm = 48 * pair
                    pb_ev = 80 * pair
                    for qk in range(2):
                        dst = qt if qk == 0 else kt
                        for ts in range(8):
                            v.wait_ge(pmm.sem, pb_mm + 8 * qk + ts + 1)
                            pev.inc(v.tensor_copy(dst[:, ts * QB:(ts + 1) * QB],
                                                  psH[0][ts % 2][:, :]))
                    for tt in range(NT):
                        v.wait_ge(pmm.sem, pb_mm + 16 + tt + 1)
                        pev.inc(v.tensor_copy(vs[:, tt, 0:64], pM[:, 0:64]))
                        pev.inc(v.tensor_copy(vs[:, tt, 68:132], pM[:, 64:128]))
                    for qb in range(NQB):
                        itq = pair * NQB + qb
                        for h in range(2):
                            v.wait_ge(avm.sem, 64 * itq + 63 + h)
                            xdv.inc(v.reciprocal(rlf[:], pAV[h][64:65, :]))
                            v.wait_ge(xdv.sem, xdv.n)
                            xdv.inc(v.tensor_copy(rlb[:], rlf[:]))
                            v.wait_ge(bcm.sem, 2 * itq + h + 1)
                            xdv.inc(v.tensor_copy(bcs[:], pBC[:, :]))
                            v.wait_ge(xdv.sem, xdv.n)
                            if itq >= 2:
                                sot.wait(v, 2 * (itq - 2) + h)
                            xdv.inc(v.tensor_mul(ots[itq % 2][:, h, :],
                                                 pAV[h][0:64, :], bcs[:]))

            @blk.scalar
            def _(s):
                for pair in range(E):
                    for qb in range(NQB):
                        itq = pair * NQB + qb
                        sb = 64 * itq
                        for kt_i in range(NT):
                            for h in range(2):
                                s.wait_ge(smm.sem, sb + 2 * kt_i + h + 1)
                                war = sb + 2 * kt_i + h - 3
                                if war > 0:
                                    s.wait_ge(avm.sem, war)
                                exc.inc(s.activation(at[:, h, kt_i % 2, :],
                                                     psH[h][kt_i % 2][:, :],
                                                     AF.Exp, scale=0.125))

            @blk.sync
            def _(sync):
                for pair in range(E):
                    for qb in range(NQB):
                        itq = pair * NQB + qb
                        for h in range(2):
                            head = 2 * pair + h
                            sync.wait_ge(xdv.sem, 8 * itq + 4 * h + 4)
                            sot.issue(2 * itq + h, sync.dma_start(
                                outT_hbm.ap()[head * 64:(head + 1) * 64,
                                              qb * QB:(qb + 1) * QB],
                                ots[itq % 2][:, h, :]))
                sot.wait_all(sync)

    old = DmaSlots(nc, "a0_old", 2)
    xld = DmaSlots(nc, "a0_xld", 3)
    omm = pr.ctr("a0_omm")
    oev = pr.ctr("a0_oev")
    ost = DmaSlots(nc, "a0_ost", 3)
    outT_v = outT_hbm.ap().rearrange("(c p) t -> p c t", p=P)

    with ExitStack() as ctx:
        En2 = ctx.enter_context
        otb = [En2(nc.sbuf_tensor(f"a0_otb{i}", [P, ND, QB], BF16)) for i in range(2)]
        xtb = [En2(nc.sbuf_tensor(f"a0_xtb{i}", [P, D], F32)) for i in range(3)]
        x1b = [En2(nc.sbuf_tensor(f"a0_x1b{i}", [P, D], F32)) for i in range(3)]
        pO = [En2(nc.psum_tensor(f"a0_pO{i}", [P, D], F32)) for i in range(2)]

        with nc.Block() as blk:
            @blk.sync
            def _(sync):
                for i in range(NT):
                    tg = i // 4
                    if i % 4 == 0:
                        if tg >= 2:
                            sync.wait_ge(omm.sem, 4 * (tg - 1))
                        old.issue(tg, sync.dma_start(otb[tg % 2][:, :, :],
                                                     outT_v[:, :, tg * QB:(tg + 1) * QB]))
                    if i >= 3:
                        sync.wait_ge(oev.sem, i - 2)
                    xld.issue(i, sync.dma_start(xtb[i % 3][:], x_dram[i * P:(i + 1) * P, :]))

            @blk.tensor
            def _(t):
                for i in range(NT):
                    tg, tl = i // 4, i % 4
                    old.wait(t, tg)
                    if i >= 2:
                        t.wait_ge(oev.sem, i - 1)
                    ps = pO[i % 2]
                    for dh in range(2):
                        dsl = slice(dh * QB, (dh + 1) * QB)
                        for c in range(ND):
                            t.matmul(ps[:, dsl], otb[tg % 2][:, c, tl * P:(tl + 1) * P],
                                     wo[:, c, dsl], start=(c == 0), stop=False)
                        last = t.matmul(ps[:, dsl], ones[0:1, 0:P], bo[:, dsl],
                                        start=False, stop=True)
                    omm.inc(last)

            @blk.vector
            def _(v):
                for i in range(NT):
                    v.wait_ge(omm.sem, i + 1)
                    xld.wait(v, i)
                    if i >= 3:
                        ost.wait(v, i - 3)
                    oev.inc(v.tensor_add(x1b[i % 3][:], xtb[i % 3][:], pO[i % 2][:]))

            @blk.gpsimd
            def _(g):
                for i in range(NT):
                    g.wait_ge(oev.sem, i + 1)
                    ost.issue(i, g.dma_start(x1_dram[i * P:(i + 1) * P, :], x1b[i % 3][:]))
                ost.wait_all(g)

    pr.release(ctrs=["a0_wld", "a0_pmm", "a0_pev", "a0_smm", "a0_avm", "a0_bcm",
                     "a0_xdv", "a0_exc", "a0_ms", "a0_omm", "a0_oev"],
               slots=[sot, old, xld, ost])


def emit_ffn0(pr, xn_tok, xin_tok, x1_dram, w1_dram, w2_dram, m_slab, g_slab):
    nc = pr.nc
    FH = FF // 2
    upm = pr.ctr("f0_upm")
    gel = pr.ctr("f0_gel")
    dnm = pr.ctr("f0_dnm")
    dnev = pr.ctr("f0_dnev")
    tr = pr.ctr("f0_tr")
    gth = DmaSlots(nc, "f0_gth", 2)
    wrt = DmaSlots(nc, "f0_wrt", 2)
    w1l = DmaSlots(nc, "f0_w1l", 2)
    w2l = DmaSlots(nc, "f0_w2l", 2)
    scat = DmaSlots(nc, "f0_scat", 2)

    with ExitStack() as ctx:
        En = ctx.enter_context
        xg = [En(nc.sbuf_tensor(f"f0_xg{i}", [P, D], BF16)) for i in range(2)]
        xinT = En(nc.sbuf_tensor("f0_xinT", [P, ND, SEL], BF16))
        w1h = [En(nc.sbuf_tensor(f"f0_w1h{i}", [P, ND, FH], BF16)) for i in range(2)]
        w2h = [En(nc.sbuf_tensor(f"f0_w2h{i}", [P, 16, D], BF16)) for i in range(2)]
        hT = En(nc.sbuf_tensor("f0_hT", [P, 16, SEL], BF16))
        stage = [En(nc.sbuf_tensor(f"f0_st{i}", [P, D], F32)) for i in range(2)]
        pU = [En(nc.psum_tensor(f"f0_pU{i}", [P, D], F32)) for i in range(2)]
        pD = [En(nc.psum_tensor(f"f0_pD{i}", [P, D], F32)) for i in range(2)]

        with nc.Block() as blk:
            @blk.gpsimd
            def _(g):
                for sc in range(8):
                    if sc >= 2:
                        wrt.wait(g, sc - 2)
                    gth.issue(sc, g.indirect_dma_start(
                        xg[sc % 2][:], None, xn_tok,
                        IndirectOffsetOnAxis(m_slab[:, sc:sc + 1], 0)))
                for e in range(E):
                    for fh in range(2):
                        fi = 2 * e + fh
                        if fi >= 2:
                            g.wait_ge(upm.sem, 32 * (fi - 1))
                        w1l.issue(fi, g.dma_start(
                            w1h[fi % 2][:, :, :],
                            w1_dram.ap()[e].rearrange("(c p) f -> p c f", p=P)
                            [:, :, fh * FH:(fh + 1) * FH]))
                        if fi >= 2:
                            g.wait_ge(dnm.sem, 8 * (fi - 1))
                        w2l.issue(fi, g.dma_start(
                            w2h[fi % 2][:, :, :],
                            w2_dram.ap()[e].rearrange("(c p) d -> p c d", p=P)
                            [:, fh * 16:(fh + 1) * 16, :]))
                    if e + 1 < E:
                        for sc in range(8):
                            gi = 8 * (e + 1) + sc
                            wrt.wait(g, gi - 2)
                            gth.issue(gi, g.indirect_dma_start(
                                xg[gi % 2][:], None, xn_tok,
                                IndirectOffsetOnAxis(
                                    m_slab[:, (e + 1) * 8 + sc:(e + 1) * 8 + sc + 1], 0)))
                    for fh in range(2):
                        scat.wait_all(g)
                        for selt in range(8):
                            sidx = 16 * e + 8 * fh + selt
                            g.wait_ge(dnev.sem, 16 * e + 8 * fh + selt + 1)
                            scat.issue(sidx, g.indirect_dma_start(
                                x1_dram, IndirectOffsetOnAxis(
                                    m_slab[:, e * 8 + selt:e * 8 + selt + 1], 0),
                                stage[sidx % 2][:], None, compute_op=ALU.add))
                gth.wait_all(g)
                scat.wait_all(g)

            @blk.sync
            def _(sync):
                for e in range(E):
                    for sc in range(8):
                        gi = 8 * e + sc
                        gth.wait(sync, gi)
                        if sc == 0:
                            sync.wait_ge(tr.sem, 16 * 8 * e)
                        wrt.issue(gi, sync.dma_start(
                            xin_tok.ap()[sc * P:(sc + 1) * P, :], xg[gi % 2][:]))
                    wrt.wait_all(sync)
                    for c in range(ND):
                        if c == 0:
                            sync.wait_ge(upm.sem, 64 * e)
                        tr.inc(sync.dma_start(
                            xinT[:, c, :], xin_tok.ap()[:, c * P:(c + 1) * P],
                            transpose=True), 16)

            @blk.tensor
            def _(t):
                for e in range(E):
                    for fh in range(2):
                        fi = 2 * e + fh
                        w1l.wait(t, fi)
                        if fh == 0:
                            t.wait_ge(tr.sem, 16 * 8 * (e + 1))
                        for ft in range(16):
                            gg = 32 * e + 16 * fh + ft
                            if gg >= 2:
                                t.wait_ge(gel.sem, gg - 1)
                            ps = pU[ft % 2]
                            for sb in range(2):
                                ssl = slice(sb * 512, (sb + 1) * 512)
                                for c in range(ND):
                                    mm = t.matmul(ps[:, ssl],
                                                  w1h[fi % 2][:, c, ft * P:(ft + 1) * P],
                                                  xinT[:, c, ssl],
                                                  start=(c == 0), stop=(c == ND - 1))
                                upm.inc(mm)
                        w2l.wait(t, fi)
                        t.wait_ge(gel.sem, 16 * (fi + 1))
                        for selt in range(8):
                            dg = 16 * e + 8 * fh + selt
                            if dg >= 2:
                                t.wait_ge(dnev.sem, dg - 1)
                            ps = pD[selt % 2]
                            for dh in range(2):
                                dsl = slice(dh * 512, (dh + 1) * 512)
                                for fc in range(16):
                                    mm = t.matmul(ps[:, dsl],
                                                  hT[:, fc, selt * P:(selt + 1) * P],
                                                  w2h[fi % 2][:, fc, dsl],
                                                  start=(fc == 0), stop=(fc == 15))
                            dnm.inc(mm)

            @blk.scalar
            def _(s):
                for e in range(E):
                    for fh in range(2):
                        fi = 2 * e + fh
                        for ft in range(16):
                            s.wait_ge(upm.sem, 64 * e + 32 * fh + 2 * (ft + 1))
                            if fi >= 1:
                                s.wait_ge(dnm.sem, 8 * fi)
                            gel.inc(s.activation(hT[:, ft, :], pU[ft % 2][:],
                                                 AF.Gelu))

            @blk.vector
            def _(v):
                for e in range(E):
                    for fh in range(2):
                        for selt in range(8):
                            sidx = 16 * e + 8 * fh + selt
                            v.wait_ge(dnm.sem, sidx + 1)
                            if sidx >= 2:
                                scat.wait(v, sidx - 2)
                            dnev.inc(v.tensor_scalar_mul(
                                stage[sidx % 2][:], pD[selt % 2][:],
                                g_slab[:, e * 8 + selt:e * 8 + selt + 1]))

    pr.release(ctrs=["f0_upm", "f0_gel", "f0_dnm", "f0_dnev", "f0_tr"],
               slots=[gth, wrt, w1l, w2l, scat])

def emit_attn1_proj(pr, xnT, W, qraw, kraw, vtok, combT_hbm, c_ident, rw_dram, rb_dram, comb_mode=3):
    skip_comb = comb_mode == 0
    nc = pr.nc
    wld = pr.ctr("a1_wld")
    prm = pr.ctr("a1_prm")
    prr = pr.ctr("a1_prr")
    pct = pr.ctr("a1_pct")
    dvq = pr.ctr("a1_dvq")
    dvr = pr.ctr("a1_dvr")
    ccp = pr.ctr("a1_ccp")
    stq = DmaSlots(nc, "a1_stq", 2)
    stk = DmaSlots(nc, "a1_stk", 2)
    stv = DmaSlots(nc, "a1_stv", 2)
    pr.t["a1_cstslots"] = DmaSlots(nc, "a1_cst", 2)

    with ExitStack() as ctx:
        En = ctx.enter_context
        wq = En(nc.sbuf_tensor("a1_wq", [P, ND, D], BF16))
        wk = En(nc.sbuf_tensor("a1_wk", [P, ND, D], BF16))
        wv = En(nc.sbuf_tensor("a1_wv", [P, ND, D], BF16))
        rw = En(nc.sbuf_tensor("a1_rw", [P, ND, E], BF16))
        brow = En(nc.sbuf_tensor("a1_brow", [1, 3 * D + E], BF16))
        ones = En(nc.sbuf_tensor("a1_ones", [1, 512], BF16))
        qsb = [En(nc.sbuf_tensor(f"a1_q{i}", [P, D], F32)) for i in range(2)]
        ksb = [En(nc.sbuf_tensor(f"a1_k{i}", [P, D], F32)) for i in range(2)]
        vsb = [En(nc.sbuf_tensor(f"a1_v{i}", [P, D], BF16)) for i in range(2)]
        rp = En(nc.sbuf_tensor("a1_rp", [P, 8], F32))
        rst = En(nc.sbuf_tensor("a1_rst", [P, 8], F32))
        eqt = En(nc.sbuf_tensor("a1_eq", [P, 8], F32))
        pr.t["a1_comb"] = En(nc.sbuf_tensor("a1_cmb", [P, NT, E], F32))
        combT_st = En(nc.sbuf_tensor("a1_cTs", [8, 2, P], BF16))
        pP = [En(nc.psum_tensor(f"a1_pP{i}", [P, D], F32)) for i in range(2)]
        pR = En(nc.psum_tensor("a1_pR", [P, E], F32))
        pC = En(nc.psum_tensor("a1_pC", [8, 2, P], F32))

        bq = brow[0:1, 0:D]
        bk = brow[0:1, D:2 * D]
        bv = brow[0:1, 2 * D:3 * D]
        rb = brow[0:1, 3 * D:3 * D + E]

        with nc.Block() as blk:
            @blk.gpsimd
            def _(g):
                for wsb, wd in ((wq, W["wq"]), (wk, W["wk"]), (wv, W["wv"])):
                    wld.inc(g.dma_start(wsb[:, :, :],
                                        wd.ap().rearrange("(c p) e -> p c e", p=P)), 16)
                wld.inc(g.dma_start(rw[:, :, :],
                                    rw_dram.ap().rearrange("(c p) e -> p c e", p=P)), 16)
                for i, bd in enumerate([W["bq"], W["bk"], W["bv"]]):
                    wld.inc(g.dma_start(brow[0:1, i * D:(i + 1) * D], bd.ap()), 16)
                wld.inc(g.dma_start(brow[0:1, 3 * D:3 * D + E], rb_dram.ap()), 16)
                g.wait_ge(wld.sem, wld.n)

            @blk.vector
            def _(v):
                v.memset(ones[:], 1.0)

        with nc.Block() as blk:
            @blk.tensor
            def _(t):
                for tt in range(NT):
                    tsl = slice(tt * P, (tt + 1) * P)
                    for j, (wsb, bias) in enumerate(((wq, bq), (wk, bk), (wv, bv))):
                        pj = 3 * tt + j
                        if pj >= 2:
                            t.wait_ge(dvq.sem, pj - 1)
                        ps = pP[pj % 2]
                        for dh in range(2):
                            dsl = slice(dh * 512, (dh + 1) * 512)
                            for c in range(ND):
                                t.matmul(ps[:, dsl], xnT[:, c, tsl], wsb[:, c, dsl],
                                         start=(c == 0), stop=False)
                            mm = t.matmul(ps[:, dsl], ones[0:1, 0:P], bias[:, dsl],
                                          start=False, stop=True)
                        prm.inc(mm)
                    if tt >= 1:
                        t.wait_ge(dvr.sem, 13 * (tt - 1) + 3)
                    for c in range(ND):
                        t.matmul(pR[:, :], xnT[:, c, tsl], rw[:, c, :],
                                 start=(c == 0), stop=False)
                    prr.inc(t.matmul(pR[:, :], ones[0:1, 0:P], rb[:],
                                     start=False, stop=True))
                    jt = tt - 1
                    if jt >= 0 and not skip_comb:
                        t.wait_ge(dvr.sem, 13 * (jt + 1))
                        if jt >= 2 and comb_mode >= 2:
                            t.wait_ge(ccp.sem, jt - 1)
                        pct.inc(t.transpose(pC[:, jt % 2, :],
                                            comb_sl(pr, jt, None), c_ident[:]))
                for jt in ([] if skip_comb else (NT - 1,)):
                    t.wait_ge(dvr.sem, 13 * (jt + 1))
                    if jt >= 2 and comb_mode >= 2:
                        t.wait_ge(ccp.sem, jt - 1)
                    pct.inc(t.transpose(pC[:, jt % 2, :],
                                        comb_sl(pr, jt, None), c_ident[:]))

            @blk.vector
            def _(v):
                for tt in range(NT + 1):
                    if tt < NT:
                        b = tt % 2
                        v.wait_ge(prm.sem, 3 * tt + 1)
                        if tt >= 2:
                            stq.wait(v, tt - 2)
                        dvq.inc(v.tensor_copy(qsb[b][:], pP[(3 * tt) % 2][:]))
                        v.wait_ge(prm.sem, 3 * tt + 2)
                        if tt >= 2:
                            stk.wait(v, tt - 2)
                        dvq.inc(v.tensor_copy(ksb[b][:], pP[(3 * tt + 1) % 2][:]))
                        v.wait_ge(prm.sem, 3 * tt + 3)
                        if tt >= 2:
                            stv.wait(v, tt - 2)
                        dvq.inc(v.tensor_copy(vsb[b][:], pP[(3 * tt + 2) % 2][:]))
                        v.wait_ge(prr.sem, tt + 1)
                        dvr.inc(v.tensor_reduce(rst[:, 0:1], pR[:], axis=AXX, op=ALU.max))
                        v.wait_ge(dvr.sem, dvr.n)
                        dvr.inc(v.tensor_scalar_mul(rst[:, 1:2], rst[:, 0:1], -1.0))
                        dvr.inc(v.tensor_copy(rp[:], pR[:]))
                        v.wait_ge(dvr.sem, dvr.n)
                        dvr.inc(v.tensor_scalar(rp[:], rp[:], rst[:, 1:2], None, op0=ALU.add))
                        v.wait_ge(dvr.sem, dvr.n)
                        v.wait_ge(pr.ctr("a1_rex").sem, tt + 1)
                        dvr.inc(v.tensor_reduce(rst[:, 2:3], rp[:], axis=AXX, op=ALU.add))
                        v.wait_ge(dvr.sem, dvr.n)
                        dvr.inc(v.reciprocal(rst[:, 3:4], rst[:, 2:3]))
                        v.wait_ge(dvr.sem, dvr.n)
                        dvr.inc(v.tensor_scalar_mul(rp[:], rp[:], rst[:, 3:4]))
                        v.wait_ge(dvr.sem, dvr.n)
                        dvr.inc(v.tensor_reduce(rst[:, 4:5], rp[:], axis=AXX, op=ALU.max))
                        v.wait_ge(dvr.sem, dvr.n)
                        dvr.inc(v.tensor_scalar(eqt[:], rp[:], rst[:, 4:5], -1e30,
                                                op0=ALU.is_ge, op1=ALU.mult))
                        v.wait_ge(dvr.sem, dvr.n)
                        dvr.inc(v.tensor_add(eqt[:], eqt[:], rp[:]))
                        v.wait_ge(dvr.sem, dvr.n)
                        dvr.inc(v.tensor_reduce(rst[:, 5:6], eqt[:], axis=AXX, op=ALU.max))
                        v.wait_ge(dvr.sem, dvr.n)
                        dvr.inc(v.tensor_scalar(eqt[:], rp[:], rst[:, 5:6], None, op0=ALU.is_ge))
                        v.wait_ge(dvr.sem, dvr.n)
                        dvr.inc(v.tensor_mul(comb_sl(pr, tt, None), rp[:], eqt[:]))
                    jt = tt - 1
                    if jt >= 0 and comb_mode >= 2:
                        v.wait_ge(pct.sem, jt + 1)
                        if jt >= 2 and comb_mode >= 3:
                            pr.t["a1_cstslots"].wait(v, jt - 2)
                        ccp.inc(v.tensor_copy(combT_st[:, jt % 2, :],
                                              pC[:, jt % 2, :]))

            @blk.scalar
            def _(s):
                rex = pr.ctr("a1_rex")
                for tt in range(NT):
                    s.wait_ge(pr.c["a1_dvr"].sem, 13 * tt + 4)
                    rex.inc(s.activation(rp[:], rp[:], AF.Exp))

            @blk.sync
            def _(sync):
                combT_v = combT_hbm.ap().rearrange("o (e t) -> (o e) t", e=E)
                cslots = pr.t["a1_cstslots"]
                for tt in range(NT + 1):
                    if tt < NT:
                        b = tt % 2
                        sync.wait_ge(dvq.sem, 3 * tt + 1)
                        stq.issue(tt, sync.dma_start(qraw[tt * P:(tt + 1) * P, :], qsb[b][:]))
                        sync.wait_ge(dvq.sem, 3 * tt + 2)
                        stk.issue(tt, sync.dma_start(kraw[tt * P:(tt + 1) * P, :], ksb[b][:]))
                        sync.wait_ge(dvq.sem, 3 * tt + 3)
                        stv.issue(tt, sync.dma_start(vtok[tt * P:(tt + 1) * P, :], vsb[b][:]))
                    jt = tt - 1
                    if jt >= 0 and comb_mode >= 3:
                        sync.wait_ge(ccp.sem, jt + 1)
                        cslots.issue(jt, sync.dma_start(
                            combT_v[:, jt * P:(jt + 1) * P],
                            combT_st[:, jt % 2, :]))
                stq.wait_all(sync)
                stk.wait_all(sync)
                stv.wait_all(sync)
                if comb_mode >= 3:
                    cslots.wait_all(sync)

    pr.release(ctrs=["a1_wld", "a1_prm", "a1_prr", "a1_pct", "a1_dvq", "a1_dvr",
                     "a1_ccp", "a1_rex"],
               slots=[stq, stk, stv, pr.t.pop("a1_cstslots")])


def comb_sl(pr, tt, e):
    cslab = pr.t["a1_comb"]
    if e is None:
        return cslab[:, tt, :]
    return cslab[:, tt, e:e + 1]

def emit_attn1_core(pr, knT, qnT_hbm, vtok, wo_dram, bo_dram, ls_dram,
                    x1_dram, y_dram):
    nc = pr.nc
    QB = 512
    NQB = T // QB
    wld = pr.ctr("c1_wld")
    fld = pr.ctr("c1_fld")
    smm = pr.ctr("c1_smm")
    sig = pr.ctr("c1_sig")
    avm = pr.ctr("c1_avm")
    ave = pr.ctr("c1_ave")
    omm = pr.ctr("c1_omm")
    oev = pr.ctr("c1_oev")
    qld = DmaSlots(nc, "c1_qld", 2)
    vld = DmaSlots(nc, "c1_vld", 2)
    xld = DmaSlots(nc, "c1_xld", 3)
    yst = DmaSlots(nc, "c1_yst", 3)

    with ExitStack() as ctx:
        En = ctx.enter_context
        wo = En(nc.sbuf_tensor("c1_wo", [P, ND, D], BF16))
        lsb = En(nc.sbuf_tensor("c1_lsb", [1, 2 * D], BF16))
        lsbc = En(nc.sbuf_tensor("c1_lsbc", [P, D], BF16))
        ones = En(nc.sbuf_tensor("c1_ones", [1, P], BF16))
        qnb = [En(nc.sbuf_tensor(f"c1_qnb{i}", [P, ND, QB], BF16)) for i in range(2)]
        vb = [En(nc.sbuf_tensor(f"c1_vb{i}", [P, D], BF16)) for i in range(2)]
        at = En(nc.sbuf_tensor("c1_at", [P, NT, QB], BF16))
        otq = En(nc.sbuf_tensor("c1_otq", [P, ND, QB], BF16))
        xtb = [En(nc.sbuf_tensor(f"c1_xtb{i}", [P, D], F32)) for i in range(3)]
        ysb = [En(nc.sbuf_tensor(f"c1_ysb{i}", [P, D], F32)) for i in range(3)]
        psS = [En(nc.psum_tensor(f"c1_psS{i}", [P, 2 * QB], F32)) for i in range(2)]
        pA = [En(nc.psum_tensor(f"c1_pA{i}", [P, QB], F32)) for i in range(4)]

        with nc.Block() as blk:
            @blk.gpsimd
            def _(g):
                wld.inc(g.dma_start(wo[:, :, :],
                                    wo_dram.ap().rearrange("(c p) e -> p c e", p=P)), 16)
                wld.inc(g.dma_start(lsb[0:1, 0:D], ls_dram.ap()), 16)
                wld.inc(g.dma_start(lsb[0:1, D:2 * D], bo_dram.ap()), 16)
                g.wait_ge(wld.sem, wld.n)

            @blk.vector
            def _(v):
                v.memset(ones[:], 1.0)

        with nc.Block() as blk:
            @blk.tensor
            def _(t):
                for half in range(2):
                    fld.inc(t.matmul(psS[0][:, half * QB:(half + 1) * QB],
                                     ones[0:1, 0:P],
                                     lsb[0:1, half * QB:(half + 1) * QB],
                                     start=True, stop=True))

            @blk.vector
            def _(v):
                v.wait_ge(fld.sem, 2)
                fld.inc(v.tensor_copy(lsbc[:], psS[0][:, 0:D]))
                v.wait_ge(fld.sem, fld.n)
                for c in range(ND):
                    fld.inc(v.tensor_mul(wo[:, c, :], wo[:, c, :], lsbc[:]))
                fld.inc(v.tensor_mul(lsb[0:1, D:2 * D], lsb[0:1, D:2 * D],
                                     lsb[0:1, 0:D]))
                v.wait_ge(fld.sem, fld.n)

        bols = lsb[0:1, D:2 * D]

        with nc.Block() as blk:
            @blk.sync
            def _(sync):
                for qb in range(NQB):
                    if qb >= 2:
                        sync.wait_ge(smm.sem, 32 * (qb - 1))
                    qld.issue(qb, sync.dma_start(
                        qnb[qb % 2][:, :, :],
                        qnT_hbm.ap()[:, :, qb * QB:(qb + 1) * QB]))
                    for i in range(4):
                        ii = 4 * qb + i
                        if ii >= 3:
                            sync.wait_ge(oev.sem, ii - 2)
                        xld.issue(ii, sync.dma_start(
                            xtb[ii % 3][:],
                            x1_dram[(qb * QB + i * P):(qb * QB + (i + 1) * P), :]))

            @blk.tensor
            def _(t):
                for qb in range(NQB):
                    qld.wait(t, qb)
                    for g_ in range(16):
                        if qb * 16 + g_ >= 2:
                            t.wait_ge(sig.sem, qb * 16 + g_ - 1)
                        if g_ < 2 and qb >= 1:
                            t.wait_ge(oev.sem, 4 * qb)
                        ps = psS[g_ % 2]
                        for k2 in range(2):
                            kt_i = 2 * g_ + k2
                            for c in range(ND):
                                mm = t.matmul(ps[:, k2 * QB:(k2 + 1) * QB],
                                              knT[:, c, kt_i * P:(kt_i + 1) * P],
                                              qnb[qb % 2][:, c, :],
                                              start=(c == 0), stop=(c == ND - 1))
                            smm.inc(mm)
                    for ps_ in range(2):
                        for kc in range(NT):
                            vi = qb * 64 + ps_ * NT + kc
                            vld.wait(t, vi)
                            if kc % 2 == 0:
                                t.wait_ge(sig.sem, qb * 16 + kc // 2 + 1)
                            if kc == 0 and (qb * 2 + ps_) >= 1:
                                t.wait_ge(ave.sem, 4 * (2 * qb + ps_))
                            for et in range(4):
                                mm = t.matmul(pA[et][:, :],
                                              vb[vi % 2][:, (ps_ * 4 + et) * P:
                                                          (ps_ * 4 + et + 1) * P],
                                              at[:, kc, :],
                                              start=(kc == 0), stop=(kc == NT - 1))
                            avm.inc(mm)
                    t.wait_ge(ave.sem, 8 * (qb + 1))
                    for tl in range(4):
                        ii = 4 * qb + tl
                        if ii >= 2:
                            t.wait_ge(oev.sem, ii - 1)
                        ps = psS[tl % 2]
                        for dh in range(2):
                            dsl = slice(dh * QB, (dh + 1) * QB)
                            for c in range(ND):
                                t.matmul(ps[:, dsl], otq[:, c, tl * P:(tl + 1) * P],
                                         wo[:, c, dsl], start=(c == 0), stop=False)
                            mm = t.matmul(ps[:, dsl], ones[0:1, 0:P], bols[:, dsl],
                                          start=False, stop=True)
                        omm.inc(mm)

            @blk.scalar
            def _(s):
                for qb in range(NQB):
                    for g_ in range(16):
                        s.wait_ge(smm.sem, qb * 32 + 2 * (g_ + 1))
                        if qb >= 1:
                            s.wait_ge(avm.sem, 64 * (qb - 1) + 32 + 2 * g_ + 2)
                        sig.inc(s.activation(at[:, 2 * g_:2 * g_ + 2, :],
                                             psS[g_ % 2][:, :], AF.Sigmoid,
                                             scale=1.0 / 32.0,
                                             bias=pr.const(-8.317766166719343)))

            @blk.vector
            def _(v):
                for qb in range(NQB):
                    for ps_ in range(2):
                        v.wait_ge(avm.sem, qb * 64 + 32 * (ps_ + 1))
                        for et in range(4):
                            ave.inc(v.tensor_copy(otq[:, ps_ * 4 + et, :], pA[et][:, :]))
                    for tl in range(4):
                        ii = 4 * qb + tl
                        v.wait_ge(omm.sem, ii + 1)
                        xld.wait(v, ii)
                        if ii >= 3:
                            yst.wait(v, ii - 3)
                        oev.inc(v.tensor_add(ysb[ii % 3][:], xtb[ii % 3][:],
                                             psS[tl % 2][:]))

            @blk.gpsimd
            def _(g):
                for qb in range(NQB):
                    for ps in range(2):
                        for kc in range(NT):
                            vi = qb * 64 + ps * NT + kc
                            if vi >= 2:
                                g.wait_ge(avm.sem, vi - 1)
                            vld.issue(vi, g.dma_start(
                                vb[vi % 2][:], vtok[kc * P:(kc + 1) * P, :]))
                    for tl in range(4):
                        ii = 4 * qb + tl
                        g.wait_ge(oev.sem, ii + 1)
                        yst.issue(ii, g.dma_start(
                            y_dram[(qb * QB + tl * P):(qb * QB + (tl + 1) * P), :],
                            ysb[ii % 3][:]))
                yst.wait_all(g)
                vld.wait_all(g)

    pr.release(ctrs=["c1_wld", "c1_fld", "c1_smm", "c1_sig", "c1_avm", "c1_ave",
                     "c1_omm", "c1_oev"],
               slots=[qld, vld, xld, yst])


def emit_ffn1(pr, xnT_hbm, combT, up_dram, dn_dram, y_dram):
    nc = pr.nc
    TB = 512
    NTB = T // TB
    upm = pr.ctr("f1_upm")
    gel = pr.ctr("f1_gel")
    cbm = pr.ctr("f1_cbm")
    cbb = pr.ctr("f1_cbb")
    cbe = pr.ctr("f1_cbe")
    dnm = pr.ctr("f1_dnm")
    dnev = pr.ctr("f1_dnev")
    wul = DmaSlots(nc, "f1_wul", 1)
    wdl = DmaSlots(nc, "f1_wdl", 1)
    xbl = DmaSlots(nc, "f1_xbl", 2)
    acc = DmaSlots(nc, "f1_acc", 8)
    cbl = DmaSlots(nc, "f1_cbl", 1)

    with ExitStack() as ctx:
        En = ctx.enter_context
        up = En(nc.sbuf_tensor("f1_up", [P, ND, FF], BF16))
        dn = En(nc.sbuf_tensor("f1_dn", [P, NF, D], BF16))
        hq = En(nc.sbuf_tensor("f1_hq", [P, NF, TB], BF16))
        xb = [En(nc.sbuf_tensor(f"f1_xb{i}", [P, ND, TB], BF16)) for i in range(2)]
        cb = En(nc.sbuf_tensor("f1_cb", [P, TB], BF16))
        cbrow = En(nc.sbuf_tensor("f1_cbrow", [1, T], BF16))
        stg = [En(nc.sbuf_tensor(f"f1_stg{i}", [P, TB], F32)) for i in range(8)]
        ones = En(nc.sbuf_tensor("f1_ones", [1, P], BF16))
        pU = [En(nc.psum_tensor(f"f1_pU{i}", [P, TB], F32)) for i in range(2)]
        pD = [En(nc.psum_tensor(f"f1_pD{i}", [P, TB], F32)) for i in range(2)]
        pC = En(nc.psum_tensor("f1_pC", [P, TB], F32))

        with nc.Block() as blk:
            @blk.vector
            def _(v):
                v.memset(ones[:], 1.0)

        with nc.Block() as blk:
            @blk.gpsimd
            def _(g):
                for e in range(E):
                    if e > 0:
                        g.wait_ge(cbb.sem, NTB * e)
                    cbl.issue(e, g.dma_start(cbrow[0:1, :],
                                             combT[0:1, e * T:(e + 1) * T]))
                    if e > 0:
                        g.wait_ge(upm.sem, 32 * NTB * e)
                    wul.issue(e, g.dma_start(
                        up[:, :, :], up_dram.ap()[e].rearrange("(c p) f -> p c f", p=P)))
                    if e > 0:
                        g.wait_ge(dnm.sem, 8 * NTB * e)
                    wdl.issue(e, g.dma_start(
                        dn[:, :, :], dn_dram.ap()[e].rearrange("(c p) d -> p c d", p=P)))
                    def acc_batch(g, tj):
                        tbj = tj % NTB
                        for tl in range(4):
                            for dh in range(2):
                                si = tj * 8 + tl * 2 + dh
                                g.wait_ge(dnev.sem, si + 1)
                                acc.issue(si, g.dma_start(
                                    y_dram[(tbj * TB + tl * P):(tbj * TB + (tl + 1) * P),
                                           dh * TB:(dh + 1) * TB],
                                    stg[si % 8][:], accum_op=ALU.add))
                    for tb in range(NTB):
                        ti = e * NTB + tb
                        if ti >= 2:
                            g.wait_ge(upm.sem, 32 * (ti - 1))
                        xbl.issue(ti, g.dma_start(
                            xb[ti % 2][:, :, :],
                            xnT_hbm.ap()[:, :, tb * TB:(tb + 1) * TB]))
                        if ti >= 1:
                            acc_batch(g, ti - 1)
                    if e == E - 1:
                        acc_batch(g, E * NTB - 1)
                acc.wait_all(g)

            @blk.tensor
            def _(t):
                for e in range(E):
                    wul.wait(t, e)
                    for tb in range(NTB):
                        ti = e * NTB + tb
                        xbl.wait(t, ti)
                        if ti >= 1:
                            t.wait_ge(cbe.sem, ti)
                        if tb == 0:
                            cbl.wait(t, e)
                        cbb.inc(t.matmul(pC[:, :], ones[0:1, 0:P],
                                         cbrow[0:1, tb * TB:(tb + 1) * TB],
                                         start=True, stop=True))
                        for ft in range(NF):
                            gg = ti * 32 + ft
                            if gg >= 2:
                                t.wait_ge(gel.sem, gg - 1)
                            ps = pU[ft % 2]
                            for c in range(ND):
                                mm = t.matmul(ps[:, :], up[:, c, ft * P:(ft + 1) * P],
                                              xb[ti % 2][:, c, :],
                                              start=(c == 0), stop=(c == ND - 1))
                            upm.inc(mm)
                        if tb == 0:
                            wdl.wait(t, e)
                        t.wait_ge(cbm.sem, 32 * (ti + 1))
                        for tl in range(4):
                            for dh in range(2):
                                dg = ti * 8 + tl * 2 + dh
                                if dg >= 2:
                                    t.wait_ge(dnev.sem, dg - 1)
                                ps = pD[dg % 2]
                                for fc in range(NF):
                                    mm = t.matmul(ps[:, :], hq[:, fc, tl * P:(tl + 1) * P],
                                                  dn[:, fc, dh * TB:(dh + 1) * TB],
                                                  start=(fc == 0), stop=(fc == NF - 1))
                                dnm.inc(mm)

            @blk.scalar
            def _(s):
                for e in range(E):
                    for tb in range(NTB):
                        ti = e * NTB + tb
                        for ft in range(NF):
                            s.wait_ge(upm.sem, ti * 32 + ft + 1)
                            if ti >= 1:
                                s.wait_ge(dnm.sem, 8 * ti)
                            gel.inc(s.activation(hq[:, ft, :], pU[ft % 2][:], AF.Gelu))

            @blk.vector
            def _(v):
                for e in range(E):
                    for tb in range(NTB):
                        ti = e * NTB + tb
                        v.wait_ge(cbb.sem, ti + 1)
                        if ti >= 1:
                            v.wait_ge(cbm.sem, 32 * ti)
                        cbe.inc(v.tensor_copy(cb[:], pC[:, :]))
                        v.wait_ge(cbe.sem, cbe.n)
                        for ft in range(NF):
                            v.wait_ge(gel.sem, ti * 32 + ft + 1)
                            cbm.inc(v.tensor_mul(hq[:, ft, :], hq[:, ft, :], cb[:]))
                        for tl in range(4):
                            for dh in range(2):
                                dg = ti * 8 + tl * 2 + dh
                                v.wait_ge(dnm.sem, dg + 1)
                                if dg >= 8:
                                    acc.wait(v, dg - 8)
                                dnev.inc(v.tensor_copy(stg[dg % 8][:], pD[dg % 2][:]))

    pr.release(ctrs=["f1_upm", "f1_gel", "f1_cbm", "f1_cbb", "f1_cbe", "f1_dnm",
                     "f1_dnev"],
               slots=[wul, wdl, xbl, acc, cbl])
def make_consts(nc):
    ones128 = nc.inline_tensor(np.ones((P, P), np.float32), "c_ones128")
    utri = nc.inline_tensor(np.triu(np.ones((P, P), np.float32)), "c_utri")
    slots = nc.inline_tensor(
        np.tile(np.arange(1, SEL + 1, dtype=np.float32), (P, 1)), "c_slots")
    tok = nc.inline_tensor(
        (np.arange(NT, dtype=np.float32)[None, :] * P
         + np.arange(P, dtype=np.float32)[:, None]), "c_tok")
    ident = nc.inline_tensor(np.eye(P, dtype=np.float32), "c_ident")
    return {"ones": ones128, "utri": utri, "slots": slots, "tok": tok,
            "ident": ident}


def emit_load_consts(pr, cdram, csb):
    nc = pr.nc
    ld = pr.ctr("cst_ld")
    with nc.Block() as blk:
        @blk.sync
        def _(sync):
            for k in cdram:
                ld.inc(sync.dma_start(csb[k][:], cdram[k].ap()), 16)
            sync.wait_ge(ld.sem, ld.n)


def build_program(debug_stop=None):
    nc = bass.Bass("TRN2", target_bir_lowering=False)
    pr = Prog(nc)
    l1dbg = debug_stop in ("ln1", "attn1p", "attn1q", "attn1t", "attn1u", "lnq1", "attn1c")
    full = debug_stop is None or l1dbg

    x = nc.dram_tensor("x", [T, D], F32, kind="ExternalInput")
    gwT = nc.dram_tensor("gwT", [D, E], F32, kind="ExternalInput")
    W0 = {}
    for nm in ("wq", "wk", "wv", "wo"):
        W0[nm] = nc.dram_tensor(nm + "0", [D, D], F32, kind="ExternalInput")
    for nm in ("bq", "bk", "bv", "bo"):
        W0[nm] = nc.dram_tensor(nm + "0", [1, D], F32, kind="ExternalInput")
    w1 = nc.dram_tensor("w1", [E, D, FF], F32, kind="ExternalInput")
    w2 = nc.dram_tensor("w2", [E, FF, D], F32, kind="ExternalInput")
    W1 = {}
    if full:
        for nm in ("wq", "wk", "wv", "wo"):
            W1[nm] = nc.dram_tensor(nm + "1", [D, D], F32, kind="ExternalInput")
        for nm in ("bq", "bk", "bv", "bo"):
            W1[nm] = nc.dram_tensor(nm + "1", [1, D], F32, kind="ExternalInput")
        ls = nc.dram_tensor("ls", [1, D], F32, kind="ExternalInput")
        rw = nc.dram_tensor("rw", [D, E], F32, kind="ExternalInput")
        rb = nc.dram_tensor("rb", [1, E], F32, kind="ExternalInput")
        upw = nc.dram_tensor("upw", [E, D, FF], F32, kind="ExternalInput")
        dnw = nc.dram_tensor("dnw", [E, FF, D], F32, kind="ExternalInput")
    y = nc.dram_tensor("y", [T, D], F32, kind="ExternalOutput")

    xn_tok = nc.dram_tensor("xn_tok", [T, D], BF16, kind="Internal")
    x1 = nc.dram_tensor("x1", [T, D], F32, kind="Internal")
    outT_hbm = nc.dram_tensor("outT", [D, T], BF16, kind="Internal")
    xin_tok = nc.dram_tensor("xin_tok", [SEL, D], BF16, kind="Internal")
    if full:
        qraw = nc.dram_tensor("qraw", [T, D], F32, kind="Internal")
        kraw = nc.dram_tensor("kraw", [T, D], F32, kind="Internal")
        vtok1 = nc.dram_tensor("vtok1", [T, D], BF16, kind="Internal")
        qn_tok = nc.dram_tensor("qn_tok", [T, D], BF16, kind="Internal")
        kn_tok = nc.dram_tensor("kn_tok", [T, D], BF16, kind="Internal")
        xnT_hbm = nc.dram_tensor("xnT_hbm", [P, ND, T], BF16, kind="Internal")
        qnT_hbm = nc.dram_tensor("qnT_hbm", [P, ND, T], BF16, kind="Internal")
        combT_hbm = nc.dram_tensor("combT_hbm", [1, E * T], BF16, kind="Internal")

    cdram = make_consts(nc)

    with ExitStack() as perm:
        En = perm.enter_context
        pr.init_consts(perm, [LN_EPS, -float(np.log(T))])
        csb_ident = En(nc.sbuf_tensor("cs_ident", [P, P], F32))
        l0s = ExitStack()
        En0 = l0s.enter_context
        csb = {
            "ones": En0(nc.sbuf_tensor("cs_ones", [P, P], F32)),
            "utri": En0(nc.sbuf_tensor("cs_utri", [P, P], F32)),
            "slots": En0(nc.sbuf_tensor("cs_slots", [P, SEL], F32)),
            "tok": En0(nc.sbuf_tensor("cs_tok", [P, NT], F32)),
            "ident": csb_ident,
        }
        m_slab = En0(nc.sbuf_tensor("m_slab", [P, E * 8], I32))
        g_slab = En0(nc.sbuf_tensor("g_slab", [P, E * 8], F32))
        probs = En0(nc.sbuf_tensor("probs", [P, E, NT], F32))

        emit_load_consts(pr, cdram, csb)

        emit_ln(pr, x.ap(), xn_tok.ap(), "0")
        if debug_stop == "ln":
            xn_dbg = nc.dram_tensor("xn_dbg", [T, D], BF16, kind="ExternalOutput")
            xnT_dbg = nc.dram_tensor("xnT_dbg", [P, ND, T], BF16, kind="ExternalOutput")
            with ExitStack() as ctx:
                xnT = ctx.enter_context(nc.sbuf_tensor("xnT", [P, ND, T], BF16))
                emit_xnT(pr, xn_tok.ap(), xnT, "0")
                cp = pr.ctr("dbgcp")
                with nc.Block() as blk:
                    @blk.sync
                    def _(sync):
                        cp.inc(sync.dma_start(xn_dbg.ap(), xn_tok.ap()), 16)
                        cp.inc(sync.dma_start(xnT_dbg.ap(), xnT[:, :, :]), 16)
                        sync.wait_ge(cp.sem, cp.n)
            l0s.close()
            return nc

        dbg = None
        if debug_stop == "sel":
            dbg = {
                "probs": nc.dram_tensor("d_probs", [P, E, NT], F32, kind="ExternalOutput"),
                "tlo": nc.dram_tensor("d_tlo", [P, E], F32, kind="ExternalOutput"),
                "m": nc.dram_tensor("d_m", [P, E * 8], I32, kind="ExternalOutput"),
                "g": nc.dram_tensor("d_g", [P, E * 8], F32, kind="ExternalOutput"),
            }
        with ExitStack() as ctx:
            xnT = ctx.enter_context(nc.sbuf_tensor("xnT", [P, ND, T], BF16))
            emit_xnT(pr, xn_tok.ap(), xnT, "0")
            emit_sel(pr, xnT, gwT, csb, m_slab, g_slab, probs,
                     csb['ident'], dbg=dbg)
            if debug_stop != "sel":
                with ExitStack() as actx:
                    emit_attn0(pr, xnT, W0, x.ap(), x1.ap(), outT_hbm, actx)
        if debug_stop == "sel":
            l0s.close()
            return nc
        if debug_stop == "attn0":
            x1_dbg = nc.dram_tensor("x1_dbg", [T, D], F32, kind="ExternalOutput")
            _dbg_copy(pr, nc, x1_dbg, x1)
            l0s.close()
            return nc
        emit_ffn0(pr, xn_tok.ap(), xin_tok, x1.ap(), w1, w2, m_slab, g_slab)
        l0s.close()
        if debug_stop == "l0":
            x1_dbg = nc.dram_tensor("x1_dbg", [T, D], F32, kind="ExternalOutput")
            _dbg_copy(pr, nc, x1_dbg, x1)
            return nc

        emit_ln(pr, x1.ap(), xn_tok.ap(), "1")
        if debug_stop == "ln1":
            xn_dbg = nc.dram_tensor("xn_dbg", [T, D], BF16, kind="ExternalOutput")
            xnT_dbg = nc.dram_tensor("xnT_dbg", [P, ND, T], BF16, kind="ExternalOutput")
            with ExitStack() as ctx:
                xnT1 = ctx.enter_context(nc.sbuf_tensor("xnT1", [P, ND, T], BF16))
                emit_xnT(pr, xn_tok.ap(), xnT1, "1", xnT_hbm=xnT_hbm)
                _dbg_copy(pr, nc, xn_dbg, xn_tok)
                _dbg_copy(pr, nc, xnT_dbg, xnT_hbm)
            return nc
        with ExitStack() as ctx:
            xnT1 = ctx.enter_context(nc.sbuf_tensor("xnT1", [P, ND, T], BF16))
            emit_xnT(pr, xn_tok.ap(), xnT1, "1", xnT_hbm=xnT_hbm)
            emit_attn1_proj(pr, xnT1, W1, qraw.ap(), kraw.ap(), vtok1.ap(),
                            combT_hbm, csb_ident, rw, rb,
                            comb_mode={"attn1q": 0, "attn1t": 1,
                                       "attn1u": 2}.get(debug_stop, 3))
        if debug_stop in ("attn1q", "attn1t", "attn1u"):
            q_dbg = nc.dram_tensor("q_dbg", [T, D], F32, kind="ExternalOutput")
            k_dbg = nc.dram_tensor("k_dbg", [T, D], F32, kind="ExternalOutput")
            _dbg_copy(pr, nc, q_dbg, qraw)
            _dbg_copy(pr, nc, k_dbg, kraw)
            return nc
        if debug_stop == "attn1p":
            q_dbg = nc.dram_tensor("q_dbg", [T, D], F32, kind="ExternalOutput")
            k_dbg = nc.dram_tensor("k_dbg", [T, D], F32, kind="ExternalOutput")
            cT_dbg = nc.dram_tensor("cT_dbg", [1, E * T], BF16, kind="ExternalOutput")
            _dbg_copy(pr, nc, q_dbg, qraw)
            _dbg_copy(pr, nc, k_dbg, kraw)
            _dbg_copy(pr, nc, cT_dbg, combT_hbm)
            return nc
        emit_ln(pr, qraw.ap(), qn_tok.ap(), "q1")
        emit_ln(pr, kraw.ap(), kn_tok.ap(), "k1")
        if debug_stop == "lnq1":
            qn_dbg = nc.dram_tensor("qn_dbg", [T, D], BF16, kind="ExternalOutput")
            kn_dbg = nc.dram_tensor("kn_dbg", [T, D], BF16, kind="ExternalOutput")
            _dbg_copy(pr, nc, qn_dbg, qn_tok)
            _dbg_copy(pr, nc, kn_dbg, kn_tok)
            return nc
        with ExitStack() as ctx:
            slab2 = ctx.enter_context(nc.sbuf_tensor("slab2", [P, ND, T], BF16))
            emit_xnT(pr, qn_tok.ap(), slab2, "tq", xnT_hbm=qnT_hbm)
            emit_xnT(pr, kn_tok.ap(), slab2, "tk")
            emit_attn1_core(pr, slab2, qnT_hbm, vtok1.ap(), W1["wo"], W1["bo"],
                            ls, x1.ap(), y.ap())
        if debug_stop == "attn1c":
            return nc
        emit_ffn1(pr, xnT_hbm, combT_hbm, upw, dnw, y.ap())
    return nc


def _dbg_copy(pr, nc, dst, srcd):
    cp = pr.ctr("dbgcp2")
    with nc.Block() as blk:
        @blk.sync
        def _(sync):
            cp.inc(sync.dma_start(dst.ap(), srcd.ap()), 16)
            sync.wait_ge(cp.sem, cp.n)


def _c(a):
    return np.ascontiguousarray(np.asarray(a, dtype=np.float32))


def kernel(L0_attn_x, L0_attn_Wq, L0_attn_bq, L0_attn_Wk, L0_attn_bk,
           L0_attn_Wv, L0_attn_bv, L0_attn_Wo, L0_attn_bo,
           L0_ffn_gate_w, L0_ffn_w1, L0_ffn_w2,
           L1_attn_Wq, L1_attn_bq, L1_attn_Wk, L1_attn_bk, L1_attn_Wv,
           L1_attn_bv, L1_attn_Wo, L1_attn_bo, L1_attn_layerscale,
           L1_ffn_router_w, L1_ffn_router_b, L1_ffn_expert_up,
           L1_ffn_expert_down):
    nc = build_program()
    common = {
        "gwT": _c(np.asarray(L0_ffn_gate_w).T),
        "wq0": _c(np.asarray(L0_attn_Wq).T), "wk0": _c(np.asarray(L0_attn_Wk).T),
        "wv0": _c(np.asarray(L0_attn_Wv).T), "wo0": _c(np.asarray(L0_attn_Wo).T),
        "bq0": _c(L0_attn_bq)[None, :], "bk0": _c(L0_attn_bk)[None, :],
        "bv0": _c(L0_attn_bv)[None, :], "bo0": _c(L0_attn_bo)[None, :],
        "w1": _c(L0_ffn_w1), "w2": _c(L0_ffn_w2),
        "wq1": _c(np.asarray(L1_attn_Wq).T), "wk1": _c(np.asarray(L1_attn_Wk).T),
        "wv1": _c(np.asarray(L1_attn_Wv).T), "wo1": _c(np.asarray(L1_attn_Wo).T),
        "bq1": _c(L1_attn_bq)[None, :], "bk1": _c(L1_attn_bk)[None, :],
        "bv1": _c(L1_attn_bv)[None, :], "bo1": _c(L1_attn_bo)[None, :],
        "ls": _c(L1_attn_layerscale)[None, :],
        "rw": _c(L1_ffn_router_w), "rb": _c(L1_ffn_router_b)[None, :],
        "upw": _c(np.asarray(L1_ffn_expert_up).transpose(0, 2, 1)),
        "dnw": _c(np.asarray(L1_ffn_expert_down).transpose(0, 2, 1)),
    }
    xs = np.asarray(L0_attn_x, dtype=np.float32)
    in_maps = [dict(common, x=np.ascontiguousarray(xs[b])) for b in range(xs.shape[0])]
    res = run_bass_kernel_spmd(nc, in_maps, core_ids=list(range(xs.shape[0])))
    return np.stack([r["y"] for r in res.results], axis=0)
